# version 14
# baseline (speedup 1.0000x reference)
import contextlib

import numpy as np

import concourse.bass as bass
import concourse.mybir as mybir
from concourse.bass_utils import run_bass_kernel_spmd

F32 = mybir.dt.float32
BF16 = mybir.dt.bfloat16
AF = mybir.ActivationFunctionType
ALU = mybir.AluOpType
AX = mybir.AxisListType

D = 1024
S = 4096
NT = 16
TS = 256
NKT = 32
NH = 16
DH = 64
LAT = 256
EPS = 1e-5
INV_SQRT_DH = 0.125


class Sem:
    def __init__(self, handle):
        self.h = handle
        self.n = 0


class Planner:

    ENGS = ("sync", "tensor", "vector", "scalar", "gpsimd")

    def __init__(self):
        self.recs = {e: [] for e in self.ENGS}

    SERIAL = ("vector", "scalar", "gpsimd")

    def __init_serial(self):
        if not hasattr(self, "_own"):
            self._own = {}

    def set_own_sems(self, mapping):
        self._own = dict(mapping)

    def op(self, eng, fn, waits=(), inc=None, amt=1):
        w = [x for x in waits if x is not None]
        own = getattr(self, "_own", {}).get(eng)
        if eng in self.SERIAL and own is not None:
            w = [(own, own.n)] + w
            if inc is None:
                inc = own
        if inc is not None:
            inc.n += amt
        self.recs[eng].append((fn, w, inc, amt))
        return (inc, inc.n) if inc is not None else None

    def dma(self, eng, out, in_, waits=(), inc=None, chain=True):
        w = list(waits)
        if inc is not None and chain:
            w.append((inc, inc.n))
        return self.op(eng, lambda e: e.dma_start(out=out, in_=in_),
                       waits=w, inc=inc, amt=16)

    def emit(self, eng_name, eng):
        last_wait = {}
        dedup = eng_name != "sync"
        for fn, waits, inc, amt in self.recs[eng_name]:
            for sem, val in waits:
                if dedup and last_wait.get(id(sem), -1) >= val:
                    continue
                eng.wait_ge(sem.h, val)
                last_wait[id(sem)] = val
            inst = fn(eng)
            if inc is not None:
                inst.then_inc(inc.h, amt)


def _host_consts():
    import ml_dtypes
    bf = ml_dtypes.bfloat16
    ident = np.eye(128, dtype=np.float32).astype(bf)
    ones = np.ones((128, 128), dtype=np.float32).astype(bf)
    return ident, ones


def host_masks(rank):
    import ml_dtypes
    bf = ml_dtypes.bfloat16
    k = np.arange(128)[:, None]
    q = np.arange(TS)[None, :]
    mlow = (q >= k).astype(np.float32)
    mhigh = (q >= 128 + k).astype(np.float32)
    ones = np.ones((128, TS), np.float32)
    zero = np.zeros((128, TS), np.float32)
    stack = [ones, ones, mlow, mhigh] if rank == 1 else [mlow, mhigh, zero, zero]
    return np.stack(stack).astype(bf)


def build(nt=8, nkt=NKT, stages="all"):
    nc = bass.Bass("TRN2", target_bir_lowering=False, debug=False)

    x_in = nc.dram_tensor("x", [S, D], F32, kind="ExternalInput").ap()
    xq_in = nc.dram_tensor("xq", [8 * TS, D], F32, kind="ExternalInput").ap()
    maskin = nc.dram_tensor("maskin", [4, 128, TS], BF16, kind="ExternalInput").ap()
    cfgi_in = nc.dram_tensor("cfgi", [1, 2], mybir.dt.int32, kind="ExternalInput").ap()
    cfgf_in = nc.dram_tensor("cfgf", [1, 2], F32, kind="ExternalInput").ap()
    w_dq_in = nc.dram_tensor("W_dq", [D, LAT], F32, kind="ExternalInput").ap()
    w_uq_in = nc.dram_tensor("W_uq", [LAT, D], F32, kind="ExternalInput").ap()
    qg_in = nc.dram_tensor("q_g", [LAT], F32, kind="ExternalInput").ap()
    qb_in = nc.dram_tensor("q_b", [LAT], F32, kind="ExternalInput").ap()
    w_dkv_in = nc.dram_tensor("W_dkv", [D, LAT], F32, kind="ExternalInput").ap()
    w_ukv_in = nc.dram_tensor("W_ukv", [LAT, 2 * D], F32, kind="ExternalInput").ap()
    kg_in = nc.dram_tensor("kv_g", [LAT], F32, kind="ExternalInput").ap()
    kb_in = nc.dram_tensor("kv_b", [LAT], F32, kind="ExternalInput").ap()
    w_o_in = nc.dram_tensor("W_o", [D, D], F32, kind="ExternalInput").ap()
    w_r_in = nc.dram_tensor("W_r", [D, D], F32, kind="ExternalInput").ap()
    w_k_in = nc.dram_tensor("W_k", [D, D], F32, kind="ExternalInput").ap()
    w_v_in = nc.dram_tensor("W_v", [D, D], F32, kind="ExternalInput").ap()
    mu_r_in = nc.dram_tensor("mu_r", [D], F32, kind="ExternalInput").ap()
    mu_k_in = nc.dram_tensor("mu_k", [D], F32, kind="ExternalInput").ap()
    out = nc.dram_tensor("out", [8 * TS, D], F32, kind="ExternalOutput").ap()

    ident_np, ones_np = _host_consts()
    ident_d = nc.inline_tensor(ident_np, "ident_c")
    ones_d = nc.inline_tensor(ones_np, "ones_c")

    k_dram = nc.dram_tensor("k_scr", [NKT, 128, 1024], BF16)
    v_dram = nc.dram_tensor("v_scr", [NKT, 128, 1040], BF16)
    x1_dram = nc.dram_tensor("x1_scr", [8, TS, D], F32)
    yfm_dram = nc.dram_tensor("yfm_scr", [8, 128, 8 * TS], BF16)
    yrow_src = nc.dram_tensor("yrow_src", [128, 64], BF16)
    yrow_gath = nc.dram_tensor("yrow_gath", [2, 128, 64], BF16)

    taps = {}

    with contextlib.ExitStack() as cx:
        sb = lambda name, shape, dt: cx.enter_context(nc.sbuf_tensor(name, shape, dt))
        w_dq = sb("w_dq", [128, 8 * LAT], BF16)
        w_uq = sb("w_uq", [128, 2 * D], BF16)
        w_dkv = sb("w_dkv", [128, 8 * LAT], BF16)
        w_ukv = sb("w_ukv", [128, 2 * 2 * D], BF16)
        w_oT = sb("w_oT", [128, 8 * D], BF16)
        w_rT = sb("w_rT", [128, 8 * D], BF16)
        w_kT = sb("w_kT", [128, 8 * D], BF16)
        w_vT = sb("w_vT", [128, 8 * D], BF16)
        mu_rc = sb("mu_rc", [128, 8], F32)
        mu_kc = sb("mu_kc", [128, 8], F32)
        gbq = sb("gbq", [128, LAT], F32)
        bbq = sb("bbq", [128, LAT], F32)
        gbk = sb("gbk", [128, LAT], F32)
        bbk = sb("bbk", [128, LAT], F32)
        ident = sb("ident", [128, 128], BF16)
        onesb = sb("onesb", [128, 128], BF16)
        mask_sb = sb("mask_sb", [128, 4 * TS], BF16)
        carry_all = sb("carry_all", [128, 64], BF16)
        yrows_all = sb("yrows_all", [128, 64], BF16)
        carry_A = sb("carry_A", [128, 64], BF16)
        carry_B = sb("carry_B", [128, 64], BF16)
        gA_col = sb("gA_col", [128, 1], F32)
        gB_col = sb("gB_col", [128, 1], F32)
        cfg_sb = sb("cfg_sb", [1, 4], F32)
        xt = sb("xt", [128, 2 * D], F32)
        wload = xt
        xln = sb("xln", [128, 2 * D], BF16)
        xfm = sb("xfm", [128, 8 * TS], BF16)
        cq_sb = sb("cq_sb", [128, 2 * LAT], F32)
        scr_s = sb("scr_s", [128, 2 * LAT], F32)
        cqn = sb("cqn", [128, 2 * LAT], BF16)
        cqfm = sb("cqfm", [128, 2 * LAT], BF16)
        qfm = sb("qfm", [128, 8 * TS], BF16)
        kstage = sb("kstage", [128, 2 * D], BF16)
        vstage = sb("vstage", [128, 2 * NH * 65], BF16)
        k2s = sb("k2s", [128, D], BF16)
        kbuf = sb("kbuf", [128, 6 * 256], BF16)
        vbuf = sb("vbuf", [128, 6 * 260], BF16)
        pbuf = sb("pbuf", [128, 4 * 1024], BF16)
        aev = sb("aev", [128, 4 * 256], BF16)
        atok = sb("atok", [128, 8 * 64], BF16)
        rc = sb("rc", [128, 8], F32)
        attn_fm = sb("attn_fm", [128, 8 * TS], BF16)
        x1 = sb("x1", [128, 2 * D], F32)
        ytm = xln
        yfm = sb("yfm", [128, 8 * TS], BF16)
        dscr = sb("dscr", [128, 8 * TS], BF16)
        q2 = dscr
        xr = sb("xr", [128, 8 * TS], BF16)
        xk = sb("xk", [128, 8 * TS], BF16)
        r_sb = sb("r_sb", [128, 2 * D], F32)
        e_scr = sb("e_scr", [128, 2 * D], F32)
        kact = dscr
        kactfm = sb("kactfm", [128, 8 * TS], BF16)
        o_sb = sb("o_sb", [128, 2 * D], F32)
        x2 = xt
        vrow = xt[0:1, 0:4 * LAT]
        vrow_b = pbuf[0:1, 0:8 * LAT]
        jnk = sb("jnk", [128, D], BF16)
        st = sb("st", [128, 16], F32)
        sc = sb("sc", [1, 16], F32)
        scb = sb("scb", [1, 16], BF16)
        bias_b = sb("bias_b", [128, 1], F32)
        cvals = sb("cvals", [128, 4], F32)

        PS = cx.enter_context(nc.psum_tensor("PS", [128, 4096], F32))

        sm_in = Sem(cx.enter_context(nc.semaphore("sm_in")))
        sm_sp = Sem(cx.enter_context(nc.semaphore("sm_sp")))
        sm_out = Sem(cx.enter_context(nc.semaphore("sm_out")))
        sm_pe = Sem(cx.enter_context(nc.semaphore("sm_pe")))
        sm_act = Sem(cx.enter_context(nc.semaphore("sm_act")))
        sm_dve = Sem(cx.enter_context(nc.semaphore("sm_dve")))
        sm_pl = Sem(cx.enter_context(nc.semaphore("sm_pl")))
        sm_kvs = [Sem(cx.enter_context(nc.semaphore(f"sm_kv{i}"))) for i in range(6)]
        sm_spl = [Sem(cx.enter_context(nc.semaphore(f"sm_spl{i}"))) for i in range(4)]
        sm_cl = Sem(cx.enter_context(nc.semaphore("sm_cl")))

        P = Planner()
        P.set_own_sems({"vector": sm_dve, "scalar": sm_act, "gpsimd": sm_pl})
        BK = lambda b, off=0, n=512: PS[:, b * 512 + off: b * 512 + off + n]

        P.op("vector", lambda v: v.memset(cvals[:, 0:1], EPS), inc=sm_dve)
        P.op("vector", lambda v: v.memset(cvals[:, 1:2], 1.0), inc=sm_dve)
        P.op("vector", lambda v: v.memset(cvals[:, 2:3], 1e-12), inc=sm_dve)
        cv_mark = (sm_dve, sm_dve.n)
        C_EPS = cvals[:, 0:1]
        C_ONE = cvals[:, 1:2]
        C_TINY = cvals[:, 2:3]

        P.dma("sync", ident[:], ident_d.ap(), inc=sm_in)
        P.dma("sync", onesb[:], ones_d.ap(), inc=sm_in)
        for _s in range(4):
            P.dma("sync", mask_sb[:, _s * TS:(_s + 1) * TS], maskin[_s], inc=sm_in)
        P.dma("sync", cfg_sb[0:1, 0:2].bitcast(mybir.dt.int32), cfgi_in, inc=sm_in)
        P.dma("sync", cfg_sb[0:1, 2:4], cfgf_in, inc=sm_in)
        for c in range(8):
            P.dma("sync", mu_rc[:, c:c + 1],
                  mu_r_in[c * 128:(c + 1) * 128].rearrange("(p o) -> p o", o=1), inc=sm_in)
            P.dma("sync", mu_kc[:, c:c + 1],
                  mu_k_in[c * 128:(c + 1) * 128].rearrange("(p o) -> p o", o=1), inc=sm_in)
        for i, src_ap in enumerate((qg_in, qb_in, kg_in, kb_in)):
            P.dma("sync", vrow[0:1, i * LAT:(i + 1) * LAT],
                  src_ap.rearrange("(o l) -> o l", o=1), inc=sm_in)
        consts_ld = (sm_in, sm_in.n)

        P.op("vector", lambda v: v.tensor_copy(out=vrow_b[0:1, 0:4 * LAT], in_=vrow[:]),
             waits=[consts_ld], inc=sm_dve)
        P.op("vector", lambda v: v.tensor_tensor(out=vrow_b[0:1, 4 * LAT:8 * LAT],
                                                 in0=vrow[:], in1=vrow_b[0:1, 0:4 * LAT],
                                                 op=ALU.subtract), inc=sm_dve)
        gb_prep = (sm_dve, sm_dve.n)
        for i, dst in enumerate((gbq, bbq, gbk, bbk)):
            ii = i
            P.op("tensor", lambda t, ii=ii: t.matmul(
                BK(0, 0, LAT), onesb[0:1, :], vrow_b[0:1, ii * LAT:(ii + 1) * LAT],
                start=True, stop=False), waits=[gb_prep, (sm_dve, sm_dve.n)])
            mk = P.op("tensor", lambda t, ii=ii: t.matmul(
                BK(0, 0, LAT), onesb[0:1, :], vrow_b[0:1, (4 + ii) * LAT:(5 + ii) * LAT],
                start=False, stop=True), inc=sm_pe)
            P.op("vector", lambda v, dst=dst: v.tensor_copy(out=dst[:], in_=BK(0, 0, LAT)),
                 waits=[mk], inc=sm_dve)
        gb_done = (sm_dve, sm_dve.n)

        def load_cast(dst, src_ap, cols, waits=()):
            ld = P.dma("sync", wload[:, 0:cols], src_ap, waits=list(waits) + [gb_done],
                       inc=sm_in)
            return P.op("vector", lambda v: v.tensor_copy(out=dst, in_=wload[:, 0:cols]),
                        waits=[ld], inc=sm_dve)

        prev = None
        for c in range(8):
            prev = load_cast(w_dq[:, c * LAT:(c + 1) * LAT],
                             w_dq_in[c * 128:(c + 1) * 128, :], LAT,
                             waits=[prev] if prev else ())
        for c in range(8):
            prev = load_cast(w_dkv[:, c * LAT:(c + 1) * LAT],
                             w_dkv_in[c * 128:(c + 1) * 128, :], LAT, waits=[prev])
        for l in range(2):
            prev = load_cast(w_uq[:, l * D:(l + 1) * D],
                             w_uq_in[l * 128:(l + 1) * 128, :], D, waits=[prev])
        for l in range(2):
            prev = load_cast(w_ukv[:, l * 2 * D:(l + 1) * 2 * D],
                             w_ukv_in[l * 128:(l + 1) * 128, :], 2 * D, waits=[prev])

        for (w_nat, w_t) in ((w_o_in, w_oT), (w_r_in, w_rT),
                             (w_k_in, w_kT), (w_v_in, w_vT)):
            for i in range(8):
                ld = P.dma("sync", wload[:, 0:D], w_nat[i * 128:(i + 1) * 128, :],
                           waits=[prev], inc=sm_in)
                cst = P.op("vector", lambda v: v.tensor_copy(out=xln[:, 0:D],
                                                             in_=wload[:, 0:D]),
                           waits=[ld], inc=sm_dve)
                mks = []
                for c in range(8):
                    mk = P.op("tensor", lambda t, c=c: t.matmul(
                        BK(0, 0, 1024 * 2)[:, c * 128:(c + 1) * 128],
                        xln[:, c * 128:(c + 1) * 128], ident[:],
                        start=True, stop=True), waits=[cst], inc=sm_pe)
                    mks.append(mk)
                prev = P.op("scalar", lambda s, w_t=w_t, i=i: s.activation(
                    out=w_t.rearrange("p (c f) -> p c f", c=8)[:, :, i * 128:(i + 1) * 128],
                    in_=PS[:, 0:1024].rearrange("p (c f) -> p c f", c=8),
                    func=AF.Copy), waits=[mks[-1], cv_mark], inc=sm_act)
        w_done = prev
        P.op("gpsimd", lambda g: g.memset(vstage[:], 1.0), inc=sm_pl)
        P.op("gpsimd", lambda g: g.memset(yrows_all[:], 0.0), inc=sm_pl)
        P.op("gpsimd", lambda g: g.memset(sc[:], 0.0), inc=sm_pl)
        p0_pool = (sm_pl, sm_pl.n)

        def ln_stats(src_ap, n_elems, slot, waits, src_is_psum=False):
            s_sum = st[:, slot:slot + 1]
            s_sq = st[:, slot + 1:slot + 2]
            if src_is_psum:
                raise NotImplementedError
            d1 = P.op("vector", lambda v: v.tensor_reduce(
                out=s_sum, in_=src_ap, axis=AX.X, op=ALU.add), waits=waits, inc=sm_dve)
            d2 = P.op("vector", lambda v: v.scalar_tensor_tensor(
                out=jnk[:, 0:n_elems], in0=src_ap, scalar=1.0, in1=src_ap,
                op0=ALU.bypass, op1=ALU.mult, accum_out=s_sq), waits=waits, inc=sm_dve)
            m = st[:, slot + 2:slot + 3]
            d3 = P.op("vector", lambda v: v.tensor_scalar(
                out=m, in0=s_sum, scalar1=1.0 / n_elems, scalar2=0.0,
                op0=ALU.mult, op1=ALU.add), waits=[d1], inc=sm_dve)
            d4 = P.op("vector", lambda v: v.scalar_tensor_tensor(
                out=st[:, slot + 3:slot + 4], in0=m, scalar=1.0, in1=m,
                op0=ALU.mult, op1=ALU.mult), waits=[d3], inc=sm_dve)
            d5 = P.op("vector", lambda v: v.tensor_scalar(
                out=s_sq, in0=s_sq, scalar1=1.0 / n_elems,
                scalar2=st[:, slot + 3:slot + 4],
                op0=ALU.mult, op1=ALU.subtract), waits=[d2, d4], inc=sm_dve)
            d6 = P.op("scalar", lambda s: s.activation(
                out=s_sq, in_=s_sq, func=AF.Ln, bias=C_EPS), waits=[d5], inc=sm_act)
            d7 = P.op("scalar", lambda s: s.activation(
                out=s_sum, in_=s_sq, func=AF.Exp, scale=-0.5), waits=[d6], inc=sm_act)
            d8 = P.op("vector", lambda v: v.scalar_tensor_tensor(
                out=m, in0=m, scalar=-1.0, in1=s_sum,
                op0=ALU.mult, op1=ALU.mult), waits=[d7], inc=sm_dve)
            return d8

        def ln_apply(dst_ap, src_ap, slot, waits, eng="vector"):
            return P.op(eng, lambda v: v.tensor_scalar(
                out=dst_ap, in0=src_ap, scalar1=st[:, slot:slot + 1],
                scalar2=st[:, slot + 2:slot + 3],
                op0=ALU.mult, op1=ALU.add), waits=waits,
                inc=sm_dve if eng == "vector" else sm_pl)

        def idmm_batch(blocks, evac_out, evac_in, waits, evac_waits=()):
            mks = []
            for lhsT_ap, po in blocks:
                mk = P.op("tensor", lambda t, a=lhsT_ap, po=po: t.matmul(
                    PS[:, po:po + 128], a, ident[:], start=True, stop=True),
                    waits=waits if not mks else (), inc=sm_pe)
                mks.append(mk)
            ev = P.op("scalar", lambda s: s.activation(
                out=evac_out, in_=evac_in, func=AF.Copy),
                waits=[mks[-1]] + list(evac_waits), inc=sm_act)
            return mks[-1], ev

        spill_marks = []
        k2_chain_mark = None
        pv = {"ap1": [None, None], "mmx": [None, None], "evx": None,
              "mkv": None, "evk": None, "evv": None, "evc": None,
              "vmm": [None, None], "sp1": [None, None], "sp2": [None, None],
              "kmm2": None, "rm": None, "ev2": [None, None]}
        for kt in range(nkt):
            s = kt % 2
            ldx = P.dma("sync", xt[:, s * D:(s + 1) * D],
                        x_in[kt * 128:(kt + 1) * 128, :],
                        waits=[pv["ap1"][s]] if pv["ap1"][s] is not None
                        else [w_done, p0_pool], inc=sm_in)
            stt = ln_stats(xt[:, s * D:(s + 1) * D], D, 0, [ldx])
            ap1 = ln_apply(xln[:, s * D:(s + 1) * D], xt[:, s * D:(s + 1) * D],
                           0, [stt, pv["mmx"][s]])
            pv["ap1"][s] = ap1
            blocks = [(xln[:, s * D + c * 128: s * D + (c + 1) * 128], c * 128)
                      for c in range(8)]
            mmx, evx = idmm_batch(
                blocks,
                xfm.rearrange("p (c f) -> p c f", c=8)[:, :, 0:128],
                PS[:, 0:1024].rearrange("p (c f) -> p c f", c=8),
                [ap1, pv["evx"]], evac_waits=[pv["mkv"]])
            pv["mmx"][s] = mmx
            pv["evx"] = evx
            mkv = None
            for c in range(8):
                mkv = P.op("tensor", lambda t, c=c, s=s: t.matmul(
                    PS[:, 1536 + s * 256: 1792 + s * 256],
                    xfm[:, c * TS:c * TS + 128],
                    w_dkv[:, c * LAT:(c + 1) * LAT],
                    start=c == 0, stop=c == 7),
                    waits=[evx, pv["ev2"][s]] if c == 0 else (), inc=sm_pe)
            pv["mkv"] = mkv
            ev2 = P.op("vector", lambda v, s=s: v.tensor_scalar(
                out=cq_sb[:, 0:LAT], in0=PS[:, 1536 + s * 256: 1792 + s * 256],
                scalar1=1.0, scalar2=0.0,
                op0=ALU.mult, op1=ALU.add, accum_out=st[:, 4:5]), waits=[mkv], inc=sm_dve)
            pv["ev2"][s] = ev2
            sq2 = P.op("vector", lambda v: v.scalar_tensor_tensor(
                out=jnk[:, 0:LAT], in0=cq_sb[:, 0:LAT], scalar=1.0, in1=cq_sb[:, 0:LAT],
                op0=ALU.bypass, op1=ALU.mult, accum_out=st[:, 5:6]), waits=[ev2], inc=sm_dve)
            d3 = P.op("vector", lambda v: v.tensor_scalar(
                out=st[:, 6:7], in0=st[:, 4:5], scalar1=1.0 / LAT, scalar2=0.0,
                op0=ALU.mult, op1=ALU.add), waits=[ev2], inc=sm_dve)
            d4 = P.op("vector", lambda v: v.scalar_tensor_tensor(
                out=st[:, 7:8], in0=st[:, 6:7], scalar=1.0, in1=st[:, 6:7],
                op0=ALU.mult, op1=ALU.mult), waits=[d3], inc=sm_dve)
            d5 = P.op("vector", lambda v: v.tensor_scalar(
                out=st[:, 5:6], in0=st[:, 5:6], scalar1=1.0 / LAT, scalar2=st[:, 7:8],
                op0=ALU.mult, op1=ALU.subtract), waits=[sq2, d4], inc=sm_dve)
            d6 = P.op("scalar", lambda s2: s2.activation(
                out=st[:, 5:6], in_=st[:, 5:6], func=AF.Ln, bias=C_EPS), waits=[d5], inc=sm_act)
            d7 = P.op("scalar", lambda s2: s2.activation(
                out=st[:, 4:5], in_=st[:, 5:6], func=AF.Exp, scale=-0.5), waits=[d6], inc=sm_act)
            d8 = P.op("vector", lambda v: v.scalar_tensor_tensor(
                out=st[:, 6:7], in0=st[:, 6:7], scalar=-1.0, in1=st[:, 4:5],
                op0=ALU.mult, op1=ALU.mult), waits=[d7], inc=sm_dve)
            a1 = P.op("vector", lambda v: v.tensor_scalar(
                out=scr_s[:, 0:LAT], in0=cq_sb[:, 0:LAT], scalar1=st[:, 4:5],
                scalar2=st[:, 6:7], op0=ALU.mult, op1=ALU.add), waits=[d8], inc=sm_dve)
            a2 = P.op("vector", lambda v: v.scalar_tensor_tensor(
                out=scr_s[:, LAT:2 * LAT], in0=scr_s[:, 0:LAT], scalar=1.0,
                in1=gbk[:], op0=ALU.mult, op1=ALU.mult), waits=[a1, gb_done], inc=sm_dve)
            a3 = P.op("vector", lambda v: v.tensor_tensor(
                out=cqn[:, 0:LAT], in0=scr_s[:, LAT:2 * LAT], in1=bbk[:],
                op=ALU.add), waits=[a2], inc=sm_dve)
            co = s * 256
            blocks2 = [(cqn[:, l * 128:(l + 1) * 128], 1024 + l * 128) for l in range(2)]
            _, evc = idmm_batch(
                blocks2, cqfm[:, co:co + 256].rearrange("p (l f) -> p l f", l=2),
                PS[:, 1024:1280].rearrange("p (l f) -> p l f", l=2),
                [a3, pv["evc"]], evac_waits=[pv["vmm"][s]])
            pv["evc"] = evc
            kmm = None
            for c in range(8):
                for l in range(2):
                    first = (c == 0 and l == 0) or (c == 4 and l == 0)
                    stop = (c == 3 and l == 1) or (c == 7 and l == 1)
                    kmm = P.op("tensor", lambda t, c=c, l=l, first=first, stop=stop,
                               co=co: t.matmul(
                        PS[:, 2048 + c * 128: 2048 + c * 128 + 128],
                        w_ukv[:, l * 2048 + c * 128: l * 2048 + c * 128 + 128],
                        cqfm[:, co + l * 128: co + (l + 1) * 128],
                        start=first, stop=stop),
                        waits=[evc, pv["evk"]] if (c == 0 and l == 0) else (), inc=sm_pe)
            evk = P.op("scalar", lambda s2, s=s: s2.activation(
                out=kstage[:, s * D:(s + 1) * D], in_=PS[:, 2048:3072], func=AF.Copy),
                waits=[kmm, pv["sp1"][s]], inc=sm_act)
            pv["evk"] = evk
            vmm = None
            for n in range(2):
                for l in range(2):
                    vmm = P.op("tensor", lambda t, n=n, l=l, co=co: t.matmul(
                        PS[:, 3072 + n * 512: 3072 + (n + 1) * 512],
                        cqfm[:, co + l * 128: co + (l + 1) * 128],
                        w_ukv[:, l * 2048 + D + n * 512: l * 2048 + D + (n + 1) * 512],
                        start=l == 0, stop=l == 1),
                        waits=[pv["evv"]] if (n == 0 and l == 0) else (), inc=sm_pe)
            pv["vmm"][s] = vmm
            evv = P.op("scalar", lambda s2, s=s: s2.activation(
                out=vstage[:, s * NH * 65:(s + 1) * NH * 65]
                    .rearrange("p (h f) -> p h f", h=NH)[:, :, 0:64],
                in_=PS[:, 3072:4096].rearrange("p (h f) -> p h f", h=NH),
                func=AF.Copy), waits=[vmm, pv["sp2"][s], p0_pool], inc=sm_act)
            pv["evv"] = evv
            k2p = P.op("gpsimd", lambda g, s=s: g.tensor_tensor(
                out=k2s[:], in0=kstage[:, s * D:(s + 1) * D],
                in1=kstage[:, s * D:(s + 1) * D], op=ALU.mult),
                waits=[evk, pv["kmm2"]], inc=sm_pl)
            kmm2 = None
            for c in range(8):
                kmm2 = P.op("tensor", lambda t, c=c: t.matmul(
                    PS[0:1, 1280:1408], onesb[:, 0:1], k2s[:, c * 128:(c + 1) * 128],
                    start=c == 0, stop=c == 7),
                    waits=[k2p, pv["rm"]] if c == 0 else (), inc=sm_pe)
            pv["kmm2"] = kmm2
            rm = P.op("vector", lambda v: v.tensor_reduce(
                out=sc[0:1, 1:2], in_=PS[0:1, 1280:1408], axis=AX.X, op=ALU.max),
                waits=[kmm2], inc=sm_dve)
            pv["rm"] = rm
            k2_chain_mark = P.op("vector", lambda v: v.tensor_max(
                out=sc[0:1, 0:1], in0=sc[0:1, 0:1], in1=sc[0:1, 1:2]),
                waits=[rm, p0_pool], inc=sm_dve)
            sp1 = P.dma("sync", k_dram[kt], kstage[:, s * D:(s + 1) * D],
                        waits=[evk], inc=sm_sp)
            sp2 = P.dma("sync", v_dram[kt],
                        vstage[:, s * NH * 65:(s + 1) * NH * 65],
                        waits=[evv], inc=sm_sp, chain=False)
            pv["sp1"][s] = sp2
            pv["sp2"][s] = sp2
            spill_marks.append((sp2[0], sp2[1]))
        p1_done = [k2_chain_mark, spill_marks[-1]]

        out_marks = []
        spillj_marks = []
        yrc_marks = []
        for t in range(nt):
            nkt_t = 4 * t + 4
            ldx0 = P.dma("sync", xt[:, 0:D], xq_in[t * TS:t * TS + 128, :],
                         waits=p1_done if t == 0 else [spillj_marks[-1]], inc=sm_in)
            ldx1 = P.dma("sync", xt[:, D:2 * D], xq_in[t * TS + 128:t * TS + 256, :],
                         inc=sm_in, chain=False)
            xfm_evs = []
            for s_i, ldm in ((0, ldx1), (1, ldx1)):
                stt = ln_stats(xt[:, s_i * D:(s_i + 1) * D], D, 0 + 4 * s_i, [ldm])
                ap1 = ln_apply(xln[:, s_i * D:(s_i + 1) * D],
                               xt[:, s_i * D:(s_i + 1) * D], 0 + 4 * s_i, [stt])
                blocks = [(xln[:, s_i * D + c * 128: s_i * D + (c + 1) * 128],
                           c * 128) for c in range(8)]
                _, evx = idmm_batch(
                    blocks,
                    xfm.rearrange("p (c f) -> p c f", c=8)[:, :, s_i * 128:(s_i + 1) * 128],
                    PS[:, 0:1024].rearrange("p (c f) -> p c f", c=8),
                    [ap1] + (xfm_evs[-1:]))
                xfm_evs.append(evx)
            q_evs = []
            evc_prev = None
            for s_i in range(2):
                for c in range(8):
                    mcq = P.op("tensor", lambda tt, c=c, s_i=s_i: tt.matmul(
                        BK(0, 0, LAT), xfm[:, c * TS + s_i * 128: c * TS + s_i * 128 + 128],
                        w_dq[:, c * LAT:(c + 1) * LAT], start=c == 0, stop=c == 7),
                        waits=list(xfm_evs) if c == 0 else (), inc=sm_pe)
                ev2 = P.op("vector", lambda v: v.tensor_scalar(
                    out=cq_sb[:, 0:LAT], in0=BK(0, 0, LAT), scalar1=1.0, scalar2=0.0,
                    op0=ALU.mult, op1=ALU.add, accum_out=st[:, 8:9]), waits=[mcq], inc=sm_dve)
                sq2 = P.op("vector", lambda v: v.scalar_tensor_tensor(
                    out=jnk[:, 0:LAT], in0=cq_sb[:, 0:LAT], scalar=1.0, in1=cq_sb[:, 0:LAT],
                    op0=ALU.bypass, op1=ALU.mult, accum_out=st[:, 9:10]), waits=[ev2], inc=sm_dve)
                d3 = P.op("vector", lambda v: v.tensor_scalar(
                    out=st[:, 10:11], in0=st[:, 8:9], scalar1=1.0 / LAT, scalar2=0.0,
                    op0=ALU.mult, op1=ALU.add), waits=[ev2], inc=sm_dve)
                d4 = P.op("vector", lambda v: v.scalar_tensor_tensor(
                    out=st[:, 11:12], in0=st[:, 10:11], scalar=1.0, in1=st[:, 10:11],
                    op0=ALU.mult, op1=ALU.mult), waits=[d3], inc=sm_dve)
                d5 = P.op("vector", lambda v: v.tensor_scalar(
                    out=st[:, 9:10], in0=st[:, 9:10], scalar1=1.0 / LAT, scalar2=st[:, 11:12],
                    op0=ALU.mult, op1=ALU.subtract), waits=[sq2, d4], inc=sm_dve)
                d6 = P.op("scalar", lambda s: s.activation(
                    out=st[:, 9:10], in_=st[:, 9:10], func=AF.Ln, bias=C_EPS),
                    waits=[d5], inc=sm_act)
                d7 = P.op("scalar", lambda s: s.activation(
                    out=st[:, 8:9], in_=st[:, 9:10], func=AF.Exp, scale=-0.5),
                    waits=[d6], inc=sm_act)
                d8 = P.op("vector", lambda v: v.scalar_tensor_tensor(
                    out=st[:, 10:11], in0=st[:, 10:11], scalar=-1.0, in1=st[:, 8:9],
                    op0=ALU.mult, op1=ALU.mult), waits=[d7], inc=sm_dve)
                a1 = P.op("vector", lambda v: v.tensor_scalar(
                    out=scr_s[:, 0:LAT], in0=cq_sb[:, 0:LAT], scalar1=st[:, 8:9],
                    scalar2=st[:, 10:11], op0=ALU.mult, op1=ALU.add), waits=[d8], inc=sm_dve)
                a2 = P.op("vector", lambda v: v.scalar_tensor_tensor(
                    out=scr_s[:, LAT:2 * LAT], in0=scr_s[:, 0:LAT], scalar=1.0,
                    in1=gbq[:], op0=ALU.mult, op1=ALU.mult), waits=[a1], inc=sm_dve)
                a3 = P.op("vector", lambda v: v.tensor_tensor(
                    out=cqn[:, 0:LAT], in0=scr_s[:, LAT:2 * LAT], in1=bbq[:],
                    op=ALU.add), waits=[a2], inc=sm_dve)
                blocks2 = [(cqn[:, l * 128:(l + 1) * 128], 512 + l * 128) for l in range(2)]
                _, evc = idmm_batch(
                    blocks2,
                    cqfm.rearrange("p (l f) -> p l f", l=2)[:, :, s_i * 128:(s_i + 1) * 128],
                    PS[:, 512:768].rearrange("p (l f) -> p l f", l=2),
                    [a3] + ([evc_prev] if (s_i == 1 and evc_prev is not None) else []))
                evc_prev = evc
                qmm = None
                for c in range(8):
                    for l in range(2):
                        first = (l == 0 and c % 2 == 0)
                        stop = (l == 1 and c % 2 == 1)
                        qmm = P.op("tensor", lambda tt, c=c, l=l, first=first, stop=stop, s_i=s_i: tt.matmul(
                            PS[:, 2048 + c * 256 + s_i * 128: 2048 + c * 256 + s_i * 128 + 128],
                            w_uq[:, l * D + c * 128: l * D + (c + 1) * 128],
                            cqfm[:, l * LAT + s_i * 128: l * LAT + s_i * 128 + 128],
                            start=first, stop=stop),
                            waits=([evc] + q_evs[-1:]) if (c == 0 and l == 0) else (),
                            inc=sm_pe)
                ev_q = P.op("scalar", lambda s, s_i=s_i: s.activation(
                    out=qfm.rearrange("p (c f) -> p c f", c=8)[:, :, s_i * 128:(s_i + 1) * 128],
                    in_=PS[:, 2048:4096].rearrange("p (c f) -> p c f", c=8)[:, :, s_i * 128:(s_i + 1) * 128],
                    func=AF.Copy), waits=[qmm], inc=sm_act)
                q_evs.append(ev_q)
            q2p = P.op("gpsimd", lambda g: g.tensor_tensor(
                out=q2[:], in0=qfm[:], in1=qfm[:], op=ALU.mult),
                waits=q_evs, inc=sm_pl)
            for c in range(8):
                qmm2 = P.op("tensor", lambda tt, c=c: tt.matmul(
                    PS[0:1, 0:TS], onesb[:, 0:1], q2[:, c * TS:(c + 1) * TS],
                    start=c == 0, stop=c == 7),
                    waits=[q2p] + xfm_evs if c == 0 else (), inc=sm_pe)
            b1 = P.op("vector", lambda v: v.tensor_reduce(
                out=sc[0:1, 2:3], in_=PS[0:1, 0:TS], axis=AX.X, op=ALU.max),
                waits=[qmm2], inc=sm_dve)
            b2 = P.op("vector", lambda v: v.tensor_mul(
                out=sc[0:1, 3:4], in0=sc[0:1, 2:3], in1=sc[0:1, 0:1]),
                waits=[b1], inc=sm_dve)
            b3 = P.op("scalar", lambda s: s.activation(
                out=sc[0:1, 3:4], in_=sc[0:1, 3:4], func=AF.Ln, bias=cvals[0:1, 2:3]),
                waits=[b2], inc=sm_act)
            b4 = P.op("scalar", lambda s: s.activation(
                out=sc[0:1, 3:4], in_=sc[0:1, 3:4], func=AF.Exp, scale=0.5),
                waits=[b3], inc=sm_act)
            b5 = P.op("vector", lambda v: v.tensor_scalar(
                out=scb[0:1, 0:1], in0=sc[0:1, 3:4], scalar1=-1.02 * INV_SQRT_DH,
                scalar2=0.0, op0=ALU.mult, op1=ALU.add), waits=[b4], inc=sm_dve)
            b6 = P.op("tensor", lambda tt: tt.matmul(
                PS[:, 256:257], onesb[0:1, :], scb[0:1, 0:1], start=True, stop=True),
                waits=[b5], inc=sm_pe)
            bias_mark = P.op("vector", lambda v: v.tensor_copy(
                out=bias_b[:], in_=PS[:, 256:257]), waits=[b6], inc=sm_dve)

            if stages == "qpath":
                om = P.dma("sync", out[t * TS: t * TS + 128, :], xt[:, 0:D],
                           waits=[bias_mark], inc=sm_out)
                out_marks.append(om)
                continue
            sub_pv = stages not in ("attn_scores", "attn_mm", "attn_ld")
            sub_epi = stages not in ("attn_scores", "attn_pv", "attn_mm", "attn_ld")
            attn_ev_marks = []
            pv_last_prev_quarter = None
            for g in range(4):
                kt_loads = []
                score_marks = []
                exp_marks = []
                pv_marks = []
                mask_marks = {}
                LAG = 3
                for kt in range(nkt_t + LAG):
                  if kt < nkt_t:
                    slot = kt % 6
                    w_sp = spill_marks[kt]
                    skv = sm_kvs[slot]
                    war = [score_marks[kt - 6], pv_marks[kt - 6]] if kt >= 6 else \
                        ([pv_last_prev_quarter] if pv_last_prev_quarter else [])
                    l1 = P.dma("sync", kbuf[:, slot * 256:(slot + 1) * 256],
                               k_dram[kt][:, g * 256:(g + 1) * 256],
                               waits=[w_sp] + war, inc=skv)
                    l2 = P.dma("sync", vbuf[:, slot * 260: slot * 260 + 260],
                               v_dram[kt][:, g * 260:(g + 1) * 260],
                               inc=skv, chain=False)
                    kt_loads.append((l2[0], l2[1]))
                    pb = (kt % 2) * 2
                    deps = [kt_loads[kt]]
                    if kt >= 2:
                        deps.append(exp_marks[kt - 2])
                    if kt == 0:
                        deps.append(bias_mark)
                    if kt <= 1 and attn_ev_marks:
                        deps.append(attn_ev_marks[-1])
                    smm = None
                    if stages == "attn_ld":
                        score_marks.append(kt_loads[kt]); exp_marks.append(kt_loads[kt])
                        pv_marks.append(kt_loads[kt])
                        continue
                    for hh in range(4):
                        pr = hh % 2
                        ch = 2 * g + hh // 2
                        soff = pr * 512 + (hh // 2) * 256
                        smm = P.op("tensor", lambda tt, hh=hh, pr=pr, ch=ch, pb=pb, slot=slot, soff=soff: tt.matmul(
                            PS[:, pb * 512 + soff: pb * 512 + soff + 256],
                            kbuf[pr * 64:(pr + 1) * 64,
                                 slot * 256 + (hh // 2) * 128: slot * 256 + (hh // 2) * 128 + 128],
                            qfm[pr * 64:(pr + 1) * 64, ch * TS:(ch + 1) * TS],
                            start=(hh // 2) == 0, stop=(hh // 2) == 1,
                            tile_position=(pr * 64, 0)),
                            waits=deps if hh == 0 else (), inc=sm_pe)
                    score_marks.append(smm)
                    pslot = kt % 4
                    edeps = [smm]
                    if kt >= 4 and kt - 4 - LAG + 1 > 0:
                        pass
                    if kt - 4 >= 0:
                        edeps.append(pv_marks[kt - 4] if kt - 4 < len(pv_marks) else None)
                    if stages == "attn_mm":
                        em = P.op("scalar", lambda s: s.nop(), waits=edeps, inc=sm_act)
                    else:
                        em = P.op("scalar", lambda s, pb=pb, pslot=pslot: s.activation(
                            out=pbuf[:, pslot * 1024:(pslot + 1) * 1024],
                            in_=PS[:, pb * 512: pb * 512 + 1024],
                            func=AF.Exp, bias=bias_b[:, 0:1], scale=INV_SQRT_DH),
                            waits=edeps, inc=sm_act)
                    exp_marks.append(em)
                    if sub_pv and kt >= nkt_t - 4:
                        sidx = kt - (nkt_t - 4)
                        mm_ = None
                        for hh in range(4):
                            hoff = (hh % 2) * 512 + (hh // 2) * 256
                            mm_ = P.op("gpsimd", lambda gg, sidx=sidx, pslot=pslot, hoff=hoff: gg.tensor_tensor(
                                out=pbuf[:, pslot * 1024 + hoff: pslot * 1024 + hoff + 256],
                                in0=pbuf[:, pslot * 1024 + hoff: pslot * 1024 + hoff + 256],
                                in1=mask_sb[:, sidx * TS:(sidx + 1) * TS], op=ALU.mult),
                                waits=[em] if hh == 0 else (), inc=sm_pl)
                        mask_marks[kt] = mm_
                  kp = kt - LAG
                  if kp >= 0 and sub_pv and stages != "attn_ld":
                    pv_dep = [mask_marks.get(kp, exp_marks[kp])]
                    pslotp = kp % 4
                    slotp = kp % 6
                    pvm = None
                    for hh in range(4):
                        pvm = P.op("tensor", lambda tt, hh=hh, pslot=pslotp, slot=slotp, kt=kp, nkt_t=nkt_t: tt.matmul(
                            PS[0:65, (4 + hh) * 512: (4 + hh) * 512 + 256],
                            vbuf[:, slot * 260 + hh * 65: slot * 260 + (hh + 1) * 65],
                            pbuf[:, pslot * 1024 + (hh % 2) * 512 + (hh // 2) * 256:
                                 pslot * 1024 + (hh % 2) * 512 + (hh // 2) * 256 + 256],
                            start=kt == 0, stop=kt == nkt_t - 1),
                            waits=pv_dep if hh == 0 else (), inc=sm_pe)
                    pv_marks.append(pvm)
                  elif kp >= 0 and (not sub_pv) and stages != "attn_ld":
                    pv_marks.append(exp_marks[kp])
                pv_last_prev_quarter = pv_marks[-1]
                if not sub_epi:
                    attn_ev_marks.append(pv_marks[-1])
                    continue
                evq = P.op("scalar", lambda s: s.activation(
                    out=aev[0:65, :].rearrange("p (h f) -> p h f", h=4),
                    in_=PS[0:65, 2048:4096].rearrange("p (h f) -> p h f", h=4)[:, :, 0:256],
                    func=AF.Copy), waits=[pv_marks[-1]], inc=sm_act)
                tmks = []
                for sl in range(8):
                    hh, qh = sl // 2, sl % 2
                    po = 65 * sl if sl < 7 else 512
                    tmks.append(P.op("tensor", lambda tt, hh=hh, qh=qh, po=po: tt.matmul(
                        PS[0:128, po:po + 65],
                        aev[0:65, hh * 256 + qh * 128: hh * 256 + qh * 128 + 128],
                        ident[0:65, 0:65], start=True, stop=True),
                        waits=[evq, exp_marks[-1], exp_marks[-2] if nkt_t >= 2 else evq]
                        if sl == 0 else (), inc=sm_pe))
                nmks = []
                for sl in range(8):
                    po = 65 * sl if sl < 7 else 512
                    r1 = P.op("vector", lambda v, po=po, sl=sl: v.reciprocal(
                        out=rc[:, sl:sl + 1], in_=PS[:, po + 64: po + 65]),
                        waits=[tmks[-1]] if sl == 0 else (), inc=sm_dve)
                    n1 = P.op("vector", lambda v, po=po, sl=sl: v.tensor_scalar(
                        out=atok[:, sl * 64:(sl + 1) * 64], in0=PS[:, po:po + 64],
                        scalar1=rc[:, sl:sl + 1], scalar2=0.0,
                        op0=ALU.mult, op1=ALU.add), waits=[r1], inc=sm_dve)
                    nmks.append(n1)
                bmks = []
                for sl in range(8):
                    hh, qh = sl // 2, sl % 2
                    po = 1024 + (hh // 2) * 512 + qh * 128
                    cp = (hh % 2) * 64
                    bmks.append(P.op("tensor", lambda tt, sl=sl, po=po, cp=cp: tt.matmul(
                        PS[cp:cp + 64, po:po + 128],
                        atok[:, sl * 64:(sl + 1) * 64], ident[:],
                        start=True, stop=True, tile_position=(0, cp)),
                        waits=[nmks[-1]] if sl == 0 else (), inc=sm_pe))
                evf = P.op("scalar", lambda s, g=g: s.activation(
                    out=attn_fm[:, 2 * g * TS: (2 * g + 2) * TS].rearrange(
                        "p (b x) -> p b x", b=2),
                    in_=PS[:, 1024:2048].rearrange("p (b x) -> p b x", b=2)[:, :, 0:256],
                    func=AF.Copy),
                    waits=[bmks[-1]], inc=sm_act)
                attn_ev_marks.append(evf)

            if stages.startswith("attn"):
                om = P.dma("sync", out[t * TS: t * TS + 128, :], xt[:, 0:D],
                           waits=list(attn_ev_marks), inc=sm_out)
                out_marks.append(om)
                continue
            omm = {}
            for s_i in range(2):
                for n in range(2):
                    for c in range(8):
                        omm[(s_i, n)] = P.op("tensor", lambda tt, s_i=s_i, n=n, c=c: tt.matmul(
                            PS[:, (2 * s_i + n) * 512: (2 * s_i + n + 1) * 512],
                            attn_fm[:, c * TS + s_i * 128: c * TS + s_i * 128 + 128],
                            w_oT[:, c * D + n * 512: c * D + (n + 1) * 512],
                            start=c == 0, stop=c == 7),
                            waits=attn_ev_marks + nmks if (c == 0 and s_i == 0 and n == 0) else (),
                            inc=sm_pe)
            y_evs = []
            for s_i in range(2):
                e1 = P.op("vector", lambda v, s_i=s_i: v.tensor_scalar(
                    out=o_sb[:, s_i * D: s_i * D + 512], in0=BK(2 * s_i, 0, 512),
                    scalar1=1.0, scalar2=0.0, op0=ALU.mult, op1=ALU.add,
                    accum_out=st[:, 0:1]), waits=[omm[(s_i, 0)]], inc=sm_dve)
                e2 = P.op("vector", lambda v, s_i=s_i: v.tensor_scalar(
                    out=o_sb[:, s_i * D + 512: (s_i + 1) * D], in0=BK(2 * s_i + 1, 0, 512),
                    scalar1=1.0, scalar2=0.0, op0=ALU.mult, op1=ALU.add,
                    accum_out=st[:, 4:5]), waits=[omm[(s_i, 1)]], inc=sm_dve)
                e3 = P.op("vector", lambda v: v.tensor_add(
                    out=st[:, 0:1], in0=st[:, 0:1], in1=st[:, 4:5]), waits=[e1, e2],
                    inc=sm_dve)
                sq = P.op("vector", lambda v, s_i=s_i: v.scalar_tensor_tensor(
                    out=jnk[:, 0:D], in0=o_sb[:, s_i * D:(s_i + 1) * D], scalar=1.0,
                    in1=o_sb[:, s_i * D:(s_i + 1) * D],
                    op0=ALU.bypass, op1=ALU.mult,
                    accum_out=st[:, 1:2]), waits=[e1, e2], inc=sm_dve)
                d3 = P.op("vector", lambda v: v.tensor_scalar(
                    out=st[:, 2:3], in0=st[:, 0:1], scalar1=1.0 / D, scalar2=0.0,
                    op0=ALU.mult, op1=ALU.add), waits=[e3], inc=sm_dve)
                d4 = P.op("vector", lambda v: v.scalar_tensor_tensor(
                    out=st[:, 3:4], in0=st[:, 2:3], scalar=1.0, in1=st[:, 2:3],
                    op0=ALU.mult, op1=ALU.mult), waits=[d3], inc=sm_dve)
                d5 = P.op("vector", lambda v: v.tensor_scalar(
                    out=st[:, 1:2], in0=st[:, 1:2], scalar1=1.0 / D, scalar2=st[:, 3:4],
                    op0=ALU.mult, op1=ALU.subtract), waits=[sq, d4], inc=sm_dve)
                d6 = P.op("scalar", lambda s: s.activation(
                    out=st[:, 1:2], in_=st[:, 1:2], func=AF.Ln, bias=C_EPS), waits=[d5],
                    inc=sm_act)
                d7 = P.op("scalar", lambda s: s.activation(
                    out=st[:, 0:1], in_=st[:, 1:2], func=AF.Exp, scale=-0.5), waits=[d6],
                    inc=sm_act)
                d8 = P.op("vector", lambda v: v.scalar_tensor_tensor(
                    out=st[:, 2:3], in0=st[:, 2:3], scalar=-1.0, in1=st[:, 0:1],
                    op0=ALU.mult, op1=ALU.mult), waits=[d7], inc=sm_dve)
                xm = P.op("vector", lambda v, s_i=s_i: v.tensor_scalar(
                    out=e_scr[:, s_i * D:(s_i + 1) * D], in0=xt[:, s_i * D:(s_i + 1) * D],
                    scalar1=1.0, scalar2=st[:, 2:3], op0=ALU.mult, op1=ALU.add),
                    waits=[d8], inc=sm_dve)
                x1m = P.op("vector", lambda v, s_i=s_i: v.scalar_tensor_tensor(
                    out=x1[:, s_i * D:(s_i + 1) * D], in0=o_sb[:, s_i * D:(s_i + 1) * D],
                    scalar=st[:, 0:1], in1=e_scr[:, s_i * D:(s_i + 1) * D],
                    op0=ALU.mult, op1=ALU.add), waits=[xm], inc=sm_dve)
                sty = ln_stats(x1[:, s_i * D:(s_i + 1) * D], D, 8 + 4 * s_i, [x1m])
                y_evs.append(ln_apply(ytm[:, s_i * D:(s_i + 1) * D],
                                      x1[:, s_i * D:(s_i + 1) * D], 8 + 4 * s_i, [sty]))
            if stages == "oproj":
                om = P.dma("sync", out[t * TS: t * TS + 128, :], x1[:, 0:D],
                           waits=y_evs, inc=sm_out)
                out_marks.append(om)
                continue
            yfm_evs = []
            for s_i in range(2):
                blocks = [(ytm[:, s_i * D + c * 128: s_i * D + (c + 1) * 128], c * 128)
                          for c in range(8)]
                _, evy = idmm_batch(
                    blocks,
                    yfm.rearrange("p (c f) -> p c f", c=8)[:, :, s_i * 128:(s_i + 1) * 128],
                    PS[:, 0:1024].rearrange("p (c f) -> p c f", c=8),
                    [y_evs[s_i]] + ([omm[(1, 1)]] if s_i == 0 else yfm_evs[-1:]))
                yfm_evs.append(evy)
            sx0 = P.dma("sync", x1_dram[t, 0:128, :], x1[:, 0:D],
                        waits=y_evs + yfm_evs, inc=sm_sp)
            sx1 = P.dma("sync", x1_dram[t, 128:256, :], x1[:, D:2 * D],
                        inc=sm_sp, chain=False)
            sy = P.dma("sync", yfm_dram[t], yfm[:], inc=sm_sp, chain=False)
            yrc = P.op("gpsimd", lambda g, t=t: g.tensor_copy(
                out=yrows_all[:, t * 8:(t + 1) * 8],
                in_=yfm.rearrange("p (c f) -> p c f", c=8)[:, :, 255:256]),
                waits=yfm_evs, inc=sm_pl)
            yrc_marks.append(yrc)
            spillj_marks.append(sy)

        srow_sp = P.dma("sync", yrow_src[:], yrows_all[:],
                        waits=[yrc_marks[-1]], inc=sm_sp)
        cl = P.op("gpsimd", lambda g: g.collective_compute(
            "AllGather", ALU.bypass,
            replica_groups=[[0, 1], [2, 3], [4, 5], [6, 7]],
            ins=[yrow_src.ap().opt()], outs=[yrow_gath.ap().opt()]),
            waits=[(sm_sp, sm_sp.n)], inc=sm_cl)
        ldA = P.dma("sync", carry_A[:], yrow_gath[0], waits=[cl], inc=sm_in)
        ldB = P.dma("sync", carry_B[:], yrow_gath[1], inc=sm_in, chain=False)
        gcast = P.op("vector", lambda v: v.tensor_copy(
            out=scb[0:1, 1:3], in_=cfg_sb[0:1, 2:4]), waits=[consts_ld], inc=sm_dve)
        gmm = P.op("tensor", lambda tt: tt.matmul(
            PS[:, 256:258], onesb[0:1, :], scb[0:1, 1:3], start=True, stop=True),
            waits=[gcast, cl], inc=sm_pe)
        ga = P.op("vector", lambda v: v.tensor_copy(
            out=gA_col[:], in_=PS[:, 256:257]), waits=[gmm], inc=sm_dve)
        gb = P.op("vector", lambda v: v.tensor_copy(
            out=gB_col[:], in_=PS[:, 257:258]), waits=[gmm], inc=sm_dve)
        c1 = P.op("vector", lambda v: v.tensor_scalar(
            out=carry_all[:], in0=carry_A[:],
            scalar1=gA_col[:, 0:1], scalar2=0.0, op0=ALU.mult, op1=ALU.add),
            waits=[ga, ldB], inc=sm_dve)
        cprep = P.op("vector", lambda v: v.scalar_tensor_tensor(
            out=carry_all[:, 8:64], in0=carry_B[:, 0:56], scalar=gB_col[:, 0:1],
            in1=carry_all[:, 8:64], op0=ALU.mult, op1=ALU.add),
            waits=[gb, c1], inc=sm_dve)

        for t in range(nt):
            x1l0 = P.dma("sync", x1[:, 0:D], x1_dram[t, 0:128, :],
                         waits=[(sm_sp, sm_sp.n), (out_marks[-1] if out_marks else cl)],
                         inc=sm_in)
            x1l1 = P.dma("sync", x1[:, D:2 * D], x1_dram[t, 128:256, :],
                         inc=sm_in, chain=False)
            yfl = P.dma("sync", yfm[:], yfm_dram[t], inc=sm_in, chain=False)
            dm = P.op("gpsimd", lambda g: g.tensor_tensor(
                out=dscr.rearrange("p (c f) -> p c f", c=8)[:, :, 1:256],
                in0=yfm.rearrange("p (c f) -> p c f", c=8)[:, :, 1:256],
                in1=yfm.rearrange("p (c f) -> p c f", c=8)[:, :, 0:255],
                op=ALU.subtract), waits=[yfl], inc=sm_pl)
            mix_marks = []
            for c in range(8):
                d0 = P.op("gpsimd", lambda g, c=c, t=t: g.tensor_tensor(
                    out=dscr[:, c * TS:c * TS + 1], in0=yfm[:, c * TS:c * TS + 1],
                    in1=carry_all[:, t * 8 + c: t * 8 + c + 1], op=ALU.subtract),
                    waits=[dm, cprep], inc=sm_pl)
                m1 = P.op("vector", lambda g, c=c, t=t: g.scalar_tensor_tensor(
                    out=xr[:, c * TS:c * TS + 1], in0=dscr[:, c * TS:c * TS + 1],
                    scalar=mu_rc[:, c:c + 1], in1=carry_all[:, t * 8 + c: t * 8 + c + 1],
                    op0=ALU.mult, op1=ALU.add), waits=[d0], inc=sm_dve)
                m2 = P.op("vector", lambda g, c=c, t=t: g.scalar_tensor_tensor(
                    out=xk[:, c * TS:c * TS + 1], in0=dscr[:, c * TS:c * TS + 1],
                    scalar=mu_kc[:, c:c + 1], in1=carry_all[:, t * 8 + c: t * 8 + c + 1],
                    op0=ALU.mult, op1=ALU.add), waits=[d0], inc=sm_dve)
                m3 = P.op("vector", lambda g, c=c: g.scalar_tensor_tensor(
                    out=xr[:, c * TS + 1:(c + 1) * TS],
                    in0=dscr[:, c * TS + 1:(c + 1) * TS],
                    scalar=mu_rc[:, c:c + 1],
                    in1=yfm[:, c * TS:(c + 1) * TS - 1],
                    op0=ALU.mult, op1=ALU.add), waits=[dm], inc=sm_dve)
                m4 = P.op("vector", lambda g, c=c: g.scalar_tensor_tensor(
                    out=xk[:, c * TS + 1:(c + 1) * TS],
                    in0=dscr[:, c * TS + 1:(c + 1) * TS],
                    scalar=mu_kc[:, c:c + 1],
                    in1=yfm[:, c * TS:(c + 1) * TS - 1],
                    op0=ALU.mult, op1=ALU.add), waits=[dm], inc=sm_dve)
                mix_marks += [m1, m2, m3, m4]
            mix_done = mix_marks[-1]
            rmm = {}
            kmm = {}
            for s_i in range(2):
                for n in range(2):
                    for c in range(8):
                        rmm[(s_i, n)] = P.op("tensor", lambda tt, s_i=s_i, n=n, c=c: tt.matmul(
                            PS[:, (2 * s_i + n) * 512: (2 * s_i + n + 1) * 512],
                            xr[:, c * TS + s_i * 128: c * TS + s_i * 128 + 128],
                            w_rT[:, c * D + n * 512: c * D + (n + 1) * 512],
                            start=c == 0, stop=c == 7),
                            waits=[mix_done]
                            if (s_i == 0 and n == 0 and c == 0) else (), inc=sm_pe)
            for s_i in range(2):
                for n in range(2):
                    for c in range(8):
                        kmm[(s_i, n)] = P.op("tensor", lambda tt, s_i=s_i, n=n, c=c: tt.matmul(
                            PS[:, 2048 + (2 * s_i + n) * 512: 2048 + (2 * s_i + n + 1) * 512],
                            xk[:, c * TS + s_i * 128: c * TS + s_i * 128 + 128],
                            w_kT[:, c * D + n * 512: c * D + (n + 1) * 512],
                            start=c == 0, stop=c == 7),
                            waits=[mix_done] if (s_i == 0 and n == 0 and c == 0) else (),
                            inc=sm_pe)
            r_marks = []
            for s_i in range(2):
                s1 = P.op("scalar", lambda s, s_i=s_i: s.activation(
                    out=e_scr[:, s_i * D:(s_i + 1) * D],
                    in_=PS[:, 2 * s_i * 512: (2 * s_i + 2) * 512],
                    func=AF.Exp, scale=-1.0), waits=[rmm[(s_i, 1)]], inc=sm_act)
                s2 = P.op("scalar", lambda s, s_i=s_i: s.activation(
                    out=e_scr[:, s_i * D:(s_i + 1) * D],
                    in_=e_scr[:, s_i * D:(s_i + 1) * D],
                    func=AF.Ln, bias=C_ONE), waits=[s1], inc=sm_act)
                s3 = P.op("scalar", lambda s, s_i=s_i: s.activation(
                    out=r_sb[:, s_i * D:(s_i + 1) * D],
                    in_=e_scr[:, s_i * D:(s_i + 1) * D],
                    func=AF.Exp, scale=-1.0), waits=[s2], inc=sm_act)
                r_marks.append(s3)
            ka_marks = []
            for s_i in range(2):
                for n in range(2):
                    rl = P.op("vector", lambda v, s_i=s_i, n=n: v.tensor_scalar_max(
                        out=kact[:, s_i * D + n * 512: s_i * D + (n + 1) * 512],
                        in0=PS[:, 2048 + (2 * s_i + n) * 512: 2048 + (2 * s_i + n + 1) * 512],
                        scalar1=0.0), waits=[kmm[(s_i, n)]], inc=sm_dve)
                    ka_marks.append(rl)
            sqk = P.op("gpsimd", lambda g: g.tensor_tensor(
                out=kact[:], in0=kact[:], in1=kact[:], op=ALU.mult),
                waits=ka_marks, inc=sm_pl)
            kfm_evs = []
            for s_i in range(2):
                blocks = [(kact[:, s_i * D + c * 128: s_i * D + (c + 1) * 128], c * 128)
                          for c in range(8)]
                _, evk2 = idmm_batch(
                    blocks,
                    kactfm.rearrange("p (c f) -> p c f", c=8)[:, :, s_i * 128:(s_i + 1) * 128],
                    PS[:, 0:1024].rearrange("p (c f) -> p c f", c=8),
                    [sqk] + ([r_marks[1]] if s_i == 0 else kfm_evs[-1:]))
                kfm_evs.append(evk2)
            vmm2 = {}
            for s_i in range(2):
                for n in range(2):
                    for c in range(8):
                        vmm2[(s_i, n)] = P.op("tensor", lambda tt, s_i=s_i, n=n, c=c: tt.matmul(
                            PS[:, 2048 + (2 * s_i + n) * 512: 2048 + (2 * s_i + n + 1) * 512],
                            kactfm[:, c * TS + s_i * 128: c * TS + s_i * 128 + 128],
                            w_vT[:, c * D + n * 512: c * D + (n + 1) * 512],
                            start=c == 0, stop=c == 7),
                            waits=kfm_evs + [ka_marks[-1]]
                            if (s_i == 0 and n == 0 and c == 0) else (), inc=sm_pe)
            for s_i in range(2):
                f1 = P.op("vector", lambda v, s_i=s_i: v.scalar_tensor_tensor(
                    out=o_sb[:, s_i * D: s_i * D + 512],
                    in0=PS[:, 2048 + 2 * s_i * 512: 2048 + (2 * s_i + 1) * 512],
                    scalar=1.0, in1=r_sb[:, s_i * D: s_i * D + 512],
                    op0=ALU.mult, op1=ALU.mult, accum_out=st[:, 0:1]),
                    waits=[vmm2[(s_i, 0)], r_marks[s_i]], inc=sm_dve)
                f2 = P.op("vector", lambda v, s_i=s_i: v.scalar_tensor_tensor(
                    out=o_sb[:, s_i * D + 512: (s_i + 1) * D],
                    in0=PS[:, 2048 + (2 * s_i + 1) * 512: 2048 + (2 * s_i + 2) * 512],
                    scalar=1.0, in1=r_sb[:, s_i * D + 512: (s_i + 1) * D],
                    op0=ALU.mult, op1=ALU.mult, accum_out=st[:, 4:5]),
                    waits=[vmm2[(s_i, 1)]], inc=sm_dve)
                e3 = P.op("vector", lambda v: v.tensor_add(
                    out=st[:, 0:1], in0=st[:, 0:1], in1=st[:, 4:5]), waits=[f1, f2],
                    inc=sm_dve)
                sq = P.op("vector", lambda v, s_i=s_i: v.scalar_tensor_tensor(
                    out=jnk[:, 0:D], in0=o_sb[:, s_i * D:(s_i + 1) * D], scalar=1.0,
                    in1=o_sb[:, s_i * D:(s_i + 1) * D],
                    op0=ALU.bypass, op1=ALU.mult,
                    accum_out=st[:, 1:2]), waits=[f1, f2], inc=sm_dve)
                d3 = P.op("vector", lambda v: v.tensor_scalar(
                    out=st[:, 2:3], in0=st[:, 0:1], scalar1=1.0 / D, scalar2=0.0,
                    op0=ALU.mult, op1=ALU.add), waits=[e3], inc=sm_dve)
                d4 = P.op("vector", lambda v: v.scalar_tensor_tensor(
                    out=st[:, 3:4], in0=st[:, 2:3], scalar=1.0, in1=st[:, 2:3],
                    op0=ALU.mult, op1=ALU.mult), waits=[d3], inc=sm_dve)
                d5 = P.op("vector", lambda v: v.tensor_scalar(
                    out=st[:, 1:2], in0=st[:, 1:2], scalar1=1.0 / D, scalar2=st[:, 3:4],
                    op0=ALU.mult, op1=ALU.subtract), waits=[sq, d4], inc=sm_dve)
                d6 = P.op("scalar", lambda s: s.activation(
                    out=st[:, 1:2], in_=st[:, 1:2], func=AF.Ln, bias=C_EPS), waits=[d5],
                    inc=sm_act)
                d7 = P.op("scalar", lambda s: s.activation(
                    out=st[:, 0:1], in_=st[:, 1:2], func=AF.Exp, scale=-0.5), waits=[d6],
                    inc=sm_act)
                d8 = P.op("vector", lambda v: v.scalar_tensor_tensor(
                    out=st[:, 2:3], in0=st[:, 2:3], scalar=-1.0, in1=st[:, 0:1],
                    op0=ALU.mult, op1=ALU.mult), waits=[d7], inc=sm_dve)
                xm = P.op("vector", lambda v, s_i=s_i: v.tensor_scalar(
                    out=e_scr[:, s_i * D:(s_i + 1) * D], in0=x1[:, s_i * D:(s_i + 1) * D],
                    scalar1=1.0, scalar2=st[:, 2:3], op0=ALU.mult, op1=ALU.add),
                    waits=[d8, yfl], inc=sm_dve)
                x2m = P.op("vector", lambda v, s_i=s_i: v.scalar_tensor_tensor(
                    out=x2[:, s_i * D:(s_i + 1) * D], in0=o_sb[:, s_i * D:(s_i + 1) * D],
                    scalar=st[:, 0:1], in1=e_scr[:, s_i * D:(s_i + 1) * D],
                    op0=ALU.mult, op1=ALU.add), waits=[xm], inc=sm_dve)
                om = P.dma("sync", out[t * TS + s_i * 128: t * TS + (s_i + 1) * 128, :],
                           x2[:, s_i * D:(s_i + 1) * D], waits=[x2m], inc=sm_out)
                out_marks.append(om)

        P.op("gpsimd", lambda g: g.nop(), waits=[out_marks[-1]], inc=sm_pl)

        with nc.Block() as block:
            @block.sync
            def _(sync):
                P.emit("sync", sync)

            @block.tensor
            def _(tensor):
                P.emit("tensor", tensor)

            @block.vector
            def _(vector):
                P.emit("vector", vector)

            @block.scalar
            def _(scalar):
                P.emit("scalar", scalar)

            @block.gpsimd
            def _(gpsimd):
                P.emit("gpsimd", gpsimd)

    return nc


_NC_CACHE = {}


def kernel(L0_attn_x, L0_attn_W_dq, L0_attn_W_uq, L0_attn_q_ln_g, L0_attn_q_ln_b,
           L0_attn_W_dkv, L0_attn_W_ukv, L0_attn_kv_ln_g, L0_attn_kv_ln_b,
           L0_attn_W_o, L0_ffn_W_r, L0_ffn_W_k, L0_ffn_W_v, L0_ffn_mu_r, L0_ffn_mu_k):
    if "nc" not in _NC_CACHE:
        _NC_CACHE["nc"] = build()
    nc = _NC_CACHE["nc"]
    x = np.ascontiguousarray(np.asarray(L0_attn_x), dtype=np.float32)
    shared = {
        "W_dq": np.ascontiguousarray(L0_attn_W_dq, np.float32),
        "W_uq": np.ascontiguousarray(L0_attn_W_uq, np.float32),
        "q_g": np.ascontiguousarray(L0_attn_q_ln_g, np.float32),
        "q_b": np.ascontiguousarray(L0_attn_q_ln_b, np.float32),
        "W_dkv": np.ascontiguousarray(L0_attn_W_dkv, np.float32),
        "W_ukv": np.ascontiguousarray(L0_attn_W_ukv, np.float32),
        "kv_g": np.ascontiguousarray(L0_attn_kv_ln_g, np.float32),
        "kv_b": np.ascontiguousarray(L0_attn_kv_ln_b, np.float32),
        "W_o": np.ascontiguousarray(L0_attn_W_o, np.float32),
        "W_r": np.ascontiguousarray(L0_ffn_W_r, np.float32),
        "W_k": np.ascontiguousarray(L0_ffn_W_k, np.float32),
        "W_v": np.ascontiguousarray(L0_ffn_W_v, np.float32),
        "mu_r": np.ascontiguousarray(L0_ffn_mu_r, np.float32),
        "mu_k": np.ascontiguousarray(L0_ffn_mu_k, np.float32),
    }
    in_maps = []
    for c in range(8):
        b, r = c // 2, c % 2
        m = dict(shared)
        m["x"] = np.ascontiguousarray(x[b])
        m["xq"] = np.ascontiguousarray(
            x[b].reshape(NT, TS, D)[r::2].reshape(8 * TS, D))
        m["maskin"] = host_masks(r)
        m["cfgi"] = np.zeros((1, 2), np.int32)
        m["cfgf"] = (np.array([[0.0, 1.0]], np.float32) if r == 0
                     else np.array([[1.0, 0.0]], np.float32))
        in_maps.append(m)
    res = run_bass_kernel_spmd(nc, in_maps, core_ids=list(range(8)))
    outp = np.empty((4, NT, TS, D), dtype=np.float32)
    for b in range(4):
        for r in range(2):
            outp[b, r::2] = res.results[2 * b + r]["out"].reshape(8, TS, D)
    return outp.reshape(4, S, D)


# version 15
# speedup vs baseline: 1.2415x; 1.2415x over previous
import contextlib

import numpy as np

import concourse.bass as bass
import concourse.mybir as mybir
from concourse.bass_utils import run_bass_kernel_spmd

F32 = mybir.dt.float32
BF16 = mybir.dt.bfloat16
AF = mybir.ActivationFunctionType
ALU = mybir.AluOpType
AX = mybir.AxisListType

D = 1024
S = 4096
NT = 16
TS = 256
NKT = 32
NH = 16
DH = 64
LAT = 256
EPS = 1e-5
INV_SQRT_DH = 0.125


class Sem:
    def __init__(self, handle):
        self.h = handle
        self.n = 0


class Planner:

    ENGS = ("sync", "tensor", "vector", "scalar", "gpsimd")

    def __init__(self):
        self.recs = {e: [] for e in self.ENGS}

    SERIAL = ("vector", "scalar", "gpsimd")

    def __init_serial(self):
        if not hasattr(self, "_own"):
            self._own = {}

    def set_own_sems(self, mapping):
        self._own = dict(mapping)

    def op(self, eng, fn, waits=(), inc=None, amt=1):
        w = [x for x in waits if x is not None]
        own = getattr(self, "_own", {}).get(eng)
        if eng in self.SERIAL and own is not None:
            w = [(own, own.n)] + w
            if inc is None:
                inc = own
        if inc is not None:
            inc.n += amt
        self.recs[eng].append((fn, w, inc, amt))
        return (inc, inc.n) if inc is not None else None

    def dma(self, eng, out, in_, waits=(), inc=None, chain=True):
        w = list(waits)
        if inc is not None and chain:
            w.append((inc, inc.n))
        return self.op(eng, lambda e: e.dma_start(out=out, in_=in_),
                       waits=w, inc=inc, amt=16)

    def emit(self, eng_name, eng):
        last_wait = {}
        dedup = eng_name != "sync"
        for fn, waits, inc, amt in self.recs[eng_name]:
            for sem, val in waits:
                if dedup and last_wait.get(id(sem), -1) >= val:
                    continue
                eng.wait_ge(sem.h, val)
                last_wait[id(sem)] = val
            inst = fn(eng)
            if inc is not None:
                inst.then_inc(inc.h, amt)


def _host_consts():
    import ml_dtypes
    bf = ml_dtypes.bfloat16
    ident = np.eye(128, dtype=np.float32).astype(bf)
    ones = np.ones((128, 128), dtype=np.float32).astype(bf)
    return ident, ones


def host_masks(rank):
    import ml_dtypes
    bf = ml_dtypes.bfloat16
    k = np.arange(128)[:, None]
    q = np.arange(TS)[None, :]
    mlow = (q >= k).astype(np.float32)
    mhigh = (q >= 128 + k).astype(np.float32)
    ones = np.ones((128, TS), np.float32)
    zero = np.zeros((128, TS), np.float32)
    stack = [ones, ones, mlow, mhigh] if rank == 1 else [mlow, mhigh, zero, zero]
    return np.stack(stack).astype(bf)


def build(nt=8, nkt=NKT, stages="all"):
    nc = bass.Bass("TRN2", target_bir_lowering=False, debug=False)

    x_in = nc.dram_tensor("x", [S, D], F32, kind="ExternalInput").ap()
    xq_in = nc.dram_tensor("xq", [8 * TS, D], F32, kind="ExternalInput").ap()
    maskin = nc.dram_tensor("maskin", [4, 128, TS], BF16, kind="ExternalInput").ap()
    cfgi_in = nc.dram_tensor("cfgi", [1, 2], mybir.dt.int32, kind="ExternalInput").ap()
    cfgf_in = nc.dram_tensor("cfgf", [1, 2], F32, kind="ExternalInput").ap()
    w_dq_in = nc.dram_tensor("W_dq", [D, LAT], F32, kind="ExternalInput").ap()
    w_uq_in = nc.dram_tensor("W_uq", [LAT, D], F32, kind="ExternalInput").ap()
    qg_in = nc.dram_tensor("q_g", [LAT], F32, kind="ExternalInput").ap()
    qb_in = nc.dram_tensor("q_b", [LAT], F32, kind="ExternalInput").ap()
    w_dkv_in = nc.dram_tensor("W_dkv", [D, LAT], F32, kind="ExternalInput").ap()
    w_ukv_in = nc.dram_tensor("W_ukv", [LAT, 2 * D], F32, kind="ExternalInput").ap()
    kg_in = nc.dram_tensor("kv_g", [LAT], F32, kind="ExternalInput").ap()
    kb_in = nc.dram_tensor("kv_b", [LAT], F32, kind="ExternalInput").ap()
    w_o_in = nc.dram_tensor("W_o", [D, D], F32, kind="ExternalInput").ap()
    w_r_in = nc.dram_tensor("W_r", [D, D], F32, kind="ExternalInput").ap()
    w_k_in = nc.dram_tensor("W_k", [D, D], F32, kind="ExternalInput").ap()
    w_v_in = nc.dram_tensor("W_v", [D, D], F32, kind="ExternalInput").ap()
    mu_r_in = nc.dram_tensor("mu_r", [D], F32, kind="ExternalInput").ap()
    mu_k_in = nc.dram_tensor("mu_k", [D], F32, kind="ExternalInput").ap()
    out = nc.dram_tensor("out", [8 * TS, D], F32, kind="ExternalOutput").ap()

    ident_np, ones_np = _host_consts()
    ident_d = nc.inline_tensor(ident_np, "ident_c")
    ones_d = nc.inline_tensor(ones_np, "ones_c")

    k_dram = nc.dram_tensor("k_scr", [NKT, 128, 1024], BF16)
    v_dram = nc.dram_tensor("v_scr", [NKT, 128, 1040], BF16)
    x1_dram = nc.dram_tensor("x1_scr", [8, TS, D], F32)
    yfm_dram = nc.dram_tensor("yfm_scr", [8, 128, 8 * TS], BF16)
    yrow_src = nc.dram_tensor("yrow_src", [128, 64], BF16)
    yrow_gath = nc.dram_tensor("yrow_gath", [2, 128, 64], BF16)

    taps = {}

    with contextlib.ExitStack() as cx:
        sb = lambda name, shape, dt: cx.enter_context(nc.sbuf_tensor(name, shape, dt))
        w_dq = sb("w_dq", [128, 8 * LAT], BF16)
        w_uq = sb("w_uq", [128, 2 * D], BF16)
        w_dkv = sb("w_dkv", [128, 8 * LAT], BF16)
        w_ukv = sb("w_ukv", [128, 2 * 2 * D], BF16)
        w_oT = sb("w_oT", [128, 8 * D], BF16)
        w_rT = sb("w_rT", [128, 8 * D], BF16)
        w_kT = sb("w_kT", [128, 8 * D], BF16)
        w_vT = sb("w_vT", [128, 8 * D], BF16)
        mu_rc = sb("mu_rc", [128, 8], F32)
        mu_kc = sb("mu_kc", [128, 8], F32)
        gbq = sb("gbq", [128, LAT], F32)
        bbq = sb("bbq", [128, LAT], F32)
        gbk = sb("gbk", [128, LAT], F32)
        bbk = sb("bbk", [128, LAT], F32)
        ident = sb("ident", [128, 128], BF16)
        onesb = sb("onesb", [128, 128], BF16)
        mask_sb = sb("mask_sb", [128, 4 * TS], BF16)
        carry_all = sb("carry_all", [128, 64], BF16)
        yrows_all = sb("yrows_all", [128, 64], BF16)
        carry_A = sb("carry_A", [128, 64], BF16)
        carry_B = sb("carry_B", [128, 64], BF16)
        gA_col = sb("gA_col", [128, 1], F32)
        gB_col = sb("gB_col", [128, 1], F32)
        cfg_sb = sb("cfg_sb", [1, 4], F32)
        xt = sb("xt", [128, 2 * D], F32)
        wload = xt
        xln = sb("xln", [128, 2 * D], BF16)
        xfm = sb("xfm", [128, 8 * TS], BF16)
        cq_sb = sb("cq_sb", [128, 2 * LAT], F32)
        scr_s = sb("scr_s", [128, 2 * LAT], F32)
        cqn = sb("cqn", [128, 2 * LAT], BF16)
        cqfm = sb("cqfm", [128, 2 * LAT], BF16)
        qfm = sb("qfm", [128, 8 * TS], BF16)
        kstage = sb("kstage", [128, 2 * D], BF16)
        vstage = sb("vstage", [128, 2 * NH * 65], BF16)
        k2s = sb("k2s", [128, D], BF16)
        kbuf = sb("kbuf", [128, 6 * 256], BF16)
        vbuf = sb("vbuf", [128, 6 * 260], BF16)
        pbuf = sb("pbuf", [128, 4 * 1024], BF16)
        aev = sb("aev", [128, 4 * 256], BF16)
        atok = sb("atok", [128, 8 * 64], BF16)
        rc = sb("rc", [128, 8], F32)
        attn_fm = sb("attn_fm", [128, 8 * TS], BF16)
        x1 = sb("x1", [128, 2 * D], F32)
        ytm = xln
        yfm = sb("yfm", [128, 8 * TS], BF16)
        dscr = sb("dscr", [128, 8 * TS], BF16)
        q2 = dscr
        xr = sb("xr", [128, 8 * TS], BF16)
        xk = sb("xk", [128, 8 * TS], BF16)
        r_sb = sb("r_sb", [128, 2 * D], F32)
        e_scr = sb("e_scr", [128, 2 * D], F32)
        kact = dscr
        kactfm = sb("kactfm", [128, 8 * TS], BF16)
        o_sb = sb("o_sb", [128, 2 * D], F32)
        x2 = xt
        vrow = xt[0:1, 0:4 * LAT]
        vrow_b = pbuf[0:1, 0:8 * LAT]
        jnk = sb("jnk", [128, D], BF16)
        st = sb("st", [128, 16], F32)
        sc = sb("sc", [1, 16], F32)
        scb = sb("scb", [1, 16], BF16)
        bias_b = sb("bias_b", [128, 1], F32)
        cvals = sb("cvals", [128, 4], F32)

        PS = cx.enter_context(nc.psum_tensor("PS", [128, 4096], F32))

        sm_in = Sem(cx.enter_context(nc.semaphore("sm_in")))
        sm_sp = Sem(cx.enter_context(nc.semaphore("sm_sp")))
        sm_out = Sem(cx.enter_context(nc.semaphore("sm_out")))
        sm_pe = Sem(cx.enter_context(nc.semaphore("sm_pe")))
        sm_act = Sem(cx.enter_context(nc.semaphore("sm_act")))
        sm_dve = Sem(cx.enter_context(nc.semaphore("sm_dve")))
        sm_pl = Sem(cx.enter_context(nc.semaphore("sm_pl")))
        sm_kvs = [Sem(cx.enter_context(nc.semaphore(f"sm_kv{i}"))) for i in range(6)]
        sm_spl = [Sem(cx.enter_context(nc.semaphore(f"sm_spl{i}"))) for i in range(4)]
        sm_cl = Sem(cx.enter_context(nc.semaphore("sm_cl")))

        P = Planner()
        P.set_own_sems({"vector": sm_dve, "scalar": sm_act, "gpsimd": sm_pl})
        BK = lambda b, off=0, n=512: PS[:, b * 512 + off: b * 512 + off + n]

        P.op("vector", lambda v: v.memset(cvals[:, 0:1], EPS), inc=sm_dve)
        P.op("vector", lambda v: v.memset(cvals[:, 1:2], 1.0), inc=sm_dve)
        P.op("vector", lambda v: v.memset(cvals[:, 2:3], 1e-12), inc=sm_dve)
        cv_mark = (sm_dve, sm_dve.n)
        C_EPS = cvals[:, 0:1]
        C_ONE = cvals[:, 1:2]
        C_TINY = cvals[:, 2:3]

        P.dma("sync", ident[:], ident_d.ap(), inc=sm_in)
        P.dma("sync", onesb[:], ones_d.ap(), inc=sm_in)
        for _s in range(4):
            P.dma("sync", mask_sb[:, _s * TS:(_s + 1) * TS], maskin[_s], inc=sm_in)
        P.dma("sync", cfg_sb[0:1, 0:2].bitcast(mybir.dt.int32), cfgi_in, inc=sm_in)
        P.dma("sync", cfg_sb[0:1, 2:4], cfgf_in, inc=sm_in)
        for c in range(8):
            P.dma("sync", mu_rc[:, c:c + 1],
                  mu_r_in[c * 128:(c + 1) * 128].rearrange("(p o) -> p o", o=1), inc=sm_in)
            P.dma("sync", mu_kc[:, c:c + 1],
                  mu_k_in[c * 128:(c + 1) * 128].rearrange("(p o) -> p o", o=1), inc=sm_in)
        for i, src_ap in enumerate((qg_in, qb_in, kg_in, kb_in)):
            P.dma("sync", vrow[0:1, i * LAT:(i + 1) * LAT],
                  src_ap.rearrange("(o l) -> o l", o=1), inc=sm_in)
        consts_ld = (sm_in, sm_in.n)

        P.op("vector", lambda v: v.tensor_copy(out=vrow_b[0:1, 0:4 * LAT], in_=vrow[:]),
             waits=[consts_ld], inc=sm_dve)
        P.op("vector", lambda v: v.tensor_tensor(out=vrow_b[0:1, 4 * LAT:8 * LAT],
                                                 in0=vrow[:], in1=vrow_b[0:1, 0:4 * LAT],
                                                 op=ALU.subtract), inc=sm_dve)
        gb_prep = (sm_dve, sm_dve.n)
        for i, dst in enumerate((gbq, bbq, gbk, bbk)):
            ii = i
            P.op("tensor", lambda t, ii=ii: t.matmul(
                BK(0, 0, LAT), onesb[0:1, :], vrow_b[0:1, ii * LAT:(ii + 1) * LAT],
                start=True, stop=False), waits=[gb_prep, (sm_dve, sm_dve.n)])
            mk = P.op("tensor", lambda t, ii=ii: t.matmul(
                BK(0, 0, LAT), onesb[0:1, :], vrow_b[0:1, (4 + ii) * LAT:(5 + ii) * LAT],
                start=False, stop=True), inc=sm_pe)
            P.op("vector", lambda v, dst=dst: v.tensor_copy(out=dst[:], in_=BK(0, 0, LAT)),
                 waits=[mk], inc=sm_dve)
        gb_done = (sm_dve, sm_dve.n)

        def load_cast(dst, src_ap, cols, waits=()):
            ld = P.dma("sync", wload[:, 0:cols], src_ap, waits=list(waits) + [gb_done],
                       inc=sm_in)
            return P.op("vector", lambda v: v.tensor_copy(out=dst, in_=wload[:, 0:cols]),
                        waits=[ld], inc=sm_dve)

        prev = None
        for c in range(8):
            prev = load_cast(w_dq[:, c * LAT:(c + 1) * LAT],
                             w_dq_in[c * 128:(c + 1) * 128, :], LAT,
                             waits=[prev] if prev else ())
        for c in range(8):
            prev = load_cast(w_dkv[:, c * LAT:(c + 1) * LAT],
                             w_dkv_in[c * 128:(c + 1) * 128, :], LAT, waits=[prev])
        for l in range(2):
            prev = load_cast(w_uq[:, l * D:(l + 1) * D],
                             w_uq_in[l * 128:(l + 1) * 128, :], D, waits=[prev])
        for l in range(2):
            prev = load_cast(w_ukv[:, l * 2 * D:(l + 1) * 2 * D],
                             w_ukv_in[l * 128:(l + 1) * 128, :], 2 * D, waits=[prev])

        for (w_nat, w_t) in ((w_o_in, w_oT), (w_r_in, w_rT),
                             (w_k_in, w_kT), (w_v_in, w_vT)):
            for i in range(8):
                ld = P.dma("sync", wload[:, 0:D], w_nat[i * 128:(i + 1) * 128, :],
                           waits=[prev], inc=sm_in)
                cst = P.op("vector", lambda v: v.tensor_copy(out=xln[:, 0:D],
                                                             in_=wload[:, 0:D]),
                           waits=[ld], inc=sm_dve)
                mks = []
                for c in range(8):
                    mk = P.op("tensor", lambda t, c=c: t.matmul(
                        BK(0, 0, 1024 * 2)[:, c * 128:(c + 1) * 128],
                        xln[:, c * 128:(c + 1) * 128], ident[:],
                        start=True, stop=True), waits=[cst], inc=sm_pe)
                    mks.append(mk)
                prev = P.op("scalar", lambda s, w_t=w_t, i=i: s.activation(
                    out=w_t.rearrange("p (c f) -> p c f", c=8)[:, :, i * 128:(i + 1) * 128],
                    in_=PS[:, 0:1024].rearrange("p (c f) -> p c f", c=8),
                    func=AF.Copy), waits=[mks[-1], cv_mark], inc=sm_act)
        w_done = prev
        P.op("gpsimd", lambda g: g.memset(vstage[:], 1.0), inc=sm_pl)
        P.op("gpsimd", lambda g: g.memset(yrows_all[:], 0.0), inc=sm_pl)
        P.op("gpsimd", lambda g: g.memset(sc[:], 0.0), inc=sm_pl)
        p0_pool = (sm_pl, sm_pl.n)

        def ln_stats(src_ap, n_elems, slot, waits, src_is_psum=False):
            s_sum = st[:, slot:slot + 1]
            s_sq = st[:, slot + 1:slot + 2]
            if src_is_psum:
                raise NotImplementedError
            d1 = P.op("vector", lambda v: v.tensor_reduce(
                out=s_sum, in_=src_ap, axis=AX.X, op=ALU.add), waits=waits, inc=sm_dve)
            d2 = P.op("vector", lambda v: v.scalar_tensor_tensor(
                out=jnk[:, 0:n_elems], in0=src_ap, scalar=1.0, in1=src_ap,
                op0=ALU.bypass, op1=ALU.mult, accum_out=s_sq), waits=waits, inc=sm_dve)
            m = st[:, slot + 2:slot + 3]
            d3 = P.op("vector", lambda v: v.tensor_scalar(
                out=m, in0=s_sum, scalar1=1.0 / n_elems, scalar2=0.0,
                op0=ALU.mult, op1=ALU.add), waits=[d1], inc=sm_dve)
            d4 = P.op("vector", lambda v: v.scalar_tensor_tensor(
                out=st[:, slot + 3:slot + 4], in0=m, scalar=1.0, in1=m,
                op0=ALU.mult, op1=ALU.mult), waits=[d3], inc=sm_dve)
            d5 = P.op("vector", lambda v: v.tensor_scalar(
                out=s_sq, in0=s_sq, scalar1=1.0 / n_elems,
                scalar2=st[:, slot + 3:slot + 4],
                op0=ALU.mult, op1=ALU.subtract), waits=[d2, d4], inc=sm_dve)
            d6 = P.op("scalar", lambda s: s.activation(
                out=s_sq, in_=s_sq, func=AF.Ln, bias=C_EPS), waits=[d5], inc=sm_act)
            d7 = P.op("scalar", lambda s: s.activation(
                out=s_sum, in_=s_sq, func=AF.Exp, scale=-0.5), waits=[d6], inc=sm_act)
            d8 = P.op("vector", lambda v: v.scalar_tensor_tensor(
                out=m, in0=m, scalar=-1.0, in1=s_sum,
                op0=ALU.mult, op1=ALU.mult), waits=[d7], inc=sm_dve)
            return d8

        def ln_apply(dst_ap, src_ap, slot, waits, eng="vector"):
            return P.op(eng, lambda v: v.tensor_scalar(
                out=dst_ap, in0=src_ap, scalar1=st[:, slot:slot + 1],
                scalar2=st[:, slot + 2:slot + 3],
                op0=ALU.mult, op1=ALU.add), waits=waits,
                inc=sm_dve if eng == "vector" else sm_pl)

        def idmm_batch(blocks, evac_out, evac_in, waits, evac_waits=()):
            mks = []
            for lhsT_ap, po in blocks:
                mk = P.op("tensor", lambda t, a=lhsT_ap, po=po: t.matmul(
                    PS[:, po:po + 128], a, ident[:], start=True, stop=True),
                    waits=waits if not mks else (), inc=sm_pe)
                mks.append(mk)
            ev = P.op("scalar", lambda s: s.activation(
                out=evac_out, in_=evac_in, func=AF.Copy),
                waits=[mks[-1]] + list(evac_waits), inc=sm_act)
            return mks[-1], ev

        spill_marks = []
        k2_chain_mark = None
        pv = {"ap1": [None, None], "mmx": [None, None], "evx": None,
              "mkv": None, "evk": None, "evv": None, "evc": None,
              "vmm": [None, None], "sp1": [None, None], "sp2": [None, None],
              "kmm2": None, "rm": None, "ev2": [None, None]}
        ldxs = {}
        for k0 in range(2):
            ldxs[k0] = P.dma("sync", xt[:, (k0 % 2) * D:(k0 % 2 + 1) * D],
                             x_in[k0 * 128:(k0 + 1) * 128, :],
                             waits=[w_done, p0_pool], inc=sm_in)
        k2_pend = None
        for kt in range(nkt):
            s = kt % 2
            stt = ln_stats(xt[:, s * D:(s + 1) * D], D, 0, [ldxs[kt]])
            ap1 = ln_apply(xln[:, s * D:(s + 1) * D], xt[:, s * D:(s + 1) * D],
                           0, [stt, pv["mmx"][s]])
            pv["ap1"][s] = ap1
            if kt + 2 < nkt:
                ldxs[kt + 2] = P.dma("sync", xt[:, s * D:(s + 1) * D],
                                     x_in[(kt + 2) * 128:(kt + 3) * 128, :],
                                     waits=[ap1], inc=sm_in)
            if k2_pend is not None:
                k2_chain_mark = k2_pend()
                k2_pend = None
            blocks = [(xln[:, s * D + c * 128: s * D + (c + 1) * 128], c * 128)
                      for c in range(8)]
            mmx, evx = idmm_batch(
                blocks,
                xfm.rearrange("p (c f) -> p c f", c=8)[:, :, 0:128],
                PS[:, 0:1024].rearrange("p (c f) -> p c f", c=8),
                [ap1, pv["evx"]], evac_waits=[pv["mkv"]])
            pv["mmx"][s] = mmx
            pv["evx"] = evx
            mkv = None
            for c in range(8):
                mkv = P.op("tensor", lambda t, c=c, s=s: t.matmul(
                    PS[:, 1536 + s * 256: 1792 + s * 256],
                    xfm[:, c * TS:c * TS + 128],
                    w_dkv[:, c * LAT:(c + 1) * LAT],
                    start=c == 0, stop=c == 7),
                    waits=[evx, pv["ev2"][s]] if c == 0 else (), inc=sm_pe)
            pv["mkv"] = mkv
            ev2 = P.op("vector", lambda v, s=s: v.tensor_scalar(
                out=cq_sb[:, 0:LAT], in0=PS[:, 1536 + s * 256: 1792 + s * 256],
                scalar1=1.0, scalar2=0.0,
                op0=ALU.mult, op1=ALU.add, accum_out=st[:, 4:5]), waits=[mkv], inc=sm_dve)
            pv["ev2"][s] = ev2
            sq2 = P.op("vector", lambda v: v.scalar_tensor_tensor(
                out=jnk[:, 0:LAT], in0=cq_sb[:, 0:LAT], scalar=1.0, in1=cq_sb[:, 0:LAT],
                op0=ALU.bypass, op1=ALU.mult, accum_out=st[:, 5:6]), waits=[ev2], inc=sm_dve)
            d3 = P.op("vector", lambda v: v.tensor_scalar(
                out=st[:, 6:7], in0=st[:, 4:5], scalar1=1.0 / LAT, scalar2=0.0,
                op0=ALU.mult, op1=ALU.add), waits=[ev2], inc=sm_dve)
            d4 = P.op("vector", lambda v: v.scalar_tensor_tensor(
                out=st[:, 7:8], in0=st[:, 6:7], scalar=1.0, in1=st[:, 6:7],
                op0=ALU.mult, op1=ALU.mult), waits=[d3], inc=sm_dve)
            d5 = P.op("vector", lambda v: v.tensor_scalar(
                out=st[:, 5:6], in0=st[:, 5:6], scalar1=1.0 / LAT, scalar2=st[:, 7:8],
                op0=ALU.mult, op1=ALU.subtract), waits=[sq2, d4], inc=sm_dve)
            d6 = P.op("scalar", lambda s2: s2.activation(
                out=st[:, 5:6], in_=st[:, 5:6], func=AF.Ln, bias=C_EPS), waits=[d5], inc=sm_act)
            d7 = P.op("scalar", lambda s2: s2.activation(
                out=st[:, 4:5], in_=st[:, 5:6], func=AF.Exp, scale=-0.5), waits=[d6], inc=sm_act)
            d8 = P.op("vector", lambda v: v.scalar_tensor_tensor(
                out=st[:, 6:7], in0=st[:, 6:7], scalar=-1.0, in1=st[:, 4:5],
                op0=ALU.mult, op1=ALU.mult), waits=[d7], inc=sm_dve)
            a1 = P.op("vector", lambda v: v.tensor_scalar(
                out=scr_s[:, 0:LAT], in0=cq_sb[:, 0:LAT], scalar1=st[:, 4:5],
                scalar2=st[:, 6:7], op0=ALU.mult, op1=ALU.add), waits=[d8], inc=sm_dve)
            a2 = P.op("vector", lambda v: v.scalar_tensor_tensor(
                out=scr_s[:, LAT:2 * LAT], in0=scr_s[:, 0:LAT], scalar=1.0,
                in1=gbk[:], op0=ALU.mult, op1=ALU.mult), waits=[a1, gb_done], inc=sm_dve)
            a3 = P.op("vector", lambda v: v.tensor_tensor(
                out=cqn[:, 0:LAT], in0=scr_s[:, LAT:2 * LAT], in1=bbk[:],
                op=ALU.add), waits=[a2], inc=sm_dve)
            co = s * 256
            blocks2 = [(cqn[:, l * 128:(l + 1) * 128], 1024 + l * 128) for l in range(2)]
            _, evc = idmm_batch(
                blocks2, cqfm[:, co:co + 256].rearrange("p (l f) -> p l f", l=2),
                PS[:, 1024:1280].rearrange("p (l f) -> p l f", l=2),
                [a3, pv["evc"]], evac_waits=[pv["vmm"][s]])
            pv["evc"] = evc
            kmm = None
            for c in range(8):
                for l in range(2):
                    first = (c == 0 and l == 0) or (c == 4 and l == 0)
                    stop = (c == 3 and l == 1) or (c == 7 and l == 1)
                    kmm = P.op("tensor", lambda t, c=c, l=l, first=first, stop=stop,
                               co=co: t.matmul(
                        PS[:, 2048 + c * 128: 2048 + c * 128 + 128],
                        w_ukv[:, l * 2048 + c * 128: l * 2048 + c * 128 + 128],
                        cqfm[:, co + l * 128: co + (l + 1) * 128],
                        start=first, stop=stop),
                        waits=[evc, pv["evk"]] if (c == 0 and l == 0) else (), inc=sm_pe)
            evk = P.op("scalar", lambda s2, s=s: s2.activation(
                out=kstage[:, s * D:(s + 1) * D], in_=PS[:, 2048:3072], func=AF.Copy),
                waits=[kmm, pv["sp1"][s]], inc=sm_act)
            pv["evk"] = evk
            vmm = None
            for n in range(2):
                for l in range(2):
                    vmm = P.op("tensor", lambda t, n=n, l=l, co=co: t.matmul(
                        PS[:, 3072 + n * 512: 3072 + (n + 1) * 512],
                        cqfm[:, co + l * 128: co + (l + 1) * 128],
                        w_ukv[:, l * 2048 + D + n * 512: l * 2048 + D + (n + 1) * 512],
                        start=l == 0, stop=l == 1),
                        waits=[pv["evv"]] if (n == 0 and l == 0) else (), inc=sm_pe)
            pv["vmm"][s] = vmm
            evv = P.op("scalar", lambda s2, s=s: s2.activation(
                out=vstage[:, s * NH * 65:(s + 1) * NH * 65]
                    .rearrange("p (h f) -> p h f", h=NH)[:, :, 0:64],
                in_=PS[:, 3072:4096].rearrange("p (h f) -> p h f", h=NH),
                func=AF.Copy), waits=[vmm, pv["sp2"][s], p0_pool], inc=sm_act)
            pv["evv"] = evv
            k2p = P.op("gpsimd", lambda g, s=s: g.tensor_tensor(
                out=k2s[:], in0=kstage[:, s * D:(s + 1) * D],
                in1=kstage[:, s * D:(s + 1) * D], op=ALU.mult),
                waits=[evk, pv["kmm2"]], inc=sm_pl)
            kmm2 = None
            for c in range(8):
                kmm2 = P.op("tensor", lambda t, c=c: t.matmul(
                    PS[0:1, 1280:1408], onesb[:, 0:1], k2s[:, c * 128:(c + 1) * 128],
                    start=c == 0, stop=c == 7),
                    waits=[k2p, pv["rm"]] if c == 0 else (), inc=sm_pe)
            pv["kmm2"] = kmm2
            def emit_k2red(kmm2=kmm2):
                rm = P.op("vector", lambda v: v.tensor_reduce(
                    out=sc[0:1, 1:2], in_=PS[0:1, 1280:1408], axis=AX.X, op=ALU.max),
                    waits=[kmm2], inc=sm_dve)
                pv["rm"] = rm
                return P.op("vector", lambda v: v.tensor_max(
                    out=sc[0:1, 0:1], in0=sc[0:1, 0:1], in1=sc[0:1, 1:2]),
                    waits=[rm, p0_pool], inc=sm_dve)
            k2_pend = emit_k2red
            sp1 = P.dma("sync", k_dram[kt], kstage[:, s * D:(s + 1) * D],
                        waits=[evk], inc=sm_sp)
            sp2 = P.dma("sync", v_dram[kt],
                        vstage[:, s * NH * 65:(s + 1) * NH * 65],
                        waits=[evv], inc=sm_sp, chain=False)
            pv["sp1"][s] = sp2
            pv["sp2"][s] = sp2
            spill_marks.append((sp2[0], sp2[1]))
        k2_chain_mark = k2_pend()
        p1_done = [k2_chain_mark, spill_marks[-1]]

        out_marks = []
        spillj_marks = []
        yrc_marks = []
        for t in range(nt):
            nkt_t = 4 * t + 4
            ldx0 = P.dma("sync", xt[:, 0:D], xq_in[t * TS:t * TS + 128, :],
                         waits=p1_done if t == 0 else [spillj_marks[-1]], inc=sm_in)
            ldx1 = P.dma("sync", xt[:, D:2 * D], xq_in[t * TS + 128:t * TS + 256, :],
                         inc=sm_in, chain=False)
            xfm_evs = []
            for s_i, ldm in ((0, ldx1), (1, ldx1)):
                stt = ln_stats(xt[:, s_i * D:(s_i + 1) * D], D, 0 + 4 * s_i, [ldm])
                ap1 = ln_apply(xln[:, s_i * D:(s_i + 1) * D],
                               xt[:, s_i * D:(s_i + 1) * D], 0 + 4 * s_i, [stt])
                blocks = [(xln[:, s_i * D + c * 128: s_i * D + (c + 1) * 128],
                           c * 128) for c in range(8)]
                _, evx = idmm_batch(
                    blocks,
                    xfm.rearrange("p (c f) -> p c f", c=8)[:, :, s_i * 128:(s_i + 1) * 128],
                    PS[:, 0:1024].rearrange("p (c f) -> p c f", c=8),
                    [ap1] + (xfm_evs[-1:]))
                xfm_evs.append(evx)
            q_evs = []
            evc_prev = None
            for s_i in range(2):
                for c in range(8):
                    mcq = P.op("tensor", lambda tt, c=c, s_i=s_i: tt.matmul(
                        BK(0, 0, LAT), xfm[:, c * TS + s_i * 128: c * TS + s_i * 128 + 128],
                        w_dq[:, c * LAT:(c + 1) * LAT], start=c == 0, stop=c == 7),
                        waits=list(xfm_evs) if c == 0 else (), inc=sm_pe)
                ev2 = P.op("vector", lambda v: v.tensor_scalar(
                    out=cq_sb[:, 0:LAT], in0=BK(0, 0, LAT), scalar1=1.0, scalar2=0.0,
                    op0=ALU.mult, op1=ALU.add, accum_out=st[:, 8:9]), waits=[mcq], inc=sm_dve)
                sq2 = P.op("vector", lambda v: v.scalar_tensor_tensor(
                    out=jnk[:, 0:LAT], in0=cq_sb[:, 0:LAT], scalar=1.0, in1=cq_sb[:, 0:LAT],
                    op0=ALU.bypass, op1=ALU.mult, accum_out=st[:, 9:10]), waits=[ev2], inc=sm_dve)
                d3 = P.op("vector", lambda v: v.tensor_scalar(
                    out=st[:, 10:11], in0=st[:, 8:9], scalar1=1.0 / LAT, scalar2=0.0,
                    op0=ALU.mult, op1=ALU.add), waits=[ev2], inc=sm_dve)
                d4 = P.op("vector", lambda v: v.scalar_tensor_tensor(
                    out=st[:, 11:12], in0=st[:, 10:11], scalar=1.0, in1=st[:, 10:11],
                    op0=ALU.mult, op1=ALU.mult), waits=[d3], inc=sm_dve)
                d5 = P.op("vector", lambda v: v.tensor_scalar(
                    out=st[:, 9:10], in0=st[:, 9:10], scalar1=1.0 / LAT, scalar2=st[:, 11:12],
                    op0=ALU.mult, op1=ALU.subtract), waits=[sq2, d4], inc=sm_dve)
                d6 = P.op("scalar", lambda s: s.activation(
                    out=st[:, 9:10], in_=st[:, 9:10], func=AF.Ln, bias=C_EPS),
                    waits=[d5], inc=sm_act)
                d7 = P.op("scalar", lambda s: s.activation(
                    out=st[:, 8:9], in_=st[:, 9:10], func=AF.Exp, scale=-0.5),
                    waits=[d6], inc=sm_act)
                d8 = P.op("vector", lambda v: v.scalar_tensor_tensor(
                    out=st[:, 10:11], in0=st[:, 10:11], scalar=-1.0, in1=st[:, 8:9],
                    op0=ALU.mult, op1=ALU.mult), waits=[d7], inc=sm_dve)
                a1 = P.op("vector", lambda v: v.tensor_scalar(
                    out=scr_s[:, 0:LAT], in0=cq_sb[:, 0:LAT], scalar1=st[:, 8:9],
                    scalar2=st[:, 10:11], op0=ALU.mult, op1=ALU.add), waits=[d8], inc=sm_dve)
                a2 = P.op("vector", lambda v: v.scalar_tensor_tensor(
                    out=scr_s[:, LAT:2 * LAT], in0=scr_s[:, 0:LAT], scalar=1.0,
                    in1=gbq[:], op0=ALU.mult, op1=ALU.mult), waits=[a1], inc=sm_dve)
                a3 = P.op("vector", lambda v: v.tensor_tensor(
                    out=cqn[:, 0:LAT], in0=scr_s[:, LAT:2 * LAT], in1=bbq[:],
                    op=ALU.add), waits=[a2], inc=sm_dve)
                blocks2 = [(cqn[:, l * 128:(l + 1) * 128], 512 + l * 128) for l in range(2)]
                _, evc = idmm_batch(
                    blocks2,
                    cqfm.rearrange("p (l f) -> p l f", l=2)[:, :, s_i * 128:(s_i + 1) * 128],
                    PS[:, 512:768].rearrange("p (l f) -> p l f", l=2),
                    [a3] + ([evc_prev] if (s_i == 1 and evc_prev is not None) else []))
                evc_prev = evc
                qmm = None
                for c in range(8):
                    for l in range(2):
                        first = (l == 0 and c % 2 == 0)
                        stop = (l == 1 and c % 2 == 1)
                        qmm = P.op("tensor", lambda tt, c=c, l=l, first=first, stop=stop, s_i=s_i: tt.matmul(
                            PS[:, 2048 + c * 256 + s_i * 128: 2048 + c * 256 + s_i * 128 + 128],
                            w_uq[:, l * D + c * 128: l * D + (c + 1) * 128],
                            cqfm[:, l * LAT + s_i * 128: l * LAT + s_i * 128 + 128],
                            start=first, stop=stop),
                            waits=([evc] + q_evs[-1:]) if (c == 0 and l == 0) else (),
                            inc=sm_pe)
                ev_q = P.op("scalar", lambda s, s_i=s_i: s.activation(
                    out=qfm.rearrange("p (c f) -> p c f", c=8)[:, :, s_i * 128:(s_i + 1) * 128],
                    in_=PS[:, 2048:4096].rearrange("p (c f) -> p c f", c=8)[:, :, s_i * 128:(s_i + 1) * 128],
                    func=AF.Copy), waits=[qmm], inc=sm_act)
                q_evs.append(ev_q)
            q2p = P.op("gpsimd", lambda g: g.tensor_tensor(
                out=q2[:], in0=qfm[:], in1=qfm[:], op=ALU.mult),
                waits=q_evs, inc=sm_pl)
            for c in range(8):
                qmm2 = P.op("tensor", lambda tt, c=c: tt.matmul(
                    PS[0:1, 0:TS], onesb[:, 0:1], q2[:, c * TS:(c + 1) * TS],
                    start=c == 0, stop=c == 7),
                    waits=[q2p] + xfm_evs if c == 0 else (), inc=sm_pe)
            b1 = P.op("vector", lambda v: v.tensor_reduce(
                out=sc[0:1, 2:3], in_=PS[0:1, 0:TS], axis=AX.X, op=ALU.max),
                waits=[qmm2], inc=sm_dve)
            b2 = P.op("vector", lambda v: v.tensor_mul(
                out=sc[0:1, 3:4], in0=sc[0:1, 2:3], in1=sc[0:1, 0:1]),
                waits=[b1], inc=sm_dve)
            b3 = P.op("scalar", lambda s: s.activation(
                out=sc[0:1, 3:4], in_=sc[0:1, 3:4], func=AF.Ln, bias=cvals[0:1, 2:3]),
                waits=[b2], inc=sm_act)
            b4 = P.op("scalar", lambda s: s.activation(
                out=sc[0:1, 3:4], in_=sc[0:1, 3:4], func=AF.Exp, scale=0.5),
                waits=[b3], inc=sm_act)
            b5 = P.op("vector", lambda v: v.tensor_scalar(
                out=scb[0:1, 0:1], in0=sc[0:1, 3:4], scalar1=-1.02 * INV_SQRT_DH,
                scalar2=0.0, op0=ALU.mult, op1=ALU.add), waits=[b4], inc=sm_dve)
            b6 = P.op("tensor", lambda tt: tt.matmul(
                PS[:, 256:257], onesb[0:1, :], scb[0:1, 0:1], start=True, stop=True),
                waits=[b5], inc=sm_pe)
            bias_mark = P.op("vector", lambda v: v.tensor_copy(
                out=bias_b[:], in_=PS[:, 256:257]), waits=[b6], inc=sm_dve)

            if stages == "qpath":
                om = P.dma("sync", out[t * TS: t * TS + 128, :], xt[:, 0:D],
                           waits=[bias_mark], inc=sm_out)
                out_marks.append(om)
                continue
            sub_pv = stages not in ("attn_scores", "attn_mm", "attn_ld")
            sub_epi = stages not in ("attn_scores", "attn_pv", "attn_mm", "attn_ld")
            attn_ev_marks = []
            pv_last_prev_quarter = None
            for g in range(4):
                kt_loads = []
                score_marks = []
                exp_marks = []
                pv_marks = []
                mask_marks = {}
                LAG = 3
                for kt in range(nkt_t + LAG):
                  if kt < nkt_t:
                    slot = kt % 6
                    w_sp = spill_marks[kt]
                    skv = sm_kvs[slot]
                    war = [score_marks[kt - 6], pv_marks[kt - 6]] if kt >= 6 else \
                        ([pv_last_prev_quarter] if pv_last_prev_quarter else [])
                    l1 = P.dma("sync", kbuf[:, slot * 256:(slot + 1) * 256],
                               k_dram[kt][:, g * 256:(g + 1) * 256],
                               waits=[w_sp] + war, inc=skv)
                    l2 = P.dma("sync", vbuf[:, slot * 260: slot * 260 + 260],
                               v_dram[kt][:, g * 260:(g + 1) * 260],
                               inc=skv, chain=False)
                    kt_loads.append((l2[0], l2[1]))
                    pb = (kt % 2) * 2
                    deps = [kt_loads[kt]]
                    if kt >= 2:
                        deps.append(exp_marks[kt - 2])
                    if kt == 0:
                        deps.append(bias_mark)
                    if kt <= 1 and attn_ev_marks:
                        deps.append(attn_ev_marks[-1])
                    smm = None
                    if stages == "attn_ld":
                        score_marks.append(kt_loads[kt]); exp_marks.append(kt_loads[kt])
                        pv_marks.append(kt_loads[kt])
                        continue
                    for hh in range(4):
                        pr = hh % 2
                        ch = 2 * g + hh // 2
                        soff = pr * 512 + (hh // 2) * 256
                        smm = P.op("tensor", lambda tt, hh=hh, pr=pr, ch=ch, pb=pb, slot=slot, soff=soff: tt.matmul(
                            PS[:, pb * 512 + soff: pb * 512 + soff + 256],
                            kbuf[pr * 64:(pr + 1) * 64,
                                 slot * 256 + (hh // 2) * 128: slot * 256 + (hh // 2) * 128 + 128],
                            qfm[pr * 64:(pr + 1) * 64, ch * TS:(ch + 1) * TS],
                            start=(hh // 2) == 0, stop=(hh // 2) == 1,
                            tile_position=(pr * 64, 0)),
                            waits=deps if hh == 0 else (), inc=sm_pe)
                    score_marks.append(smm)
                    pslot = kt % 4
                    edeps = [smm]
                    if kt >= 4 and kt - 4 - LAG + 1 > 0:
                        pass
                    if kt - 4 >= 0:
                        edeps.append(pv_marks[kt - 4] if kt - 4 < len(pv_marks) else None)
                    if stages == "attn_mm":
                        em = P.op("scalar", lambda s: s.nop(), waits=edeps, inc=sm_act)
                    else:
                        em = P.op("scalar", lambda s, pb=pb, pslot=pslot: s.activation(
                            out=pbuf[:, pslot * 1024:(pslot + 1) * 1024],
                            in_=PS[:, pb * 512: pb * 512 + 1024],
                            func=AF.Exp, bias=bias_b[:, 0:1], scale=INV_SQRT_DH),
                            waits=edeps, inc=sm_act)
                    exp_marks.append(em)
                    if sub_pv and kt >= nkt_t - 4:
                        sidx = kt - (nkt_t - 4)
                        mm_ = None
                        for hh in range(4):
                            hoff = (hh % 2) * 512 + (hh // 2) * 256
                            mm_ = P.op("gpsimd", lambda gg, sidx=sidx, pslot=pslot, hoff=hoff: gg.tensor_tensor(
                                out=pbuf[:, pslot * 1024 + hoff: pslot * 1024 + hoff + 256],
                                in0=pbuf[:, pslot * 1024 + hoff: pslot * 1024 + hoff + 256],
                                in1=mask_sb[:, sidx * TS:(sidx + 1) * TS], op=ALU.mult),
                                waits=[em] if hh == 0 else (), inc=sm_pl)
                        mask_marks[kt] = mm_
                  kp = kt - LAG
                  if kp >= 0 and sub_pv and stages != "attn_ld":
                    pv_dep = [mask_marks.get(kp, exp_marks[kp])]
                    pslotp = kp % 4
                    slotp = kp % 6
                    pvm = None
                    for hh in range(4):
                        pvm = P.op("tensor", lambda tt, hh=hh, pslot=pslotp, slot=slotp, kt=kp, nkt_t=nkt_t: tt.matmul(
                            PS[0:65, (4 + hh) * 512: (4 + hh) * 512 + 256],
                            vbuf[:, slot * 260 + hh * 65: slot * 260 + (hh + 1) * 65],
                            pbuf[:, pslot * 1024 + (hh % 2) * 512 + (hh // 2) * 256:
                                 pslot * 1024 + (hh % 2) * 512 + (hh // 2) * 256 + 256],
                            start=kt == 0, stop=kt == nkt_t - 1),
                            waits=pv_dep if hh == 0 else (), inc=sm_pe)
                    pv_marks.append(pvm)
                  elif kp >= 0 and (not sub_pv) and stages != "attn_ld":
                    pv_marks.append(exp_marks[kp])
                pv_last_prev_quarter = pv_marks[-1]
                if not sub_epi:
                    attn_ev_marks.append(pv_marks[-1])
                    continue
                evq = P.op("scalar", lambda s: s.activation(
                    out=aev[0:65, :].rearrange("p (h f) -> p h f", h=4),
                    in_=PS[0:65, 2048:4096].rearrange("p (h f) -> p h f", h=4)[:, :, 0:256],
                    func=AF.Copy), waits=[pv_marks[-1]], inc=sm_act)
                tmks = []
                for sl in range(8):
                    hh, qh = sl // 2, sl % 2
                    po = 65 * sl if sl < 7 else 512
                    tmks.append(P.op("tensor", lambda tt, hh=hh, qh=qh, po=po: tt.matmul(
                        PS[0:128, po:po + 65],
                        aev[0:65, hh * 256 + qh * 128: hh * 256 + qh * 128 + 128],
                        ident[0:65, 0:65], start=True, stop=True),
                        waits=[evq, exp_marks[-1], exp_marks[-2] if nkt_t >= 2 else evq]
                        if sl == 0 else (), inc=sm_pe))
                nmks = []
                for sl in range(8):
                    po = 65 * sl if sl < 7 else 512
                    r1 = P.op("vector", lambda v, po=po, sl=sl: v.reciprocal(
                        out=rc[:, sl:sl + 1], in_=PS[:, po + 64: po + 65]),
                        waits=[tmks[-1]] if sl == 0 else (), inc=sm_dve)
                    n1 = P.op("vector", lambda v, po=po, sl=sl: v.tensor_scalar(
                        out=atok[:, sl * 64:(sl + 1) * 64], in0=PS[:, po:po + 64],
                        scalar1=rc[:, sl:sl + 1], scalar2=0.0,
                        op0=ALU.mult, op1=ALU.add), waits=[r1], inc=sm_dve)
                    nmks.append(n1)
                bmks = []
                for sl in range(8):
                    hh, qh = sl // 2, sl % 2
                    po = 1024 + (hh // 2) * 512 + qh * 128
                    cp = (hh % 2) * 64
                    bmks.append(P.op("tensor", lambda tt, sl=sl, po=po, cp=cp: tt.matmul(
                        PS[cp:cp + 64, po:po + 128],
                        atok[:, sl * 64:(sl + 1) * 64], ident[:],
                        start=True, stop=True, tile_position=(0, cp)),
                        waits=[nmks[-1]] if sl == 0 else (), inc=sm_pe))
                evf = P.op("scalar", lambda s, g=g: s.activation(
                    out=attn_fm[:, 2 * g * TS: (2 * g + 2) * TS].rearrange(
                        "p (b x) -> p b x", b=2),
                    in_=PS[:, 1024:2048].rearrange("p (b x) -> p b x", b=2)[:, :, 0:256],
                    func=AF.Copy),
                    waits=[bmks[-1]], inc=sm_act)
                attn_ev_marks.append(evf)

            if stages.startswith("attn"):
                om = P.dma("sync", out[t * TS: t * TS + 128, :], xt[:, 0:D],
                           waits=list(attn_ev_marks), inc=sm_out)
                out_marks.append(om)
                continue
            omm = {}
            for s_i in range(2):
                for n in range(2):
                    for c in range(8):
                        omm[(s_i, n)] = P.op("tensor", lambda tt, s_i=s_i, n=n, c=c: tt.matmul(
                            PS[:, (2 * s_i + n) * 512: (2 * s_i + n + 1) * 512],
                            attn_fm[:, c * TS + s_i * 128: c * TS + s_i * 128 + 128],
                            w_oT[:, c * D + n * 512: c * D + (n + 1) * 512],
                            start=c == 0, stop=c == 7),
                            waits=attn_ev_marks + nmks if (c == 0 and s_i == 0 and n == 0) else (),
                            inc=sm_pe)
            y_evs = []
            for s_i in range(2):
                e1 = P.op("vector", lambda v, s_i=s_i: v.tensor_scalar(
                    out=o_sb[:, s_i * D: s_i * D + 512], in0=BK(2 * s_i, 0, 512),
                    scalar1=1.0, scalar2=0.0, op0=ALU.mult, op1=ALU.add,
                    accum_out=st[:, 0:1]), waits=[omm[(s_i, 0)]], inc=sm_dve)
                e2 = P.op("vector", lambda v, s_i=s_i: v.tensor_scalar(
                    out=o_sb[:, s_i * D + 512: (s_i + 1) * D], in0=BK(2 * s_i + 1, 0, 512),
                    scalar1=1.0, scalar2=0.0, op0=ALU.mult, op1=ALU.add,
                    accum_out=st[:, 4:5]), waits=[omm[(s_i, 1)]], inc=sm_dve)
                e3 = P.op("vector", lambda v: v.tensor_add(
                    out=st[:, 0:1], in0=st[:, 0:1], in1=st[:, 4:5]), waits=[e1, e2],
                    inc=sm_dve)
                sq = P.op("vector", lambda v, s_i=s_i: v.scalar_tensor_tensor(
                    out=jnk[:, 0:D], in0=o_sb[:, s_i * D:(s_i + 1) * D], scalar=1.0,
                    in1=o_sb[:, s_i * D:(s_i + 1) * D],
                    op0=ALU.bypass, op1=ALU.mult,
                    accum_out=st[:, 1:2]), waits=[e1, e2], inc=sm_dve)
                d3 = P.op("vector", lambda v: v.tensor_scalar(
                    out=st[:, 2:3], in0=st[:, 0:1], scalar1=1.0 / D, scalar2=0.0,
                    op0=ALU.mult, op1=ALU.add), waits=[e3], inc=sm_dve)
                d4 = P.op("vector", lambda v: v.scalar_tensor_tensor(
                    out=st[:, 3:4], in0=st[:, 2:3], scalar=1.0, in1=st[:, 2:3],
                    op0=ALU.mult, op1=ALU.mult), waits=[d3], inc=sm_dve)
                d5 = P.op("vector", lambda v: v.tensor_scalar(
                    out=st[:, 1:2], in0=st[:, 1:2], scalar1=1.0 / D, scalar2=st[:, 3:4],
                    op0=ALU.mult, op1=ALU.subtract), waits=[sq, d4], inc=sm_dve)
                d6 = P.op("scalar", lambda s: s.activation(
                    out=st[:, 1:2], in_=st[:, 1:2], func=AF.Ln, bias=C_EPS), waits=[d5],
                    inc=sm_act)
                d7 = P.op("scalar", lambda s: s.activation(
                    out=st[:, 0:1], in_=st[:, 1:2], func=AF.Exp, scale=-0.5), waits=[d6],
                    inc=sm_act)
                d8 = P.op("vector", lambda v: v.scalar_tensor_tensor(
                    out=st[:, 2:3], in0=st[:, 2:3], scalar=-1.0, in1=st[:, 0:1],
                    op0=ALU.mult, op1=ALU.mult), waits=[d7], inc=sm_dve)
                xm = P.op("vector", lambda v, s_i=s_i: v.tensor_scalar(
                    out=e_scr[:, s_i * D:(s_i + 1) * D], in0=xt[:, s_i * D:(s_i + 1) * D],
                    scalar1=1.0, scalar2=st[:, 2:3], op0=ALU.mult, op1=ALU.add),
                    waits=[d8], inc=sm_dve)
                x1m = P.op("vector", lambda v, s_i=s_i: v.scalar_tensor_tensor(
                    out=x1[:, s_i * D:(s_i + 1) * D], in0=o_sb[:, s_i * D:(s_i + 1) * D],
                    scalar=st[:, 0:1], in1=e_scr[:, s_i * D:(s_i + 1) * D],
                    op0=ALU.mult, op1=ALU.add), waits=[xm], inc=sm_dve)
                sty = ln_stats(x1[:, s_i * D:(s_i + 1) * D], D, 8 + 4 * s_i, [x1m])
                y_evs.append(ln_apply(ytm[:, s_i * D:(s_i + 1) * D],
                                      x1[:, s_i * D:(s_i + 1) * D], 8 + 4 * s_i, [sty]))
            if stages == "oproj":
                om = P.dma("sync", out[t * TS: t * TS + 128, :], x1[:, 0:D],
                           waits=y_evs, inc=sm_out)
                out_marks.append(om)
                continue
            yfm_evs = []
            for s_i in range(2):
                blocks = [(ytm[:, s_i * D + c * 128: s_i * D + (c + 1) * 128], c * 128)
                          for c in range(8)]
                _, evy = idmm_batch(
                    blocks,
                    yfm.rearrange("p (c f) -> p c f", c=8)[:, :, s_i * 128:(s_i + 1) * 128],
                    PS[:, 0:1024].rearrange("p (c f) -> p c f", c=8),
                    [y_evs[s_i]] + ([omm[(1, 1)]] if s_i == 0 else yfm_evs[-1:]))
                yfm_evs.append(evy)
            sx0 = P.dma("sync", x1_dram[t, 0:128, :], x1[:, 0:D],
                        waits=y_evs + yfm_evs, inc=sm_sp)
            sx1 = P.dma("sync", x1_dram[t, 128:256, :], x1[:, D:2 * D],
                        inc=sm_sp, chain=False)
            sy = P.dma("sync", yfm_dram[t], yfm[:], inc=sm_sp, chain=False)
            yrc = P.op("gpsimd", lambda g, t=t: g.tensor_copy(
                out=yrows_all[:, t * 8:(t + 1) * 8],
                in_=yfm.rearrange("p (c f) -> p c f", c=8)[:, :, 255:256]),
                waits=yfm_evs, inc=sm_pl)
            yrc_marks.append(yrc)
            spillj_marks.append(sy)

        srow_sp = P.dma("sync", yrow_src[:], yrows_all[:],
                        waits=[yrc_marks[-1]], inc=sm_sp)
        cl = P.op("gpsimd", lambda g: g.collective_compute(
            "AllGather", ALU.bypass,
            replica_groups=[[0, 1], [2, 3], [4, 5], [6, 7]],
            ins=[yrow_src.ap().opt()], outs=[yrow_gath.ap().opt()]),
            waits=[(sm_sp, sm_sp.n)], inc=sm_cl)
        ldA = P.dma("sync", carry_A[:], yrow_gath[0], waits=[cl], inc=sm_in)
        ldB = P.dma("sync", carry_B[:], yrow_gath[1], inc=sm_in, chain=False)
        gcast = P.op("vector", lambda v: v.tensor_copy(
            out=scb[0:1, 1:3], in_=cfg_sb[0:1, 2:4]), waits=[consts_ld], inc=sm_dve)
        gmm = P.op("tensor", lambda tt: tt.matmul(
            PS[:, 256:258], onesb[0:1, :], scb[0:1, 1:3], start=True, stop=True),
            waits=[gcast, cl], inc=sm_pe)
        ga = P.op("vector", lambda v: v.tensor_copy(
            out=gA_col[:], in_=PS[:, 256:257]), waits=[gmm], inc=sm_dve)
        gb = P.op("vector", lambda v: v.tensor_copy(
            out=gB_col[:], in_=PS[:, 257:258]), waits=[gmm], inc=sm_dve)
        c1 = P.op("vector", lambda v: v.tensor_scalar(
            out=carry_all[:], in0=carry_A[:],
            scalar1=gA_col[:, 0:1], scalar2=0.0, op0=ALU.mult, op1=ALU.add),
            waits=[ga, ldB], inc=sm_dve)
        cprep = P.op("vector", lambda v: v.scalar_tensor_tensor(
            out=carry_all[:, 8:64], in0=carry_B[:, 0:56], scalar=gB_col[:, 0:1],
            in1=carry_all[:, 8:64], op0=ALU.mult, op1=ALU.add),
            waits=[gb, c1], inc=sm_dve)

        for t in range(nt):
            x1l0 = P.dma("sync", x1[:, 0:D], x1_dram[t, 0:128, :],
                         waits=[(sm_sp, sm_sp.n), (out_marks[-1] if out_marks else cl)],
                         inc=sm_in)
            x1l1 = P.dma("sync", x1[:, D:2 * D], x1_dram[t, 128:256, :],
                         inc=sm_in, chain=False)
            yfl = P.dma("sync", yfm[:], yfm_dram[t], inc=sm_in, chain=False)
            dm = P.op("gpsimd", lambda g: g.tensor_tensor(
                out=dscr.rearrange("p (c f) -> p c f", c=8)[:, :, 1:256],
                in0=yfm.rearrange("p (c f) -> p c f", c=8)[:, :, 1:256],
                in1=yfm.rearrange("p (c f) -> p c f", c=8)[:, :, 0:255],
                op=ALU.subtract), waits=[yfl], inc=sm_pl)
            mix_marks = []
            for c in range(8):
                d0 = P.op("gpsimd", lambda g, c=c, t=t: g.tensor_tensor(
                    out=dscr[:, c * TS:c * TS + 1], in0=yfm[:, c * TS:c * TS + 1],
                    in1=carry_all[:, t * 8 + c: t * 8 + c + 1], op=ALU.subtract),
                    waits=[dm, cprep], inc=sm_pl)
                m1 = P.op("vector", lambda g, c=c, t=t: g.scalar_tensor_tensor(
                    out=xr[:, c * TS:c * TS + 1], in0=dscr[:, c * TS:c * TS + 1],
                    scalar=mu_rc[:, c:c + 1], in1=carry_all[:, t * 8 + c: t * 8 + c + 1],
                    op0=ALU.mult, op1=ALU.add), waits=[d0], inc=sm_dve)
                m2 = P.op("vector", lambda g, c=c, t=t: g.scalar_tensor_tensor(
                    out=xk[:, c * TS:c * TS + 1], in0=dscr[:, c * TS:c * TS + 1],
                    scalar=mu_kc[:, c:c + 1], in1=carry_all[:, t * 8 + c: t * 8 + c + 1],
                    op0=ALU.mult, op1=ALU.add), waits=[d0], inc=sm_dve)
                m3 = P.op("vector", lambda g, c=c: g.scalar_tensor_tensor(
                    out=xr[:, c * TS + 1:(c + 1) * TS],
                    in0=dscr[:, c * TS + 1:(c + 1) * TS],
                    scalar=mu_rc[:, c:c + 1],
                    in1=yfm[:, c * TS:(c + 1) * TS - 1],
                    op0=ALU.mult, op1=ALU.add), waits=[dm], inc=sm_dve)
                m4 = P.op("vector", lambda g, c=c: g.scalar_tensor_tensor(
                    out=xk[:, c * TS + 1:(c + 1) * TS],
                    in0=dscr[:, c * TS + 1:(c + 1) * TS],
                    scalar=mu_kc[:, c:c + 1],
                    in1=yfm[:, c * TS:(c + 1) * TS - 1],
                    op0=ALU.mult, op1=ALU.add), waits=[dm], inc=sm_dve)
                mix_marks += [m1, m2, m3, m4]
            mix_done = mix_marks[-1]
            rmm = {}
            kmm = {}
            for s_i in range(2):
                for n in range(2):
                    for c in range(8):
                        rmm[(s_i, n)] = P.op("tensor", lambda tt, s_i=s_i, n=n, c=c: tt.matmul(
                            PS[:, (2 * s_i + n) * 512: (2 * s_i + n + 1) * 512],
                            xr[:, c * TS + s_i * 128: c * TS + s_i * 128 + 128],
                            w_rT[:, c * D + n * 512: c * D + (n + 1) * 512],
                            start=c == 0, stop=c == 7),
                            waits=[mix_done]
                            if (s_i == 0 and n == 0 and c == 0) else (), inc=sm_pe)
            for s_i in range(2):
                for n in range(2):
                    for c in range(8):
                        kmm[(s_i, n)] = P.op("tensor", lambda tt, s_i=s_i, n=n, c=c: tt.matmul(
                            PS[:, 2048 + (2 * s_i + n) * 512: 2048 + (2 * s_i + n + 1) * 512],
                            xk[:, c * TS + s_i * 128: c * TS + s_i * 128 + 128],
                            w_kT[:, c * D + n * 512: c * D + (n + 1) * 512],
                            start=c == 0, stop=c == 7),
                            waits=[mix_done] if (s_i == 0 and n == 0 and c == 0) else (),
                            inc=sm_pe)
            r_marks = []
            for s_i in range(2):
                s1 = P.op("scalar", lambda s, s_i=s_i: s.activation(
                    out=e_scr[:, s_i * D:(s_i + 1) * D],
                    in_=PS[:, 2 * s_i * 512: (2 * s_i + 2) * 512],
                    func=AF.Exp, scale=-1.0), waits=[rmm[(s_i, 1)]], inc=sm_act)
                s2 = P.op("scalar", lambda s, s_i=s_i: s.activation(
                    out=e_scr[:, s_i * D:(s_i + 1) * D],
                    in_=e_scr[:, s_i * D:(s_i + 1) * D],
                    func=AF.Ln, bias=C_ONE), waits=[s1], inc=sm_act)
                s3 = P.op("scalar", lambda s, s_i=s_i: s.activation(
                    out=r_sb[:, s_i * D:(s_i + 1) * D],
                    in_=e_scr[:, s_i * D:(s_i + 1) * D],
                    func=AF.Exp, scale=-1.0), waits=[s2], inc=sm_act)
                r_marks.append(s3)
            ka_marks = []
            for s_i in range(2):
                for n in range(2):
                    rl = P.op("vector", lambda v, s_i=s_i, n=n: v.tensor_scalar_max(
                        out=kact[:, s_i * D + n * 512: s_i * D + (n + 1) * 512],
                        in0=PS[:, 2048 + (2 * s_i + n) * 512: 2048 + (2 * s_i + n + 1) * 512],
                        scalar1=0.0), waits=[kmm[(s_i, n)]], inc=sm_dve)
                    ka_marks.append(rl)
            sqk = P.op("gpsimd", lambda g: g.tensor_tensor(
                out=kact[:], in0=kact[:], in1=kact[:], op=ALU.mult),
                waits=ka_marks, inc=sm_pl)
            kfm_evs = []
            for s_i in range(2):
                blocks = [(kact[:, s_i * D + c * 128: s_i * D + (c + 1) * 128], c * 128)
                          for c in range(8)]
                _, evk2 = idmm_batch(
                    blocks,
                    kactfm.rearrange("p (c f) -> p c f", c=8)[:, :, s_i * 128:(s_i + 1) * 128],
                    PS[:, 0:1024].rearrange("p (c f) -> p c f", c=8),
                    [sqk] + ([r_marks[1]] if s_i == 0 else kfm_evs[-1:]))
                kfm_evs.append(evk2)
            vmm2 = {}
            for s_i in range(2):
                for n in range(2):
                    for c in range(8):
                        vmm2[(s_i, n)] = P.op("tensor", lambda tt, s_i=s_i, n=n, c=c: tt.matmul(
                            PS[:, 2048 + (2 * s_i + n) * 512: 2048 + (2 * s_i + n + 1) * 512],
                            kactfm[:, c * TS + s_i * 128: c * TS + s_i * 128 + 128],
                            w_vT[:, c * D + n * 512: c * D + (n + 1) * 512],
                            start=c == 0, stop=c == 7),
                            waits=kfm_evs + [ka_marks[-1]]
                            if (s_i == 0 and n == 0 and c == 0) else (), inc=sm_pe)
            for s_i in range(2):
                f1 = P.op("vector", lambda v, s_i=s_i: v.scalar_tensor_tensor(
                    out=o_sb[:, s_i * D: s_i * D + 512],
                    in0=PS[:, 2048 + 2 * s_i * 512: 2048 + (2 * s_i + 1) * 512],
                    scalar=1.0, in1=r_sb[:, s_i * D: s_i * D + 512],
                    op0=ALU.mult, op1=ALU.mult, accum_out=st[:, 0:1]),
                    waits=[vmm2[(s_i, 0)], r_marks[s_i]], inc=sm_dve)
                f2 = P.op("vector", lambda v, s_i=s_i: v.scalar_tensor_tensor(
                    out=o_sb[:, s_i * D + 512: (s_i + 1) * D],
                    in0=PS[:, 2048 + (2 * s_i + 1) * 512: 2048 + (2 * s_i + 2) * 512],
                    scalar=1.0, in1=r_sb[:, s_i * D + 512: (s_i + 1) * D],
                    op0=ALU.mult, op1=ALU.mult, accum_out=st[:, 4:5]),
                    waits=[vmm2[(s_i, 1)]], inc=sm_dve)
                e3 = P.op("vector", lambda v: v.tensor_add(
                    out=st[:, 0:1], in0=st[:, 0:1], in1=st[:, 4:5]), waits=[f1, f2],
                    inc=sm_dve)
                sq = P.op("vector", lambda v, s_i=s_i: v.scalar_tensor_tensor(
                    out=jnk[:, 0:D], in0=o_sb[:, s_i * D:(s_i + 1) * D], scalar=1.0,
                    in1=o_sb[:, s_i * D:(s_i + 1) * D],
                    op0=ALU.bypass, op1=ALU.mult,
                    accum_out=st[:, 1:2]), waits=[f1, f2], inc=sm_dve)
                d3 = P.op("vector", lambda v: v.tensor_scalar(
                    out=st[:, 2:3], in0=st[:, 0:1], scalar1=1.0 / D, scalar2=0.0,
                    op0=ALU.mult, op1=ALU.add), waits=[e3], inc=sm_dve)
                d4 = P.op("vector", lambda v: v.scalar_tensor_tensor(
                    out=st[:, 3:4], in0=st[:, 2:3], scalar=1.0, in1=st[:, 2:3],
                    op0=ALU.mult, op1=ALU.mult), waits=[d3], inc=sm_dve)
                d5 = P.op("vector", lambda v: v.tensor_scalar(
                    out=st[:, 1:2], in0=st[:, 1:2], scalar1=1.0 / D, scalar2=st[:, 3:4],
                    op0=ALU.mult, op1=ALU.subtract), waits=[sq, d4], inc=sm_dve)
                d6 = P.op("scalar", lambda s: s.activation(
                    out=st[:, 1:2], in_=st[:, 1:2], func=AF.Ln, bias=C_EPS), waits=[d5],
                    inc=sm_act)
                d7 = P.op("scalar", lambda s: s.activation(
                    out=st[:, 0:1], in_=st[:, 1:2], func=AF.Exp, scale=-0.5), waits=[d6],
                    inc=sm_act)
                d8 = P.op("vector", lambda v: v.scalar_tensor_tensor(
                    out=st[:, 2:3], in0=st[:, 2:3], scalar=-1.0, in1=st[:, 0:1],
                    op0=ALU.mult, op1=ALU.mult), waits=[d7], inc=sm_dve)
                xm = P.op("vector", lambda v, s_i=s_i: v.tensor_scalar(
                    out=e_scr[:, s_i * D:(s_i + 1) * D], in0=x1[:, s_i * D:(s_i + 1) * D],
                    scalar1=1.0, scalar2=st[:, 2:3], op0=ALU.mult, op1=ALU.add),
                    waits=[d8, yfl], inc=sm_dve)
                x2m = P.op("vector", lambda v, s_i=s_i: v.scalar_tensor_tensor(
                    out=x2[:, s_i * D:(s_i + 1) * D], in0=o_sb[:, s_i * D:(s_i + 1) * D],
                    scalar=st[:, 0:1], in1=e_scr[:, s_i * D:(s_i + 1) * D],
                    op0=ALU.mult, op1=ALU.add), waits=[xm], inc=sm_dve)
                om = P.dma("sync", out[t * TS + s_i * 128: t * TS + (s_i + 1) * 128, :],
                           x2[:, s_i * D:(s_i + 1) * D], waits=[x2m], inc=sm_out)
                out_marks.append(om)

        P.op("gpsimd", lambda g: g.nop(), waits=[out_marks[-1]], inc=sm_pl)

        with nc.Block() as block:
            @block.sync
            def _(sync):
                P.emit("sync", sync)

            @block.tensor
            def _(tensor):
                P.emit("tensor", tensor)

            @block.vector
            def _(vector):
                P.emit("vector", vector)

            @block.scalar
            def _(scalar):
                P.emit("scalar", scalar)

            @block.gpsimd
            def _(gpsimd):
                P.emit("gpsimd", gpsimd)

    return nc


_NC_CACHE = {}


def kernel(L0_attn_x, L0_attn_W_dq, L0_attn_W_uq, L0_attn_q_ln_g, L0_attn_q_ln_b,
           L0_attn_W_dkv, L0_attn_W_ukv, L0_attn_kv_ln_g, L0_attn_kv_ln_b,
           L0_attn_W_o, L0_ffn_W_r, L0_ffn_W_k, L0_ffn_W_v, L0_ffn_mu_r, L0_ffn_mu_k):
    if "nc" not in _NC_CACHE:
        _NC_CACHE["nc"] = build()
    nc = _NC_CACHE["nc"]
    x = np.ascontiguousarray(np.asarray(L0_attn_x), dtype=np.float32)
    shared = {
        "W_dq": np.ascontiguousarray(L0_attn_W_dq, np.float32),
        "W_uq": np.ascontiguousarray(L0_attn_W_uq, np.float32),
        "q_g": np.ascontiguousarray(L0_attn_q_ln_g, np.float32),
        "q_b": np.ascontiguousarray(L0_attn_q_ln_b, np.float32),
        "W_dkv": np.ascontiguousarray(L0_attn_W_dkv, np.float32),
        "W_ukv": np.ascontiguousarray(L0_attn_W_ukv, np.float32),
        "kv_g": np.ascontiguousarray(L0_attn_kv_ln_g, np.float32),
        "kv_b": np.ascontiguousarray(L0_attn_kv_ln_b, np.float32),
        "W_o": np.ascontiguousarray(L0_attn_W_o, np.float32),
        "W_r": np.ascontiguousarray(L0_ffn_W_r, np.float32),
        "W_k": np.ascontiguousarray(L0_ffn_W_k, np.float32),
        "W_v": np.ascontiguousarray(L0_ffn_W_v, np.float32),
        "mu_r": np.ascontiguousarray(L0_ffn_mu_r, np.float32),
        "mu_k": np.ascontiguousarray(L0_ffn_mu_k, np.float32),
    }
    in_maps = []
    for c in range(8):
        b, r = c // 2, c % 2
        m = dict(shared)
        m["x"] = np.ascontiguousarray(x[b])
        m["xq"] = np.ascontiguousarray(
            x[b].reshape(NT, TS, D)[r::2].reshape(8 * TS, D))
        m["maskin"] = host_masks(r)
        m["cfgi"] = np.zeros((1, 2), np.int32)
        m["cfgf"] = (np.array([[0.0, 1.0]], np.float32) if r == 0
                     else np.array([[1.0, 0.0]], np.float32))
        in_maps.append(m)
    res = run_bass_kernel_spmd(nc, in_maps, core_ids=list(range(8)))
    outp = np.empty((4, NT, TS, D), dtype=np.float32)
    for b in range(4):
        for r in range(2):
            outp[b, r::2] = res.results[2 * b + r]["out"].reshape(8, TS, D)
    return outp.reshape(4, S, D)


# version 16
# speedup vs baseline: 1.2676x; 1.0209x over previous
import contextlib

import numpy as np

import concourse.bass as bass
import concourse.mybir as mybir
from concourse.bass_utils import run_bass_kernel_spmd

F32 = mybir.dt.float32
BF16 = mybir.dt.bfloat16
AF = mybir.ActivationFunctionType
ALU = mybir.AluOpType
AX = mybir.AxisListType

D = 1024
S = 4096
NT = 16
TS = 256
NKT = 32
NH = 16
DH = 64
LAT = 256
EPS = 1e-5
INV_SQRT_DH = 0.125


class Sem:
    def __init__(self, handle):
        self.h = handle
        self.n = 0


class Planner:

    ENGS = ("sync", "tensor", "vector", "scalar", "gpsimd")

    def __init__(self):
        self.recs = {e: [] for e in self.ENGS}

    SERIAL = ("vector", "scalar", "gpsimd")

    def __init_serial(self):
        if not hasattr(self, "_own"):
            self._own = {}

    def set_own_sems(self, mapping):
        self._own = dict(mapping)

    def op(self, eng, fn, waits=(), inc=None, amt=1):
        w = [x for x in waits if x is not None]
        own = getattr(self, "_own", {}).get(eng)
        if eng in self.SERIAL and own is not None:
            w = [(own, own.n)] + w
            if inc is None:
                inc = own
        if inc is not None:
            inc.n += amt
        self.recs[eng].append((fn, w, inc, amt))
        return (inc, inc.n) if inc is not None else None

    def dma(self, eng, out, in_, waits=(), inc=None, chain=True):
        w = list(waits)
        if inc is not None and chain:
            w.append((inc, inc.n))
        return self.op(eng, lambda e: e.dma_start(out=out, in_=in_),
                       waits=w, inc=inc, amt=16)

    def emit(self, eng_name, eng):
        last_wait = {}
        dedup = eng_name != "sync"
        for fn, waits, inc, amt in self.recs[eng_name]:
            for sem, val in waits:
                if dedup and last_wait.get(id(sem), -1) >= val:
                    continue
                eng.wait_ge(sem.h, val)
                last_wait[id(sem)] = val
            inst = fn(eng)
            if inc is not None:
                inst.then_inc(inc.h, amt)


def _host_consts():
    import ml_dtypes
    bf = ml_dtypes.bfloat16
    ident = np.eye(128, dtype=np.float32).astype(bf)
    ones = np.ones((128, 128), dtype=np.float32).astype(bf)
    return ident, ones


def host_masks(rank):
    import ml_dtypes
    bf = ml_dtypes.bfloat16
    k = np.arange(128)[:, None]
    q = np.arange(TS)[None, :]
    mlow = (q >= k).astype(np.float32)
    mhigh = (q >= 128 + k).astype(np.float32)
    ones = np.ones((128, TS), np.float32)
    zero = np.zeros((128, TS), np.float32)
    stack = [ones, ones, mlow, mhigh] if rank == 1 else [mlow, mhigh, zero, zero]
    return np.stack(stack).astype(bf)


def build(nt=8, nkt=NKT, stages="all"):
    nc = bass.Bass("TRN2", target_bir_lowering=False, debug=False)

    x_in = nc.dram_tensor("x", [S, D], F32, kind="ExternalInput").ap()
    xq_in = nc.dram_tensor("xq", [8 * TS, D], F32, kind="ExternalInput").ap()
    maskin = nc.dram_tensor("maskin", [4, 128, TS], BF16, kind="ExternalInput").ap()
    cfgi_in = nc.dram_tensor("cfgi", [1, 2], mybir.dt.int32, kind="ExternalInput").ap()
    cfgf_in = nc.dram_tensor("cfgf", [1, 2], F32, kind="ExternalInput").ap()
    w_dq_in = nc.dram_tensor("W_dq", [D, LAT], F32, kind="ExternalInput").ap()
    w_uq_in = nc.dram_tensor("W_uq", [LAT, D], F32, kind="ExternalInput").ap()
    qg_in = nc.dram_tensor("q_g", [LAT], F32, kind="ExternalInput").ap()
    qb_in = nc.dram_tensor("q_b", [LAT], F32, kind="ExternalInput").ap()
    w_dkv_in = nc.dram_tensor("W_dkv", [D, LAT], F32, kind="ExternalInput").ap()
    w_ukv_in = nc.dram_tensor("W_ukv", [LAT, 2 * D], F32, kind="ExternalInput").ap()
    kg_in = nc.dram_tensor("kv_g", [LAT], F32, kind="ExternalInput").ap()
    kb_in = nc.dram_tensor("kv_b", [LAT], F32, kind="ExternalInput").ap()
    w_o_in = nc.dram_tensor("W_o", [D, D], F32, kind="ExternalInput").ap()
    w_r_in = nc.dram_tensor("W_r", [D, D], F32, kind="ExternalInput").ap()
    w_k_in = nc.dram_tensor("W_k", [D, D], F32, kind="ExternalInput").ap()
    w_v_in = nc.dram_tensor("W_v", [D, D], F32, kind="ExternalInput").ap()
    mu_r_in = nc.dram_tensor("mu_r", [D], F32, kind="ExternalInput").ap()
    mu_k_in = nc.dram_tensor("mu_k", [D], F32, kind="ExternalInput").ap()
    out = nc.dram_tensor("out", [8 * TS, D], F32, kind="ExternalOutput").ap()

    ident_np, ones_np = _host_consts()
    ident_d = nc.inline_tensor(ident_np, "ident_c")
    ones_d = nc.inline_tensor(ones_np, "ones_c")

    k_dram = nc.dram_tensor("k_scr", [NKT, 128, 1024], BF16)
    v_dram = nc.dram_tensor("v_scr", [NKT, 128, 1040], BF16)
    yrow_src = nc.dram_tensor("yrow_src", [128, 64], BF16)
    yrow_gath = nc.dram_tensor("yrow_gath", [2, 128, 64], BF16)

    taps = {}

    with contextlib.ExitStack() as cx:
        sb = lambda name, shape, dt: cx.enter_context(nc.sbuf_tensor(name, shape, dt))
        w_dq = sb("w_dq", [128, 8 * LAT], BF16)
        w_uq = sb("w_uq", [128, 2 * D], BF16)
        w_dkv = sb("w_dkv", [128, 8 * LAT], BF16)
        w_ukv = sb("w_ukv", [128, 2 * 2 * D], BF16)
        w_oT = sb("w_oT", [128, 8 * D], BF16)
        w_rT = sb("w_rT", [128, 8 * D], BF16)
        w_kT = sb("w_kT", [128, 8 * D], BF16)
        w_vT = sb("w_vT", [128, 8 * D], BF16)
        mu_rc = sb("mu_rc", [128, 8], F32)
        mu_kc = sb("mu_kc", [128, 8], F32)
        gbq = sb("gbq", [128, LAT], F32)
        bbq = sb("bbq", [128, LAT], F32)
        gbk = sb("gbk", [128, LAT], F32)
        bbk = sb("bbk", [128, LAT], F32)
        ident = sb("ident", [128, 128], BF16)
        onesb = sb("onesb", [128, 128], BF16)
        mask_sb = sb("mask_sb", [128, 4 * TS], BF16)
        carry_all = sb("carry_all", [128, 64], BF16)
        yrows_all = sb("yrows_all", [128, 64], BF16)
        y0_all = sb("y0_all", [128, 64], BF16)
        x10_all = sb("x10_all", [8, D], F32)
        carry_A = sb("carry_A", [128, 64], BF16)
        carry_B = sb("carry_B", [128, 64], BF16)
        gA_col = sb("gA_col", [128, 1], F32)
        gB_col = sb("gB_col", [128, 1], F32)
        cfg_sb = sb("cfg_sb", [1, 4], F32)
        xt = sb("xt", [128, 2 * D], F32)
        wload = xt
        xln = sb("xln", [128, 2 * D], BF16)
        xfm = sb("xfm", [128, 8 * TS], BF16)
        cq_sb = sb("cq_sb", [128, 2 * LAT], F32)
        scr_s = sb("scr_s", [128, 2 * LAT], F32)
        cqn = sb("cqn", [128, 2 * LAT], BF16)
        cqfm = sb("cqfm", [128, 2 * LAT], BF16)
        qfm = sb("qfm", [128, 8 * TS], BF16)
        kstage = sb("kstage", [128, 2 * D], BF16)
        vstage = sb("vstage", [128, 2 * NH * 65], BF16)
        kact = sb("kact", [128, 2 * D], BF16)
        k2s = kact[:, 0:D]
        kbuf = sb("kbuf", [128, 6 * 256], BF16)
        vbuf = sb("vbuf", [128, 6 * 260], BF16)
        pbuf = sb("pbuf", [128, 4 * 1024], BF16)
        aev = sb("aev", [128, 4 * 256], BF16)
        atok = sb("atok", [128, 8 * 64], BF16)
        rc = sb("rc", [128, 8], F32)
        attn_fm = sb("attn_fm", [128, 8 * TS], BF16)
        x1 = sb("x1", [128, 2 * D], F32)
        ytm = xln
        yfm = sb("yfm", [128, 8 * TS], BF16)
        dscr = sb("dscr", [128, 8 * TS], BF16)
        q2 = dscr
        xr = sb("xr", [128, 8 * TS], BF16)
        xk = sb("xk", [128, 8 * TS], BF16)
        r_sb = sb("r_sb", [128, 2 * D], F32)
        e_scr = sb("e_scr", [128, 2 * D], F32)
        kactfm = sb("kactfm", [128, 8 * TS], BF16)
        o_sb = sb("o_sb", [128, 2 * D], F32)
        x2 = xt
        vrow = xt[0:1, 0:4 * LAT]
        vrow_b = pbuf[0:1, 0:8 * LAT]
        jnk = dscr[:, 0:D]
        st = sb("st", [128, 16], F32)
        sc = sb("sc", [1, 16], F32)
        scb = sb("scb", [1, 16], BF16)
        bias_b = sb("bias_b", [128, 1], F32)
        cvals = sb("cvals", [128, 4], F32)

        PS = cx.enter_context(nc.psum_tensor("PS", [128, 4096], F32))

        sm_in = Sem(cx.enter_context(nc.semaphore("sm_in")))
        sm_sp = Sem(cx.enter_context(nc.semaphore("sm_sp")))
        sm_out = Sem(cx.enter_context(nc.semaphore("sm_out")))
        sm_pe = Sem(cx.enter_context(nc.semaphore("sm_pe")))
        sm_act = Sem(cx.enter_context(nc.semaphore("sm_act")))
        sm_dve = Sem(cx.enter_context(nc.semaphore("sm_dve")))
        sm_pl = Sem(cx.enter_context(nc.semaphore("sm_pl")))
        sm_kvs = [Sem(cx.enter_context(nc.semaphore(f"sm_kv{i}"))) for i in range(6)]
        sm_spl = [Sem(cx.enter_context(nc.semaphore(f"sm_spl{i}"))) for i in range(4)]
        sm_cl = Sem(cx.enter_context(nc.semaphore("sm_cl")))

        P = Planner()
        P.set_own_sems({"vector": sm_dve, "scalar": sm_act, "gpsimd": sm_pl})
        BK = lambda b, off=0, n=512: PS[:, b * 512 + off: b * 512 + off + n]

        P.op("vector", lambda v: v.memset(cvals[:, 0:1], EPS), inc=sm_dve)
        P.op("vector", lambda v: v.memset(cvals[:, 1:2], 1.0), inc=sm_dve)
        P.op("vector", lambda v: v.memset(cvals[:, 2:3], 1e-12), inc=sm_dve)
        cv_mark = (sm_dve, sm_dve.n)
        C_EPS = cvals[:, 0:1]
        C_ONE = cvals[:, 1:2]
        C_TINY = cvals[:, 2:3]

        P.dma("sync", ident[:], ident_d.ap(), inc=sm_in)
        P.dma("sync", onesb[:], ones_d.ap(), inc=sm_in)
        for _s in range(4):
            P.dma("sync", mask_sb[:, _s * TS:(_s + 1) * TS], maskin[_s], inc=sm_in)
        P.dma("sync", cfg_sb[0:1, 0:2].bitcast(mybir.dt.int32), cfgi_in, inc=sm_in)
        P.dma("sync", cfg_sb[0:1, 2:4], cfgf_in, inc=sm_in)
        for c in range(8):
            P.dma("sync", mu_rc[:, c:c + 1],
                  mu_r_in[c * 128:(c + 1) * 128].rearrange("(p o) -> p o", o=1), inc=sm_in)
            P.dma("sync", mu_kc[:, c:c + 1],
                  mu_k_in[c * 128:(c + 1) * 128].rearrange("(p o) -> p o", o=1), inc=sm_in)
        for i, src_ap in enumerate((qg_in, qb_in, kg_in, kb_in)):
            P.dma("sync", vrow[0:1, i * LAT:(i + 1) * LAT],
                  src_ap.rearrange("(o l) -> o l", o=1), inc=sm_in)
        consts_ld = (sm_in, sm_in.n)

        P.op("vector", lambda v: v.tensor_copy(out=vrow_b[0:1, 0:4 * LAT], in_=vrow[:]),
             waits=[consts_ld], inc=sm_dve)
        P.op("vector", lambda v: v.tensor_tensor(out=vrow_b[0:1, 4 * LAT:8 * LAT],
                                                 in0=vrow[:], in1=vrow_b[0:1, 0:4 * LAT],
                                                 op=ALU.subtract), inc=sm_dve)
        gb_prep = (sm_dve, sm_dve.n)
        for i, dst in enumerate((gbq, bbq, gbk, bbk)):
            ii = i
            P.op("tensor", lambda t, ii=ii: t.matmul(
                BK(0, 0, LAT), onesb[0:1, :], vrow_b[0:1, ii * LAT:(ii + 1) * LAT],
                start=True, stop=False), waits=[gb_prep, (sm_dve, sm_dve.n)])
            mk = P.op("tensor", lambda t, ii=ii: t.matmul(
                BK(0, 0, LAT), onesb[0:1, :], vrow_b[0:1, (4 + ii) * LAT:(5 + ii) * LAT],
                start=False, stop=True), inc=sm_pe)
            P.op("vector", lambda v, dst=dst: v.tensor_copy(out=dst[:], in_=BK(0, 0, LAT)),
                 waits=[mk], inc=sm_dve)
        gb_done = (sm_dve, sm_dve.n)

        def load_cast(dst, src_ap, cols, waits=()):
            ld = P.dma("sync", wload[:, 0:cols], src_ap, waits=list(waits) + [gb_done],
                       inc=sm_in)
            return P.op("vector", lambda v: v.tensor_copy(out=dst, in_=wload[:, 0:cols]),
                        waits=[ld], inc=sm_dve)

        prev = None
        for c in range(8):
            prev = load_cast(w_dq[:, c * LAT:(c + 1) * LAT],
                             w_dq_in[c * 128:(c + 1) * 128, :], LAT,
                             waits=[prev] if prev else ())
        for c in range(8):
            prev = load_cast(w_dkv[:, c * LAT:(c + 1) * LAT],
                             w_dkv_in[c * 128:(c + 1) * 128, :], LAT, waits=[prev])
        for l in range(2):
            prev = load_cast(w_uq[:, l * D:(l + 1) * D],
                             w_uq_in[l * 128:(l + 1) * 128, :], D, waits=[prev])
        for l in range(2):
            prev = load_cast(w_ukv[:, l * 2 * D:(l + 1) * 2 * D],
                             w_ukv_in[l * 128:(l + 1) * 128, :], 2 * D, waits=[prev])

        for (w_nat, w_t) in ((w_o_in, w_oT), (w_r_in, w_rT),
                             (w_k_in, w_kT), (w_v_in, w_vT)):
            for i in range(8):
                ld = P.dma("sync", wload[:, 0:D], w_nat[i * 128:(i + 1) * 128, :],
                           waits=[prev], inc=sm_in)
                cst = P.op("vector", lambda v: v.tensor_copy(out=xln[:, 0:D],
                                                             in_=wload[:, 0:D]),
                           waits=[ld], inc=sm_dve)
                mks = []
                for c in range(8):
                    mk = P.op("tensor", lambda t, c=c: t.matmul(
                        BK(0, 0, 1024 * 2)[:, c * 128:(c + 1) * 128],
                        xln[:, c * 128:(c + 1) * 128], ident[:],
                        start=True, stop=True), waits=[cst], inc=sm_pe)
                    mks.append(mk)
                prev = P.op("scalar", lambda s, w_t=w_t, i=i: s.activation(
                    out=w_t.rearrange("p (c f) -> p c f", c=8)[:, :, i * 128:(i + 1) * 128],
                    in_=PS[:, 0:1024].rearrange("p (c f) -> p c f", c=8),
                    func=AF.Copy), waits=[mks[-1], cv_mark], inc=sm_act)
        w_done = prev
        P.op("gpsimd", lambda g: g.memset(vstage[:], 1.0), inc=sm_pl)
        P.op("gpsimd", lambda g: g.memset(yrows_all[:], 0.0), inc=sm_pl)
        P.op("gpsimd", lambda g: g.memset(y0_all[:], 0.0), inc=sm_pl)
        P.op("gpsimd", lambda g: g.memset(x10_all[:], 0.0), inc=sm_pl)
        P.op("gpsimd", lambda g: g.memset(sc[:], 0.0), inc=sm_pl)
        p0_pool = (sm_pl, sm_pl.n)

        def ln_stats(src_ap, n_elems, slot, waits, src_is_psum=False):
            s_sum = st[:, slot:slot + 1]
            s_sq = st[:, slot + 1:slot + 2]
            if src_is_psum:
                raise NotImplementedError
            d1 = P.op("vector", lambda v: v.tensor_reduce(
                out=s_sum, in_=src_ap, axis=AX.X, op=ALU.add), waits=waits, inc=sm_dve)
            d2 = P.op("vector", lambda v: v.scalar_tensor_tensor(
                out=jnk[:, 0:n_elems], in0=src_ap, scalar=1.0, in1=src_ap,
                op0=ALU.bypass, op1=ALU.mult, accum_out=s_sq), waits=waits, inc=sm_dve)
            m = st[:, slot + 2:slot + 3]
            d3 = P.op("vector", lambda v: v.tensor_scalar(
                out=m, in0=s_sum, scalar1=1.0 / n_elems, scalar2=0.0,
                op0=ALU.mult, op1=ALU.add), waits=[d1], inc=sm_dve)
            d4 = P.op("vector", lambda v: v.scalar_tensor_tensor(
                out=st[:, slot + 3:slot + 4], in0=m, scalar=1.0, in1=m,
                op0=ALU.mult, op1=ALU.mult), waits=[d3], inc=sm_dve)
            d5 = P.op("vector", lambda v: v.tensor_scalar(
                out=s_sq, in0=s_sq, scalar1=1.0 / n_elems,
                scalar2=st[:, slot + 3:slot + 4],
                op0=ALU.mult, op1=ALU.subtract), waits=[d2, d4], inc=sm_dve)
            d6 = P.op("scalar", lambda s: s.activation(
                out=s_sq, in_=s_sq, func=AF.Ln, bias=C_EPS), waits=[d5], inc=sm_act)
            d7 = P.op("scalar", lambda s: s.activation(
                out=s_sum, in_=s_sq, func=AF.Exp, scale=-0.5), waits=[d6], inc=sm_act)
            d8 = P.op("vector", lambda v: v.scalar_tensor_tensor(
                out=m, in0=m, scalar=-1.0, in1=s_sum,
                op0=ALU.mult, op1=ALU.mult), waits=[d7], inc=sm_dve)
            return d8

        def ln_apply(dst_ap, src_ap, slot, waits, eng="vector"):
            return P.op(eng, lambda v: v.tensor_scalar(
                out=dst_ap, in0=src_ap, scalar1=st[:, slot:slot + 1],
                scalar2=st[:, slot + 2:slot + 3],
                op0=ALU.mult, op1=ALU.add), waits=waits,
                inc=sm_dve if eng == "vector" else sm_pl)

        def idmm_batch(blocks, evac_out, evac_in, waits, evac_waits=()):
            mks = []
            for lhsT_ap, po in blocks:
                mk = P.op("tensor", lambda t, a=lhsT_ap, po=po: t.matmul(
                    PS[:, po:po + 128], a, ident[:], start=True, stop=True),
                    waits=waits if not mks else (), inc=sm_pe)
                mks.append(mk)
            ev = P.op("scalar", lambda s: s.activation(
                out=evac_out, in_=evac_in, func=AF.Copy),
                waits=[mks[-1]] + list(evac_waits), inc=sm_act)
            return mks[-1], ev

        spill_marks = []
        k2_chain_mark = None
        pv = {"ap1": [None, None], "mmx": [None, None], "evx": None,
              "mkv": None, "evk": None, "evv": None, "evc": None,
              "vmm": [None, None], "sp1": [None, None], "sp2": [None, None],
              "kmm2": None, "rm": None, "ev2": [None, None]}
        ldxs = {}
        for k0 in range(2):
            ldxs[k0] = P.dma("sync", xt[:, (k0 % 2) * D:(k0 % 2 + 1) * D],
                             x_in[k0 * 128:(k0 + 1) * 128, :],
                             waits=[w_done, p0_pool], inc=sm_in)
        k2_pend = None
        for kt in range(nkt):
            s = kt % 2
            stt = ln_stats(xt[:, s * D:(s + 1) * D], D, 0, [ldxs[kt]])
            ap1 = ln_apply(xln[:, s * D:(s + 1) * D], xt[:, s * D:(s + 1) * D],
                           0, [stt, pv["mmx"][s]])
            pv["ap1"][s] = ap1
            if kt + 2 < nkt:
                ldxs[kt + 2] = P.dma("sync", xt[:, s * D:(s + 1) * D],
                                     x_in[(kt + 2) * 128:(kt + 3) * 128, :],
                                     waits=[ap1], inc=sm_in)
            if k2_pend is not None:
                k2_chain_mark = k2_pend()
                k2_pend = None
            blocks = [(xln[:, s * D + c * 128: s * D + (c + 1) * 128], c * 128)
                      for c in range(8)]
            mmx, evx = idmm_batch(
                blocks,
                xfm.rearrange("p (c f) -> p c f", c=8)[:, :, 0:128],
                PS[:, 0:1024].rearrange("p (c f) -> p c f", c=8),
                [ap1, pv["evx"]], evac_waits=[pv["mkv"]])
            pv["mmx"][s] = mmx
            pv["evx"] = evx
            mkv = None
            for c in range(8):
                mkv = P.op("tensor", lambda t, c=c, s=s: t.matmul(
                    PS[:, 1536 + s * 256: 1792 + s * 256],
                    xfm[:, c * TS:c * TS + 128],
                    w_dkv[:, c * LAT:(c + 1) * LAT],
                    start=c == 0, stop=c == 7),
                    waits=[evx, pv["ev2"][s]] if c == 0 else (), inc=sm_pe)
            pv["mkv"] = mkv
            ev2 = P.op("vector", lambda v, s=s: v.tensor_scalar(
                out=cq_sb[:, 0:LAT], in0=PS[:, 1536 + s * 256: 1792 + s * 256],
                scalar1=1.0, scalar2=0.0,
                op0=ALU.mult, op1=ALU.add, accum_out=st[:, 4:5]), waits=[mkv], inc=sm_dve)
            pv["ev2"][s] = ev2
            sq2 = P.op("vector", lambda v: v.scalar_tensor_tensor(
                out=jnk[:, 0:LAT], in0=cq_sb[:, 0:LAT], scalar=1.0, in1=cq_sb[:, 0:LAT],
                op0=ALU.bypass, op1=ALU.mult, accum_out=st[:, 5:6]), waits=[ev2], inc=sm_dve)
            d3 = P.op("vector", lambda v: v.tensor_scalar(
                out=st[:, 6:7], in0=st[:, 4:5], scalar1=1.0 / LAT, scalar2=0.0,
                op0=ALU.mult, op1=ALU.add), waits=[ev2], inc=sm_dve)
            d4 = P.op("vector", lambda v: v.scalar_tensor_tensor(
                out=st[:, 7:8], in0=st[:, 6:7], scalar=1.0, in1=st[:, 6:7],
                op0=ALU.mult, op1=ALU.mult), waits=[d3], inc=sm_dve)
            d5 = P.op("vector", lambda v: v.tensor_scalar(
                out=st[:, 5:6], in0=st[:, 5:6], scalar1=1.0 / LAT, scalar2=st[:, 7:8],
                op0=ALU.mult, op1=ALU.subtract), waits=[sq2, d4], inc=sm_dve)
            d6 = P.op("scalar", lambda s2: s2.activation(
                out=st[:, 5:6], in_=st[:, 5:6], func=AF.Ln, bias=C_EPS), waits=[d5], inc=sm_act)
            d7 = P.op("scalar", lambda s2: s2.activation(
                out=st[:, 4:5], in_=st[:, 5:6], func=AF.Exp, scale=-0.5), waits=[d6], inc=sm_act)
            d8 = P.op("vector", lambda v: v.scalar_tensor_tensor(
                out=st[:, 6:7], in0=st[:, 6:7], scalar=-1.0, in1=st[:, 4:5],
                op0=ALU.mult, op1=ALU.mult), waits=[d7], inc=sm_dve)
            a1 = P.op("vector", lambda v: v.tensor_scalar(
                out=scr_s[:, 0:LAT], in0=cq_sb[:, 0:LAT], scalar1=st[:, 4:5],
                scalar2=st[:, 6:7], op0=ALU.mult, op1=ALU.add), waits=[d8], inc=sm_dve)
            a2 = P.op("vector", lambda v: v.scalar_tensor_tensor(
                out=scr_s[:, LAT:2 * LAT], in0=scr_s[:, 0:LAT], scalar=1.0,
                in1=gbk[:], op0=ALU.mult, op1=ALU.mult), waits=[a1, gb_done], inc=sm_dve)
            a3 = P.op("vector", lambda v: v.tensor_tensor(
                out=cqn[:, 0:LAT], in0=scr_s[:, LAT:2 * LAT], in1=bbk[:],
                op=ALU.add), waits=[a2], inc=sm_dve)
            co = s * 256
            blocks2 = [(cqn[:, l * 128:(l + 1) * 128], 1024 + l * 128) for l in range(2)]
            _, evc = idmm_batch(
                blocks2, cqfm[:, co:co + 256].rearrange("p (l f) -> p l f", l=2),
                PS[:, 1024:1280].rearrange("p (l f) -> p l f", l=2),
                [a3, pv["evc"]], evac_waits=[pv["vmm"][s]])
            pv["evc"] = evc
            kmm = None
            for c in range(8):
                for l in range(2):
                    first = (c == 0 and l == 0) or (c == 4 and l == 0)
                    stop = (c == 3 and l == 1) or (c == 7 and l == 1)
                    kmm = P.op("tensor", lambda t, c=c, l=l, first=first, stop=stop,
                               co=co: t.matmul(
                        PS[:, 2048 + c * 128: 2048 + c * 128 + 128],
                        w_ukv[:, l * 2048 + c * 128: l * 2048 + c * 128 + 128],
                        cqfm[:, co + l * 128: co + (l + 1) * 128],
                        start=first, stop=stop),
                        waits=[evc, pv["evk"]] if (c == 0 and l == 0) else (), inc=sm_pe)
            evk = P.op("scalar", lambda s2, s=s: s2.activation(
                out=kstage[:, s * D:(s + 1) * D], in_=PS[:, 2048:3072], func=AF.Copy),
                waits=[kmm, pv["sp1"][s]], inc=sm_act)
            pv["evk"] = evk
            vmm = None
            for n in range(2):
                for l in range(2):
                    vmm = P.op("tensor", lambda t, n=n, l=l, co=co: t.matmul(
                        PS[:, 3072 + n * 512: 3072 + (n + 1) * 512],
                        cqfm[:, co + l * 128: co + (l + 1) * 128],
                        w_ukv[:, l * 2048 + D + n * 512: l * 2048 + D + (n + 1) * 512],
                        start=l == 0, stop=l == 1),
                        waits=[pv["evv"]] if (n == 0 and l == 0) else (), inc=sm_pe)
            pv["vmm"][s] = vmm
            evv = P.op("scalar", lambda s2, s=s: s2.activation(
                out=vstage[:, s * NH * 65:(s + 1) * NH * 65]
                    .rearrange("p (h f) -> p h f", h=NH)[:, :, 0:64],
                in_=PS[:, 3072:4096].rearrange("p (h f) -> p h f", h=NH),
                func=AF.Copy), waits=[vmm, pv["sp2"][s], p0_pool], inc=sm_act)
            pv["evv"] = evv
            k2p = P.op("gpsimd", lambda g, s=s: g.tensor_tensor(
                out=k2s[:], in0=kstage[:, s * D:(s + 1) * D],
                in1=kstage[:, s * D:(s + 1) * D], op=ALU.mult),
                waits=[evk, pv["kmm2"]], inc=sm_pl)
            kmm2 = None
            for c in range(8):
                kmm2 = P.op("tensor", lambda t, c=c: t.matmul(
                    PS[0:1, 1280:1408], onesb[:, 0:1], k2s[:, c * 128:(c + 1) * 128],
                    start=c == 0, stop=c == 7),
                    waits=[k2p, pv["rm"]] if c == 0 else (), inc=sm_pe)
            pv["kmm2"] = kmm2
            def emit_k2red(kmm2=kmm2):
                rm = P.op("vector", lambda v: v.tensor_reduce(
                    out=sc[0:1, 1:2], in_=PS[0:1, 1280:1408], axis=AX.X, op=ALU.max),
                    waits=[kmm2], inc=sm_dve)
                pv["rm"] = rm
                return P.op("vector", lambda v: v.tensor_max(
                    out=sc[0:1, 0:1], in0=sc[0:1, 0:1], in1=sc[0:1, 1:2]),
                    waits=[rm, p0_pool], inc=sm_dve)
            k2_pend = emit_k2red
            sp1 = P.dma("sync", k_dram[kt], kstage[:, s * D:(s + 1) * D],
                        waits=[evk], inc=sm_sp)
            sp2 = P.dma("sync", v_dram[kt],
                        vstage[:, s * NH * 65:(s + 1) * NH * 65],
                        waits=[evv], inc=sm_sp, chain=False)
            pv["sp1"][s] = sp2
            pv["sp2"][s] = sp2
            spill_marks.append((sp2[0], sp2[1]))
        k2_chain_mark = k2_pend()
        p1_done = [k2_chain_mark, spill_marks[-1]]

        out_marks = []
        spillj_marks = []
        yrc_marks = []
        for t in range(nt):
            nkt_t = 4 * t + 4
            ldx0 = P.dma("sync", xt[:, 0:D], xq_in[t * TS:t * TS + 128, :],
                         waits=p1_done if t == 0 else [jprev["x1m"]], inc=sm_in)
            ldx1 = P.dma("sync", xt[:, D:2 * D], xq_in[t * TS + 128:t * TS + 256, :],
                         inc=sm_in, chain=False)
            xfm_evs = []
            for s_i, ldm in ((0, ldx1), (1, ldx1)):
                stt = ln_stats(xt[:, s_i * D:(s_i + 1) * D], D, 0 + 4 * s_i, [ldm])
                ap1 = ln_apply(xln[:, s_i * D:(s_i + 1) * D],
                               xt[:, s_i * D:(s_i + 1) * D], 0 + 4 * s_i, [stt])
                blocks = [(xln[:, s_i * D + c * 128: s_i * D + (c + 1) * 128],
                           c * 128) for c in range(8)]
                _, evx = idmm_batch(
                    blocks,
                    xfm.rearrange("p (c f) -> p c f", c=8)[:, :, s_i * 128:(s_i + 1) * 128],
                    PS[:, 0:1024].rearrange("p (c f) -> p c f", c=8),
                    [ap1] + (xfm_evs[-1:]))
                xfm_evs.append(evx)
            q_evs = []
            evc_prev = None
            for s_i in range(2):
                for c in range(8):
                    mcq = P.op("tensor", lambda tt, c=c, s_i=s_i: tt.matmul(
                        BK(0, 0, LAT), xfm[:, c * TS + s_i * 128: c * TS + s_i * 128 + 128],
                        w_dq[:, c * LAT:(c + 1) * LAT], start=c == 0, stop=c == 7),
                        waits=(list(xfm_evs) + ([jprev["ffn03"]] if t else []))
                        if c == 0 else (), inc=sm_pe)
                ev2 = P.op("vector", lambda v: v.tensor_scalar(
                    out=cq_sb[:, 0:LAT], in0=BK(0, 0, LAT), scalar1=1.0, scalar2=0.0,
                    op0=ALU.mult, op1=ALU.add, accum_out=st[:, 8:9]), waits=[mcq], inc=sm_dve)
                sq2 = P.op("vector", lambda v: v.scalar_tensor_tensor(
                    out=jnk[:, 0:LAT], in0=cq_sb[:, 0:LAT], scalar=1.0, in1=cq_sb[:, 0:LAT],
                    op0=ALU.bypass, op1=ALU.mult, accum_out=st[:, 9:10]), waits=[ev2], inc=sm_dve)
                d3 = P.op("vector", lambda v: v.tensor_scalar(
                    out=st[:, 10:11], in0=st[:, 8:9], scalar1=1.0 / LAT, scalar2=0.0,
                    op0=ALU.mult, op1=ALU.add), waits=[ev2], inc=sm_dve)
                d4 = P.op("vector", lambda v: v.scalar_tensor_tensor(
                    out=st[:, 11:12], in0=st[:, 10:11], scalar=1.0, in1=st[:, 10:11],
                    op0=ALU.mult, op1=ALU.mult), waits=[d3], inc=sm_dve)
                d5 = P.op("vector", lambda v: v.tensor_scalar(
                    out=st[:, 9:10], in0=st[:, 9:10], scalar1=1.0 / LAT, scalar2=st[:, 11:12],
                    op0=ALU.mult, op1=ALU.subtract), waits=[sq2, d4], inc=sm_dve)
                d6 = P.op("scalar", lambda s: s.activation(
                    out=st[:, 9:10], in_=st[:, 9:10], func=AF.Ln, bias=C_EPS),
                    waits=[d5], inc=sm_act)
                d7 = P.op("scalar", lambda s: s.activation(
                    out=st[:, 8:9], in_=st[:, 9:10], func=AF.Exp, scale=-0.5),
                    waits=[d6], inc=sm_act)
                d8 = P.op("vector", lambda v: v.scalar_tensor_tensor(
                    out=st[:, 10:11], in0=st[:, 10:11], scalar=-1.0, in1=st[:, 8:9],
                    op0=ALU.mult, op1=ALU.mult), waits=[d7], inc=sm_dve)
                a1 = P.op("vector", lambda v: v.tensor_scalar(
                    out=scr_s[:, 0:LAT], in0=cq_sb[:, 0:LAT], scalar1=st[:, 8:9],
                    scalar2=st[:, 10:11], op0=ALU.mult, op1=ALU.add), waits=[d8], inc=sm_dve)
                a2 = P.op("vector", lambda v: v.scalar_tensor_tensor(
                    out=scr_s[:, LAT:2 * LAT], in0=scr_s[:, 0:LAT], scalar=1.0,
                    in1=gbq[:], op0=ALU.mult, op1=ALU.mult), waits=[a1], inc=sm_dve)
                a3 = P.op("vector", lambda v: v.tensor_tensor(
                    out=cqn[:, 0:LAT], in0=scr_s[:, LAT:2 * LAT], in1=bbq[:],
                    op=ALU.add), waits=[a2], inc=sm_dve)
                blocks2 = [(cqn[:, l * 128:(l + 1) * 128], 512 + l * 128) for l in range(2)]
                _, evc = idmm_batch(
                    blocks2,
                    cqfm.rearrange("p (l f) -> p l f", l=2)[:, :, s_i * 128:(s_i + 1) * 128],
                    PS[:, 512:768].rearrange("p (l f) -> p l f", l=2),
                    [a3] + ([evc_prev] if (s_i == 1 and evc_prev is not None) else []))
                evc_prev = evc
                qmm = None
                for c in range(8):
                    for l in range(2):
                        first = (l == 0 and c % 2 == 0)
                        stop = (l == 1 and c % 2 == 1)
                        qmm = P.op("tensor", lambda tt, c=c, l=l, first=first, stop=stop, s_i=s_i: tt.matmul(
                            PS[:, 2048 + c * 256 + s_i * 128: 2048 + c * 256 + s_i * 128 + 128],
                            w_uq[:, l * D + c * 128: l * D + (c + 1) * 128],
                            cqfm[:, l * LAT + s_i * 128: l * LAT + s_i * 128 + 128],
                            start=first, stop=stop),
                            waits=([evc] + q_evs[-1:] + ([jprev["ffn47"]] if t else []))
                            if (c == 0 and l == 0) else (),
                            inc=sm_pe)
                ev_q = P.op("scalar", lambda s, s_i=s_i: s.activation(
                    out=qfm.rearrange("p (c f) -> p c f", c=8)[:, :, s_i * 128:(s_i + 1) * 128],
                    in_=PS[:, 2048:4096].rearrange("p (c f) -> p c f", c=8)[:, :, s_i * 128:(s_i + 1) * 128],
                    func=AF.Copy), waits=[qmm], inc=sm_act)
                q_evs.append(ev_q)
            q2p = P.op("gpsimd", lambda g: g.tensor_tensor(
                out=q2[:], in0=qfm[:], in1=qfm[:], op=ALU.mult),
                waits=q_evs + ([jprev["mixd"]] if t else []), inc=sm_pl)
            for c in range(8):
                qmm2 = P.op("tensor", lambda tt, c=c: tt.matmul(
                    PS[0:1, 0:TS], onesb[:, 0:1], q2[:, c * TS:(c + 1) * TS],
                    start=c == 0, stop=c == 7),
                    waits=[q2p] + xfm_evs if c == 0 else (), inc=sm_pe)
            b1 = P.op("vector", lambda v: v.tensor_reduce(
                out=sc[0:1, 2:3], in_=PS[0:1, 0:TS], axis=AX.X, op=ALU.max),
                waits=[qmm2], inc=sm_dve)
            b2 = P.op("vector", lambda v: v.tensor_mul(
                out=sc[0:1, 3:4], in0=sc[0:1, 2:3], in1=sc[0:1, 0:1]),
                waits=[b1], inc=sm_dve)
            b3 = P.op("scalar", lambda s: s.activation(
                out=sc[0:1, 3:4], in_=sc[0:1, 3:4], func=AF.Ln, bias=cvals[0:1, 2:3]),
                waits=[b2], inc=sm_act)
            b4 = P.op("scalar", lambda s: s.activation(
                out=sc[0:1, 3:4], in_=sc[0:1, 3:4], func=AF.Exp, scale=0.5),
                waits=[b3], inc=sm_act)
            b5 = P.op("vector", lambda v: v.tensor_scalar(
                out=scb[0:1, 0:1], in0=sc[0:1, 3:4], scalar1=-1.02 * INV_SQRT_DH,
                scalar2=0.0, op0=ALU.mult, op1=ALU.add), waits=[b4], inc=sm_dve)
            b6 = P.op("tensor", lambda tt: tt.matmul(
                PS[:, 256:257], onesb[0:1, :], scb[0:1, 0:1], start=True, stop=True),
                waits=[b5], inc=sm_pe)
            bias_mark = P.op("vector", lambda v: v.tensor_copy(
                out=bias_b[:], in_=PS[:, 256:257]), waits=[b6], inc=sm_dve)

            if stages == "qpath":
                om = P.dma("sync", out[t * TS: t * TS + 128, :], xt[:, 0:D],
                           waits=[bias_mark], inc=sm_out)
                out_marks.append(om)
                continue
            sub_pv = stages not in ("attn_scores", "attn_mm", "attn_ld")
            sub_epi = stages not in ("attn_scores", "attn_pv", "attn_mm", "attn_ld")
            attn_ev_marks = []
            pv_last_prev_quarter = None
            for g in range(4):
                kt_loads = []
                score_marks = []
                exp_marks = []
                pv_marks = []
                mask_marks = {}
                LAG = 3
                for kt in range(nkt_t + LAG):
                  if kt < nkt_t:
                    slot = kt % 6
                    w_sp = spill_marks[kt]
                    skv = sm_kvs[slot]
                    war = [score_marks[kt - 6], pv_marks[kt - 6]] if kt >= 6 else \
                        ([pv_last_prev_quarter] if pv_last_prev_quarter else [])
                    l1 = P.dma("sync", kbuf[:, slot * 256:(slot + 1) * 256],
                               k_dram[kt][:, g * 256:(g + 1) * 256],
                               waits=[w_sp] + war, inc=skv)
                    l2 = P.dma("sync", vbuf[:, slot * 260: slot * 260 + 260],
                               v_dram[kt][:, g * 260:(g + 1) * 260],
                               inc=skv, chain=False)
                    kt_loads.append((l2[0], l2[1]))
                    pb = (kt % 2) * 2
                    deps = [kt_loads[kt]]
                    if kt >= 2:
                        deps.append(exp_marks[kt - 2])
                    if kt == 0:
                        deps.append(bias_mark)
                    if kt <= 1 and attn_ev_marks:
                        deps.append(attn_ev_marks[-1])
                    smm = None
                    if stages == "attn_ld":
                        score_marks.append(kt_loads[kt]); exp_marks.append(kt_loads[kt])
                        pv_marks.append(kt_loads[kt])
                        continue
                    for hh in range(4):
                        pr = hh % 2
                        ch = 2 * g + hh // 2
                        soff = pr * 512 + (hh // 2) * 256
                        smm = P.op("tensor", lambda tt, hh=hh, pr=pr, ch=ch, pb=pb, slot=slot, soff=soff: tt.matmul(
                            PS[:, pb * 512 + soff: pb * 512 + soff + 256],
                            kbuf[pr * 64:(pr + 1) * 64,
                                 slot * 256 + (hh // 2) * 128: slot * 256 + (hh // 2) * 128 + 128],
                            qfm[pr * 64:(pr + 1) * 64, ch * TS:(ch + 1) * TS],
                            start=(hh // 2) == 0, stop=(hh // 2) == 1,
                            tile_position=(pr * 64, 0)),
                            waits=deps if hh == 0 else (), inc=sm_pe)
                    score_marks.append(smm)
                    pslot = kt % 4
                    edeps = [smm]
                    if kt >= 4 and kt - 4 - LAG + 1 > 0:
                        pass
                    if kt - 4 >= 0:
                        edeps.append(pv_marks[kt - 4] if kt - 4 < len(pv_marks) else None)
                    if stages == "attn_mm":
                        em = P.op("scalar", lambda s: s.nop(), waits=edeps, inc=sm_act)
                    else:
                        em = P.op("scalar", lambda s, pb=pb, pslot=pslot: s.activation(
                            out=pbuf[:, pslot * 1024:(pslot + 1) * 1024],
                            in_=PS[:, pb * 512: pb * 512 + 1024],
                            func=AF.Exp, bias=bias_b[:, 0:1], scale=INV_SQRT_DH),
                            waits=edeps, inc=sm_act)
                    exp_marks.append(em)
                    if sub_pv and kt >= nkt_t - 4:
                        sidx = kt - (nkt_t - 4)
                        mm_ = None
                        for hh in range(4):
                            hoff = (hh % 2) * 512 + (hh // 2) * 256
                            mm_ = P.op("gpsimd", lambda gg, sidx=sidx, pslot=pslot, hoff=hoff: gg.tensor_tensor(
                                out=pbuf[:, pslot * 1024 + hoff: pslot * 1024 + hoff + 256],
                                in0=pbuf[:, pslot * 1024 + hoff: pslot * 1024 + hoff + 256],
                                in1=mask_sb[:, sidx * TS:(sidx + 1) * TS], op=ALU.mult),
                                waits=[em] if hh == 0 else (), inc=sm_pl)
                        mask_marks[kt] = mm_
                  kp = kt - LAG
                  if kp >= 0 and sub_pv and stages != "attn_ld":
                    pv_dep = [mask_marks.get(kp, exp_marks[kp])]
                    pslotp = kp % 4
                    slotp = kp % 6
                    pvm = None
                    for hh in range(4):
                        pvm = P.op("tensor", lambda tt, hh=hh, pslot=pslotp, slot=slotp, kt=kp, nkt_t=nkt_t: tt.matmul(
                            PS[0:65, (4 + hh) * 512: (4 + hh) * 512 + 256],
                            vbuf[:, slot * 260 + hh * 65: slot * 260 + (hh + 1) * 65],
                            pbuf[:, pslot * 1024 + (hh % 2) * 512 + (hh // 2) * 256:
                                 pslot * 1024 + (hh % 2) * 512 + (hh // 2) * 256 + 256],
                            start=kt == 0, stop=kt == nkt_t - 1),
                            waits=pv_dep if hh == 0 else (), inc=sm_pe)
                    pv_marks.append(pvm)
                  elif kp >= 0 and (not sub_pv) and stages != "attn_ld":
                    pv_marks.append(exp_marks[kp])
                pv_last_prev_quarter = pv_marks[-1]
                if not sub_epi:
                    attn_ev_marks.append(pv_marks[-1])
                    continue
                evq = P.op("scalar", lambda s: s.activation(
                    out=aev[0:65, :].rearrange("p (h f) -> p h f", h=4),
                    in_=PS[0:65, 2048:4096].rearrange("p (h f) -> p h f", h=4)[:, :, 0:256],
                    func=AF.Copy), waits=[pv_marks[-1]], inc=sm_act)
                tmks = []
                for sl in range(8):
                    hh, qh = sl // 2, sl % 2
                    po = 65 * sl if sl < 7 else 512
                    tmks.append(P.op("tensor", lambda tt, hh=hh, qh=qh, po=po: tt.matmul(
                        PS[0:128, po:po + 65],
                        aev[0:65, hh * 256 + qh * 128: hh * 256 + qh * 128 + 128],
                        ident[0:65, 0:65], start=True, stop=True),
                        waits=[evq, exp_marks[-1], exp_marks[-2] if nkt_t >= 2 else evq]
                        if sl == 0 else (), inc=sm_pe))
                nmks = []
                for sl in range(8):
                    po = 65 * sl if sl < 7 else 512
                    r1 = P.op("vector", lambda v, po=po, sl=sl: v.reciprocal(
                        out=rc[:, sl:sl + 1], in_=PS[:, po + 64: po + 65]),
                        waits=[tmks[-1]] if sl == 0 else (), inc=sm_dve)
                    n1 = P.op("vector", lambda v, po=po, sl=sl: v.tensor_scalar(
                        out=atok[:, sl * 64:(sl + 1) * 64], in0=PS[:, po:po + 64],
                        scalar1=rc[:, sl:sl + 1], scalar2=0.0,
                        op0=ALU.mult, op1=ALU.add), waits=[r1], inc=sm_dve)
                    nmks.append(n1)
                bmks = []
                for sl in range(8):
                    hh, qh = sl // 2, sl % 2
                    po = 1024 + (hh // 2) * 512 + qh * 128
                    cp = (hh % 2) * 64
                    bmks.append(P.op("tensor", lambda tt, sl=sl, po=po, cp=cp: tt.matmul(
                        PS[cp:cp + 64, po:po + 128],
                        atok[:, sl * 64:(sl + 1) * 64], ident[:],
                        start=True, stop=True, tile_position=(0, cp)),
                        waits=[nmks[-1]] if sl == 0 else (), inc=sm_pe))
                evf = P.op("scalar", lambda s, g=g: s.activation(
                    out=attn_fm[:, 2 * g * TS: (2 * g + 2) * TS].rearrange(
                        "p (b x) -> p b x", b=2),
                    in_=PS[:, 1024:2048].rearrange("p (b x) -> p b x", b=2)[:, :, 0:256],
                    func=AF.Copy),
                    waits=[bmks[-1]], inc=sm_act)
                attn_ev_marks.append(evf)

            if stages.startswith("attn"):
                om = P.dma("sync", out[t * TS: t * TS + 128, :], xt[:, 0:D],
                           waits=list(attn_ev_marks), inc=sm_out)
                out_marks.append(om)
                continue
            omm = {}
            for s_i in range(2):
                for n in range(2):
                    for c in range(8):
                        omm[(s_i, n)] = P.op("tensor", lambda tt, s_i=s_i, n=n, c=c: tt.matmul(
                            PS[:, (2 * s_i + n) * 512: (2 * s_i + n + 1) * 512],
                            attn_fm[:, c * TS + s_i * 128: c * TS + s_i * 128 + 128],
                            w_oT[:, c * D + n * 512: c * D + (n + 1) * 512],
                            start=c == 0, stop=c == 7),
                            waits=attn_ev_marks + nmks if (c == 0 and s_i == 0 and n == 0) else (),
                            inc=sm_pe)
            y_evs = []
            x1ms = {}
            for s_i in range(2):
                e1 = P.op("vector", lambda v, s_i=s_i: v.tensor_scalar(
                    out=o_sb[:, s_i * D: s_i * D + 512], in0=BK(2 * s_i, 0, 512),
                    scalar1=1.0, scalar2=0.0, op0=ALU.mult, op1=ALU.add,
                    accum_out=st[:, 0:1]),
                    waits=[omm[(s_i, 0)]] + ([jprev["om"]] if (t and s_i == 0) else []),
                    inc=sm_dve)
                e2 = P.op("vector", lambda v, s_i=s_i: v.tensor_scalar(
                    out=o_sb[:, s_i * D + 512: (s_i + 1) * D], in0=BK(2 * s_i + 1, 0, 512),
                    scalar1=1.0, scalar2=0.0, op0=ALU.mult, op1=ALU.add,
                    accum_out=st[:, 4:5]), waits=[omm[(s_i, 1)]], inc=sm_dve)
                e3 = P.op("vector", lambda v: v.tensor_add(
                    out=st[:, 0:1], in0=st[:, 0:1], in1=st[:, 4:5]), waits=[e1, e2],
                    inc=sm_dve)
                sq = P.op("vector", lambda v, s_i=s_i: v.scalar_tensor_tensor(
                    out=jnk[:, 0:D], in0=o_sb[:, s_i * D:(s_i + 1) * D], scalar=1.0,
                    in1=o_sb[:, s_i * D:(s_i + 1) * D],
                    op0=ALU.bypass, op1=ALU.mult,
                    accum_out=st[:, 1:2]), waits=[e1, e2], inc=sm_dve)
                d3 = P.op("vector", lambda v: v.tensor_scalar(
                    out=st[:, 2:3], in0=st[:, 0:1], scalar1=1.0 / D, scalar2=0.0,
                    op0=ALU.mult, op1=ALU.add), waits=[e3], inc=sm_dve)
                d4 = P.op("vector", lambda v: v.scalar_tensor_tensor(
                    out=st[:, 3:4], in0=st[:, 2:3], scalar=1.0, in1=st[:, 2:3],
                    op0=ALU.mult, op1=ALU.mult), waits=[d3], inc=sm_dve)
                d5 = P.op("vector", lambda v: v.tensor_scalar(
                    out=st[:, 1:2], in0=st[:, 1:2], scalar1=1.0 / D, scalar2=st[:, 3:4],
                    op0=ALU.mult, op1=ALU.subtract), waits=[sq, d4], inc=sm_dve)
                d6 = P.op("scalar", lambda s: s.activation(
                    out=st[:, 1:2], in_=st[:, 1:2], func=AF.Ln, bias=C_EPS), waits=[d5],
                    inc=sm_act)
                d7 = P.op("scalar", lambda s: s.activation(
                    out=st[:, 0:1], in_=st[:, 1:2], func=AF.Exp, scale=-0.5), waits=[d6],
                    inc=sm_act)
                d8 = P.op("vector", lambda v: v.scalar_tensor_tensor(
                    out=st[:, 2:3], in0=st[:, 2:3], scalar=-1.0, in1=st[:, 0:1],
                    op0=ALU.mult, op1=ALU.mult), waits=[d7], inc=sm_dve)
                xm = P.op("vector", lambda v, s_i=s_i: v.tensor_scalar(
                    out=e_scr[:, s_i * D:(s_i + 1) * D], in0=xt[:, s_i * D:(s_i + 1) * D],
                    scalar1=1.0, scalar2=st[:, 2:3], op0=ALU.mult, op1=ALU.add),
                    waits=[d8] + ([jprev["x10"]] if (t and s_i == 0) else []), inc=sm_dve)
                x1m = P.op("vector", lambda v, s_i=s_i: v.scalar_tensor_tensor(
                    out=x1[:, s_i * D:(s_i + 1) * D], in0=o_sb[:, s_i * D:(s_i + 1) * D],
                    scalar=st[:, 0:1], in1=e_scr[:, s_i * D:(s_i + 1) * D],
                    op0=ALU.mult, op1=ALU.add), waits=[xm], inc=sm_dve)
                x1ms[s_i] = x1m
                sty = ln_stats(x1[:, s_i * D:(s_i + 1) * D], D, 8 + 4 * s_i, [x1m])
                y_evs.append(ln_apply(ytm[:, s_i * D:(s_i + 1) * D],
                                      x1[:, s_i * D:(s_i + 1) * D], 8 + 4 * s_i, [sty]))
            if stages == "oproj":
                om = P.dma("sync", out[t * TS: t * TS + 128, :], x1[:, 0:D],
                           waits=y_evs, inc=sm_out)
                out_marks.append(om)
                continue
            yfm_evs = []
            for s_i in range(2):
                blocks = [(ytm[:, s_i * D + c * 128: s_i * D + (c + 1) * 128], c * 128)
                          for c in range(8)]
                _, evy = idmm_batch(
                    blocks,
                    yfm.rearrange("p (c f) -> p c f", c=8)[:, :, s_i * 128:(s_i + 1) * 128],
                    PS[:, 0:1024].rearrange("p (c f) -> p c f", c=8),
                    [y_evs[s_i]] + ([omm[(1, 1)]] if s_i == 0 else yfm_evs[-1:]))
                yfm_evs.append(evy)
            yrc = P.op("gpsimd", lambda g, t=t: g.tensor_copy(
                out=yrows_all[:, t * 8:(t + 1) * 8],
                in_=yfm.rearrange("p (c f) -> p c f", c=8)[:, :, 255:256]),
                waits=yfm_evs, inc=sm_pl)
            y0c = P.op("gpsimd", lambda g, t=t: g.tensor_copy(
                out=y0_all[:, t * 8:(t + 1) * 8],
                in_=yfm.rearrange("p (c f) -> p c f", c=8)[:, :, 0:1]),
                inc=sm_pl)
            yrc_marks.append(y0c)
            x10 = P.dma("sync", x10_all[t:t + 1, :], x1[0:1, 0:D],
                        waits=[x1ms[0]], inc=sm_sp)

            dm = P.op("gpsimd", lambda g: g.tensor_tensor(
                out=dscr.rearrange("p (c f) -> p c f", c=8)[:, :, 1:256],
                in0=yfm.rearrange("p (c f) -> p c f", c=8)[:, :, 1:256],
                in1=yfm.rearrange("p (c f) -> p c f", c=8)[:, :, 0:255],
                op=ALU.subtract), waits=[yfm_evs[-1]], inc=sm_pl)
            mix_marks = []
            for c in range(8):
                m1 = P.op("vector", lambda g, c=c: g.tensor_scalar(
                    out=xr[:, c * TS:c * TS + 1], in0=yfm[:, c * TS:c * TS + 1],
                    scalar1=mu_rc[:, c:c + 1], scalar2=0.0,
                    op0=ALU.mult, op1=ALU.add), waits=[dm], inc=sm_dve)
                m2 = P.op("vector", lambda g, c=c: g.tensor_scalar(
                    out=xk[:, c * TS:c * TS + 1], in0=yfm[:, c * TS:c * TS + 1],
                    scalar1=mu_kc[:, c:c + 1], scalar2=0.0,
                    op0=ALU.mult, op1=ALU.add), inc=sm_dve)
                m3 = P.op("vector", lambda g, c=c: g.scalar_tensor_tensor(
                    out=xr[:, c * TS + 1:(c + 1) * TS],
                    in0=dscr[:, c * TS + 1:(c + 1) * TS],
                    scalar=mu_rc[:, c:c + 1],
                    in1=yfm[:, c * TS:(c + 1) * TS - 1],
                    op0=ALU.mult, op1=ALU.add), inc=sm_dve)
                m4 = P.op("vector", lambda g, c=c: g.scalar_tensor_tensor(
                    out=xk[:, c * TS + 1:(c + 1) * TS],
                    in0=dscr[:, c * TS + 1:(c + 1) * TS],
                    scalar=mu_kc[:, c:c + 1],
                    in1=yfm[:, c * TS:(c + 1) * TS - 1],
                    op0=ALU.mult, op1=ALU.add), inc=sm_dve)
                mix_marks += [m1, m2, m3, m4]
            mix_done = mix_marks[-1]
            rmm = {}
            kmm = {}
            for s_i in range(2):
                for n in range(2):
                    for c in range(8):
                        rmm[(s_i, n)] = P.op("tensor", lambda tt, s_i=s_i, n=n, c=c: tt.matmul(
                            PS[:, (2 * s_i + n) * 512: (2 * s_i + n + 1) * 512],
                            xr[:, c * TS + s_i * 128: c * TS + s_i * 128 + 128],
                            w_rT[:, c * D + n * 512: c * D + (n + 1) * 512],
                            start=c == 0, stop=c == 7),
                            waits=[mix_done]
                            if (s_i == 0 and n == 0 and c == 0) else (), inc=sm_pe)
            for s_i in range(2):
                for n in range(2):
                    for c in range(8):
                        kmm[(s_i, n)] = P.op("tensor", lambda tt, s_i=s_i, n=n, c=c: tt.matmul(
                            PS[:, 2048 + (2 * s_i + n) * 512: 2048 + (2 * s_i + n + 1) * 512],
                            xk[:, c * TS + s_i * 128: c * TS + s_i * 128 + 128],
                            w_kT[:, c * D + n * 512: c * D + (n + 1) * 512],
                            start=c == 0, stop=c == 7),
                            waits=[mix_done] if (s_i == 0 and n == 0 and c == 0) else (),
                            inc=sm_pe)
            r_marks = []
            for s_i in range(2):
                s1 = P.op("scalar", lambda s, s_i=s_i: s.activation(
                    out=e_scr[:, s_i * D:(s_i + 1) * D],
                    in_=PS[:, 2 * s_i * 512: (2 * s_i + 2) * 512],
                    func=AF.Exp, scale=-1.0), waits=[rmm[(s_i, 1)]], inc=sm_act)
                s2 = P.op("scalar", lambda s, s_i=s_i: s.activation(
                    out=e_scr[:, s_i * D:(s_i + 1) * D],
                    in_=e_scr[:, s_i * D:(s_i + 1) * D],
                    func=AF.Ln, bias=C_ONE), waits=[s1], inc=sm_act)
                s3 = P.op("scalar", lambda s, s_i=s_i: s.activation(
                    out=r_sb[:, s_i * D:(s_i + 1) * D],
                    in_=e_scr[:, s_i * D:(s_i + 1) * D],
                    func=AF.Exp, scale=-1.0), waits=[s2], inc=sm_act)
                r_marks.append(s3)
            ka_marks = []
            for s_i in range(2):
                for n in range(2):
                    rl = P.op("vector", lambda v, s_i=s_i, n=n: v.tensor_scalar_max(
                        out=kact[:, s_i * D + n * 512: s_i * D + (n + 1) * 512],
                        in0=PS[:, 2048 + (2 * s_i + n) * 512: 2048 + (2 * s_i + n + 1) * 512],
                        scalar1=0.0), waits=[kmm[(s_i, n)]], inc=sm_dve)
                    ka_marks.append(rl)
            sqk = P.op("gpsimd", lambda g: g.tensor_tensor(
                out=kact[:], in0=kact[:], in1=kact[:], op=ALU.mult),
                waits=ka_marks, inc=sm_pl)
            kfm_evs = []
            for s_i in range(2):
                blocks = [(kact[:, s_i * D + c * 128: s_i * D + (c + 1) * 128], c * 128)
                          for c in range(8)]
                _, evk2 = idmm_batch(
                    blocks,
                    kactfm.rearrange("p (c f) -> p c f", c=8)[:, :, s_i * 128:(s_i + 1) * 128],
                    PS[:, 0:1024].rearrange("p (c f) -> p c f", c=8),
                    [sqk] + ([r_marks[1]] if s_i == 0 else kfm_evs[-1:]))
                kfm_evs.append(evk2)
            vmm2 = {}
            for s_i in range(2):
                for n in range(2):
                    for c in range(8):
                        vmm2[(s_i, n)] = P.op("tensor", lambda tt, s_i=s_i, n=n, c=c: tt.matmul(
                            PS[:, 2048 + (2 * s_i + n) * 512: 2048 + (2 * s_i + n + 1) * 512],
                            kactfm[:, c * TS + s_i * 128: c * TS + s_i * 128 + 128],
                            w_vT[:, c * D + n * 512: c * D + (n + 1) * 512],
                            start=c == 0, stop=c == 7),
                            waits=kfm_evs + [ka_marks[-1]]
                            if (s_i == 0 and n == 0 and c == 0) else (), inc=sm_pe)
            f2_last = None
            for s_i in range(2):
                f1 = P.op("vector", lambda v, s_i=s_i: v.scalar_tensor_tensor(
                    out=o_sb[:, s_i * D: s_i * D + 512],
                    in0=PS[:, 2048 + 2 * s_i * 512: 2048 + (2 * s_i + 1) * 512],
                    scalar=1.0, in1=r_sb[:, s_i * D: s_i * D + 512],
                    op0=ALU.mult, op1=ALU.mult, accum_out=st[:, 0:1]),
                    waits=[vmm2[(s_i, 0)], r_marks[s_i]], inc=sm_dve)
                f2 = P.op("vector", lambda v, s_i=s_i: v.scalar_tensor_tensor(
                    out=o_sb[:, s_i * D + 512: (s_i + 1) * D],
                    in0=PS[:, 2048 + (2 * s_i + 1) * 512: 2048 + (2 * s_i + 2) * 512],
                    scalar=1.0, in1=r_sb[:, s_i * D + 512: (s_i + 1) * D],
                    op0=ALU.mult, op1=ALU.mult, accum_out=st[:, 4:5]),
                    waits=[vmm2[(s_i, 1)]], inc=sm_dve)
                f2_last = f2
                e3 = P.op("vector", lambda v: v.tensor_add(
                    out=st[:, 0:1], in0=st[:, 0:1], in1=st[:, 4:5]), waits=[f1, f2],
                    inc=sm_dve)
                sq = P.op("vector", lambda v, s_i=s_i: v.scalar_tensor_tensor(
                    out=jnk[:, 0:D], in0=o_sb[:, s_i * D:(s_i + 1) * D], scalar=1.0,
                    in1=o_sb[:, s_i * D:(s_i + 1) * D],
                    op0=ALU.bypass, op1=ALU.mult,
                    accum_out=st[:, 1:2]), waits=[f1, f2], inc=sm_dve)
                d3 = P.op("vector", lambda v: v.tensor_scalar(
                    out=st[:, 2:3], in0=st[:, 0:1], scalar1=1.0 / D, scalar2=0.0,
                    op0=ALU.mult, op1=ALU.add), waits=[e3], inc=sm_dve)
                d4 = P.op("vector", lambda v: v.scalar_tensor_tensor(
                    out=st[:, 3:4], in0=st[:, 2:3], scalar=1.0, in1=st[:, 2:3],
                    op0=ALU.mult, op1=ALU.mult), waits=[d3], inc=sm_dve)
                d5 = P.op("vector", lambda v: v.tensor_scalar(
                    out=st[:, 1:2], in0=st[:, 1:2], scalar1=1.0 / D, scalar2=st[:, 3:4],
                    op0=ALU.mult, op1=ALU.subtract), waits=[sq, d4], inc=sm_dve)
                d6 = P.op("scalar", lambda s: s.activation(
                    out=st[:, 1:2], in_=st[:, 1:2], func=AF.Ln, bias=C_EPS), waits=[d5],
                    inc=sm_act)
                d7 = P.op("scalar", lambda s: s.activation(
                    out=st[:, 0:1], in_=st[:, 1:2], func=AF.Exp, scale=-0.5), waits=[d6],
                    inc=sm_act)
                d8 = P.op("vector", lambda v: v.scalar_tensor_tensor(
                    out=st[:, 2:3], in0=st[:, 2:3], scalar=-1.0, in1=st[:, 0:1],
                    op0=ALU.mult, op1=ALU.mult), waits=[d7], inc=sm_dve)
                xmF = P.op("vector", lambda v, s_i=s_i: v.tensor_scalar(
                    out=e_scr[:, s_i * D:(s_i + 1) * D], in0=x1[:, s_i * D:(s_i + 1) * D],
                    scalar1=1.0, scalar2=st[:, 2:3], op0=ALU.mult, op1=ALU.add),
                    waits=[d8], inc=sm_dve)
                x2m = P.op("vector", lambda v, s_i=s_i: v.scalar_tensor_tensor(
                    out=o_sb[:, s_i * D:(s_i + 1) * D], in0=o_sb[:, s_i * D:(s_i + 1) * D],
                    scalar=st[:, 0:1], in1=e_scr[:, s_i * D:(s_i + 1) * D],
                    op0=ALU.mult, op1=ALU.add), waits=[xmF], inc=sm_dve)
                om = P.dma("sync", out[t * TS + s_i * 128: t * TS + (s_i + 1) * 128, :],
                           o_sb[:, s_i * D:(s_i + 1) * D], waits=[x2m], inc=sm_out)
                out_marks.append(om)
            jprev = {"x1m": x1ms[1], "ffn03": r_marks[-1], "ffn47": f2_last,
                     "om": out_marks[-1], "mixd": mix_done, "x10": x10}

        srow_sp = P.dma("sync", yrow_src[:], yrows_all[:],
                        waits=[yrc_marks[-1]], inc=sm_sp)
        cl = P.op("gpsimd", lambda g: g.collective_compute(
            "AllGather", ALU.bypass,
            replica_groups=[[0, 1], [2, 3], [4, 5], [6, 7]],
            ins=[yrow_src.ap().opt()], outs=[yrow_gath.ap().opt()]),
            waits=[(sm_sp, sm_sp.n)], inc=sm_cl)
        ldA = P.dma("sync", carry_A[:], yrow_gath[0], waits=[cl], inc=sm_in)
        ldB = P.dma("sync", carry_B[:], yrow_gath[1], inc=sm_in, chain=False)
        gcast = P.op("vector", lambda v: v.tensor_copy(
            out=scb[0:1, 1:3], in_=cfg_sb[0:1, 2:4]), waits=[consts_ld], inc=sm_dve)
        gmm = P.op("tensor", lambda tt: tt.matmul(
            PS[:, 3328:3330], onesb[0:1, :], scb[0:1, 1:3], start=True, stop=True),
            waits=[gcast, cl, jprev["ffn47"]], inc=sm_pe)
        ga = P.op("vector", lambda v: v.tensor_copy(
            out=gA_col[:], in_=PS[:, 3328:3329]), waits=[gmm], inc=sm_dve)
        gb = P.op("vector", lambda v: v.tensor_copy(
            out=gB_col[:], in_=PS[:, 3329:3330]), waits=[gmm], inc=sm_dve)
        c1 = P.op("vector", lambda v: v.tensor_scalar(
            out=carry_all[:], in0=carry_A[:],
            scalar1=gA_col[:, 0:1], scalar2=0.0, op0=ALU.mult, op1=ALU.add),
            waits=[ga, ldB], inc=sm_dve)
        cprep = P.op("vector", lambda v: v.scalar_tensor_tensor(
            out=carry_all[:, 8:64], in0=carry_B[:, 0:56], scalar=gB_col[:, 0:1],
            in1=carry_all[:, 8:64], op0=ALU.mult, op1=ALU.add),
            waits=[gb, c1], inc=sm_dve)

        fm = None
        for c in range(8):
            vY = y0_all.rearrange("p (t c) -> p c t", c=8)[:, c, :]
            vC = carry_all.rearrange("p (t c) -> p c t", c=8)[:, c, :]
            vR = carry_A.rearrange("p (t c) -> p c t", c=8)[:, c, :]
            vK = carry_B.rearrange("p (t c) -> p c t", c=8)[:, c, :]
            dd = P.op("gpsimd", lambda g, vY=vY, vC=vC, vR=vR: g.tensor_tensor(
                out=vR, in0=vY, in1=vC, op=ALU.subtract),
                waits=[cprep] if c == 0 else (), inc=sm_pl)
            k1 = P.op("vector", lambda v, vR=vR, vK=vK, vC=vC, c=c: v.scalar_tensor_tensor(
                out=vK, in0=vR, scalar=mu_kc[:, c:c + 1], in1=vC,
                op0=ALU.mult, op1=ALU.add), waits=[dd], inc=sm_dve)
            fm = P.op("vector", lambda v, vR=vR, vC=vC, c=c: v.scalar_tensor_tensor(
                out=vR, in0=vR, scalar=mu_rc[:, c:c + 1], in1=vC,
                op0=ALU.mult, op1=ALU.add), waits=[k1], inc=sm_dve)
        rm0 = {}
        km0 = {}
        for c in range(8):
            lhR = carry_A.rearrange("p (t c) -> p c t", c=8)[:, c, :]
            lhK = carry_B.rearrange("p (t c) -> p c t", c=8)[:, c, :]
            for n in range(2):
                rm0[n] = P.op("tensor", lambda tt, lhR=lhR, n=n, c=c: tt.matmul(
                    PS[0:8, n * 512:(n + 1) * 512], lhR,
                    w_rT[:, c * D + n * 512: c * D + (n + 1) * 512],
                    start=c == 0, stop=c == 7),
                    waits=[fm, jprev["ffn03"], jprev["om"]]
                    if (c == 0 and n == 0) else (), inc=sm_pe)
                km0[n] = P.op("tensor", lambda tt, lhK=lhK, n=n, c=c: tt.matmul(
                    PS[0:8, 1024 + n * 512: 1024 + (n + 1) * 512], lhK,
                    w_kT[:, c * D + n * 512: c * D + (n + 1) * 512],
                    start=c == 0, stop=c == 7), inc=sm_pe)
        s1f = P.op("scalar", lambda s: s.activation(
            out=e_scr[0:8, 0:D], in_=PS[0:8, 0:D], func=AF.Exp, scale=-1.0),
            waits=[rm0[1]], inc=sm_act)
        s2f = P.op("scalar", lambda s: s.activation(
            out=e_scr[0:8, 0:D], in_=e_scr[0:8, 0:D], func=AF.Ln,
            bias=cvals[0:8, 1:2]), waits=[s1f], inc=sm_act)
        s3f = P.op("scalar", lambda s: s.activation(
            out=e_scr[0:8, 0:D], in_=e_scr[0:8, 0:D], func=AF.Exp, scale=-1.0),
            waits=[s2f], inc=sm_act)
        rl0 = []
        for n in range(2):
            rl0.append(P.op("vector", lambda v, n=n: v.tensor_scalar_max(
                out=kact[0:8, n * 512:(n + 1) * 512],
                in0=PS[0:8, 1024 + n * 512: 1024 + (n + 1) * 512],
                scalar1=0.0), waits=[km0[1]] if n == 0 else (), inc=sm_dve))
        sq0 = P.op("gpsimd", lambda g: g.tensor_tensor(
            out=kact[0:8, 0:D], in0=kact[0:8, 0:D], in1=kact[0:8, 0:D],
            op=ALU.mult), waits=rl0, inc=sm_pl)
        tr0 = None
        for c in range(8):
            tr0 = P.op("tensor", lambda tt, c=c: tt.matmul(
                PS[:, 1536 + c * 8: 1544 + c * 8],
                kact[0:8, c * 128:(c + 1) * 128], ident[0:8, 0:8],
                start=True, stop=True),
                waits=[sq0, rl0[1]] if c == 0 else (), inc=sm_pe)
        ev0 = P.op("scalar", lambda s: s.activation(
            out=yrows_all[:], in_=PS[:, 1536:1600], func=AF.Copy),
            waits=[tr0], inc=sm_act)
        vm0 = {}
        for c in range(8):
            for n in range(2):
                vm0[n] = P.op("tensor", lambda tt, n=n, c=c: tt.matmul(
                    PS[0:8, 2048 + n * 512: 2048 + (n + 1) * 512],
                    yrows_all[:, c * 8:(c + 1) * 8],
                    w_vT[:, c * D + n * 512: c * D + (n + 1) * 512],
                    start=c == 0, stop=c == 7),
                    waits=[ev0] if (c == 0 and n == 0) else (), inc=sm_pe)
        f0a = P.op("vector", lambda v: v.scalar_tensor_tensor(
            out=e_scr[0:8, D:D + 512], in0=PS[0:8, 2048:2560], scalar=1.0,
            in1=e_scr[0:8, 0:512], op0=ALU.mult, op1=ALU.mult,
            accum_out=st[0:8, 0:1]), waits=[vm0[0], s3f], inc=sm_dve)
        f0b = P.op("vector", lambda v: v.scalar_tensor_tensor(
            out=e_scr[0:8, D + 512:2 * D], in0=PS[0:8, 2560:3072], scalar=1.0,
            in1=e_scr[0:8, 512:D], op0=ALU.mult, op1=ALU.mult,
            accum_out=st[0:8, 4:5]), waits=[vm0[1]], inc=sm_dve)
        e3f = P.op("vector", lambda v: v.tensor_add(
            out=st[0:8, 0:1], in0=st[0:8, 0:1], in1=st[0:8, 4:5]),
            waits=[f0a, f0b], inc=sm_dve)
        sqf = P.op("vector", lambda v: v.scalar_tensor_tensor(
            out=jnk[0:8, 0:D], in0=e_scr[0:8, D:2 * D], scalar=1.0,
            in1=e_scr[0:8, D:2 * D], op0=ALU.bypass, op1=ALU.mult,
            accum_out=st[0:8, 1:2]), waits=[f0a, f0b], inc=sm_dve)
        d3f = P.op("vector", lambda v: v.tensor_scalar(
            out=st[0:8, 2:3], in0=st[0:8, 0:1], scalar1=1.0 / D, scalar2=0.0,
            op0=ALU.mult, op1=ALU.add), waits=[e3f], inc=sm_dve)
        d4f = P.op("vector", lambda v: v.scalar_tensor_tensor(
            out=st[0:8, 3:4], in0=st[0:8, 2:3], scalar=1.0, in1=st[0:8, 2:3],
            op0=ALU.mult, op1=ALU.mult), waits=[d3f], inc=sm_dve)
        d5f = P.op("vector", lambda v: v.tensor_scalar(
            out=st[0:8, 1:2], in0=st[0:8, 1:2], scalar1=1.0 / D, scalar2=st[0:8, 3:4],
            op0=ALU.mult, op1=ALU.subtract), waits=[sqf, d4f], inc=sm_dve)
        d6f = P.op("scalar", lambda s: s.activation(
            out=st[0:8, 1:2], in_=st[0:8, 1:2], func=AF.Ln, bias=cvals[0:8, 0:1]),
            waits=[d5f], inc=sm_act)
        d7f = P.op("scalar", lambda s: s.activation(
            out=st[0:8, 0:1], in_=st[0:8, 1:2], func=AF.Exp, scale=-0.5),
            waits=[d6f], inc=sm_act)
        d8f = P.op("vector", lambda v: v.scalar_tensor_tensor(
            out=st[0:8, 2:3], in0=st[0:8, 2:3], scalar=-1.0, in1=st[0:8, 0:1],
            op0=ALU.mult, op1=ALU.mult), waits=[d7f], inc=sm_dve)
        xmf = P.op("vector", lambda v: v.tensor_scalar(
            out=x10_all[:], in0=x10_all[:], scalar1=1.0, scalar2=st[0:8, 2:3],
            op0=ALU.mult, op1=ALU.add), waits=[d8f], inc=sm_dve)
        x2f = P.op("vector", lambda v: v.scalar_tensor_tensor(
            out=x10_all[:], in0=e_scr[0:8, D:2 * D], scalar=st[0:8, 0:1],
            in1=x10_all[:], op0=ALU.mult, op1=ALU.add), waits=[xmf], inc=sm_dve)
        fo = None
        for j in range(nt):
            fo = P.dma("sync", out[j * TS:j * TS + 1, :], x10_all[j:j + 1, :],
                       waits=[x2f] if j == 0 else (), inc=sm_out)
        out_marks.append(fo)

        P.op("gpsimd", lambda g: g.nop(), waits=[out_marks[-1]], inc=sm_pl)

        with nc.Block() as block:
            @block.sync
            def _(sync):
                P.emit("sync", sync)

            @block.tensor
            def _(tensor):
                P.emit("tensor", tensor)

            @block.vector
            def _(vector):
                P.emit("vector", vector)

            @block.scalar
            def _(scalar):
                P.emit("scalar", scalar)

            @block.gpsimd
            def _(gpsimd):
                P.emit("gpsimd", gpsimd)

    return nc


_NC_CACHE = {}


def kernel(L0_attn_x, L0_attn_W_dq, L0_attn_W_uq, L0_attn_q_ln_g, L0_attn_q_ln_b,
           L0_attn_W_dkv, L0_attn_W_ukv, L0_attn_kv_ln_g, L0_attn_kv_ln_b,
           L0_attn_W_o, L0_ffn_W_r, L0_ffn_W_k, L0_ffn_W_v, L0_ffn_mu_r, L0_ffn_mu_k):
    if "nc" not in _NC_CACHE:
        _NC_CACHE["nc"] = build()
    nc = _NC_CACHE["nc"]
    x = np.ascontiguousarray(np.asarray(L0_attn_x), dtype=np.float32)
    shared = {
        "W_dq": np.ascontiguousarray(L0_attn_W_dq, np.float32),
        "W_uq": np.ascontiguousarray(L0_attn_W_uq, np.float32),
        "q_g": np.ascontiguousarray(L0_attn_q_ln_g, np.float32),
        "q_b": np.ascontiguousarray(L0_attn_q_ln_b, np.float32),
        "W_dkv": np.ascontiguousarray(L0_attn_W_dkv, np.float32),
        "W_ukv": np.ascontiguousarray(L0_attn_W_ukv, np.float32),
        "kv_g": np.ascontiguousarray(L0_attn_kv_ln_g, np.float32),
        "kv_b": np.ascontiguousarray(L0_attn_kv_ln_b, np.float32),
        "W_o": np.ascontiguousarray(L0_attn_W_o, np.float32),
        "W_r": np.ascontiguousarray(L0_ffn_W_r, np.float32),
        "W_k": np.ascontiguousarray(L0_ffn_W_k, np.float32),
        "W_v": np.ascontiguousarray(L0_ffn_W_v, np.float32),
        "mu_r": np.ascontiguousarray(L0_ffn_mu_r, np.float32),
        "mu_k": np.ascontiguousarray(L0_ffn_mu_k, np.float32),
    }
    in_maps = []
    for c in range(8):
        b, r = c // 2, c % 2
        m = dict(shared)
        m["x"] = np.ascontiguousarray(x[b])
        m["xq"] = np.ascontiguousarray(
            x[b].reshape(NT, TS, D)[r::2].reshape(8 * TS, D))
        m["maskin"] = host_masks(r)
        m["cfgi"] = np.zeros((1, 2), np.int32)
        m["cfgf"] = (np.array([[0.0, 1.0]], np.float32) if r == 0
                     else np.array([[1.0, 0.0]], np.float32))
        in_maps.append(m)
    res = run_bass_kernel_spmd(nc, in_maps, core_ids=list(range(8)))
    outp = np.empty((4, NT, TS, D), dtype=np.float32)
    for b in range(4):
        for r in range(2):
            outp[b, r::2] = res.results[2 * b + r]["out"].reshape(8, TS, D)
    return outp.reshape(4, S, D)


# version 22
# speedup vs baseline: 1.3097x; 1.0333x over previous
import contextlib

import numpy as np

import concourse.bass as bass
import concourse.mybir as mybir
from concourse.bass_utils import run_bass_kernel_spmd

F32 = mybir.dt.float32
BF16 = mybir.dt.bfloat16
AF = mybir.ActivationFunctionType
ALU = mybir.AluOpType
AX = mybir.AxisListType

D = 1024
S = 4096
NT = 16
TS = 256
NKT = 32
NH = 16
DH = 64
LAT = 256
EPS = 1e-5
INV_SQRT_DH = 0.125


class Sem:
    def __init__(self, handle):
        self.h = handle
        self.n = 0


class Planner:

    ENGS = ("sync", "tensor", "vector", "scalar", "gpsimd")

    def __init__(self):
        self.recs = {e: [] for e in self.ENGS}

    SERIAL = ("vector", "scalar", "gpsimd")

    def __init_serial(self):
        if not hasattr(self, "_own"):
            self._own = {}

    def set_own_sems(self, mapping):
        self._own = dict(mapping)

    def op(self, eng, fn, waits=(), inc=None, amt=1):
        w = [x for x in waits if x is not None]
        own = getattr(self, "_own", {}).get(eng)
        if eng in self.SERIAL and own is not None:
            w = [(own, own.n)] + w
            if inc is None:
                inc = own
        if inc is not None:
            inc.n += amt
        self.recs[eng].append((fn, w, inc, amt))
        return (inc, inc.n) if inc is not None else None

    def dma(self, eng, out, in_, waits=(), inc=None, chain=True):
        w = list(waits)
        if inc is not None and chain:
            w.append((inc, inc.n))
        return self.op(eng, lambda e: e.dma_start(out=out, in_=in_),
                       waits=w, inc=inc, amt=16)

    def emit(self, eng_name, eng):
        last_wait = {}
        dedup = eng_name != "sync"
        for fn, waits, inc, amt in self.recs[eng_name]:
            for sem, val in waits:
                if dedup and last_wait.get(id(sem), -1) >= val:
                    continue
                eng.wait_ge(sem.h, val)
                last_wait[id(sem)] = val
            inst = fn(eng)
            if inc is not None:
                inst.then_inc(inc.h, amt)


def _host_consts():
    import ml_dtypes
    bf = ml_dtypes.bfloat16
    ident = np.eye(128, dtype=np.float32).astype(bf)
    ones = np.ones((128, 128), dtype=np.float32).astype(bf)
    return ident, ones


def host_masks(rank):
    import ml_dtypes
    bf = ml_dtypes.bfloat16
    k = np.arange(128)[:, None]
    q = np.arange(TS)[None, :]
    mlow = (q >= k).astype(np.float32)
    mhigh = (q >= 128 + k).astype(np.float32)
    ones = np.ones((128, TS), np.float32)
    zero = np.zeros((128, TS), np.float32)
    stack = [ones, ones, mlow, mhigh] if rank == 1 else [mlow, mhigh, zero, zero]
    return np.stack(stack).astype(bf)


def build(nt=8, nkt=NKT, stages="all"):
    nc = bass.Bass("TRN2", target_bir_lowering=False, debug=False)

    x_in = nc.dram_tensor("x", [S, D], F32, kind="ExternalInput").ap()
    xq_in = nc.dram_tensor("xq", [8 * TS, D], F32, kind="ExternalInput").ap()
    maskin = nc.dram_tensor("maskin", [4, 128, TS], BF16, kind="ExternalInput").ap()
    cfgi_in = nc.dram_tensor("cfgi", [1, 2], mybir.dt.int32, kind="ExternalInput").ap()
    cfgf_in = nc.dram_tensor("cfgf", [1, 2], F32, kind="ExternalInput").ap()
    w_dq_in = nc.dram_tensor("W_dq", [D, LAT], F32, kind="ExternalInput").ap()
    w_uq_in = nc.dram_tensor("W_uq", [LAT, D], F32, kind="ExternalInput").ap()
    qg_in = nc.dram_tensor("q_g", [LAT], F32, kind="ExternalInput").ap()
    qb_in = nc.dram_tensor("q_b", [LAT], F32, kind="ExternalInput").ap()
    w_dkv_in = nc.dram_tensor("W_dkv", [D, LAT], F32, kind="ExternalInput").ap()
    w_ukv_in = nc.dram_tensor("W_ukv", [LAT, 2 * D], F32, kind="ExternalInput").ap()
    kg_in = nc.dram_tensor("kv_g", [LAT], F32, kind="ExternalInput").ap()
    kb_in = nc.dram_tensor("kv_b", [LAT], F32, kind="ExternalInput").ap()
    w_o_in = nc.dram_tensor("W_o", [D, D], F32, kind="ExternalInput").ap()
    w_r_in = nc.dram_tensor("W_r", [D, D], F32, kind="ExternalInput").ap()
    w_k_in = nc.dram_tensor("W_k", [D, D], F32, kind="ExternalInput").ap()
    w_v_in = nc.dram_tensor("W_v", [D, D], F32, kind="ExternalInput").ap()
    mu_r_in = nc.dram_tensor("mu_r", [D], F32, kind="ExternalInput").ap()
    mu_k_in = nc.dram_tensor("mu_k", [D], F32, kind="ExternalInput").ap()
    out = nc.dram_tensor("out", [8 * TS, D], F32, kind="ExternalOutput").ap()

    ident_np, ones_np = _host_consts()
    ident_d = nc.inline_tensor(ident_np, "ident_c")
    ones_d = nc.inline_tensor(ones_np, "ones_c")

    k_dram = nc.dram_tensor("k_scr", [NKT, 128, 1024], BF16)
    v_dram = nc.dram_tensor("v_scr", [NKT, 128, 1040], BF16)
    yrow_src = nc.dram_tensor("yrow_src", [128, 64], BF16)
    yrow_gath = nc.dram_tensor("yrow_gath", [2, 128, 64], BF16)

    taps = {}

    with contextlib.ExitStack() as cx:
        sb = lambda name, shape, dt: cx.enter_context(nc.sbuf_tensor(name, shape, dt))
        w_dq = sb("w_dq", [128, 8 * LAT], BF16)
        w_uq = sb("w_uq", [128, 2 * D], BF16)
        w_dkv = sb("w_dkv", [128, 8 * LAT], BF16)
        w_ukv = sb("w_ukv", [128, 2 * 2 * D], BF16)
        w_oT = sb("w_oT", [128, 8 * D], BF16)
        w_rT = sb("w_rT", [128, 8 * D], BF16)
        w_kT = sb("w_kT", [128, 8 * D], BF16)
        w_vT = sb("w_vT", [128, 8 * D], BF16)
        mu_rc = sb("mu_rc", [128, 8], F32)
        mu_kc = sb("mu_kc", [128, 8], F32)
        gbq = sb("gbq", [128, LAT], F32)
        bbq = sb("bbq", [128, LAT], F32)
        gbk = sb("gbk", [128, LAT], F32)
        bbk = sb("bbk", [128, LAT], F32)
        ident = sb("ident", [128, 128], BF16)
        onesb = sb("onesb", [128, 128], BF16)
        mask_sb = sb("mask_sb", [128, 4 * TS], BF16)
        carry_all = sb("carry_all", [128, 64], BF16)
        yrows_all = sb("yrows_all", [128, 64], BF16)
        y0_all = sb("y0_all", [128, 64], BF16)
        x10_all = sb("x10_all", [8, D], F32)
        carry_A = sb("carry_A", [128, 64], BF16)
        carry_B = sb("carry_B", [128, 64], BF16)
        gA_col = sb("gA_col", [128, 1], F32)
        gB_col = sb("gB_col", [128, 1], F32)
        cfg_sb = sb("cfg_sb", [1, 4], F32)
        xt = sb("xt", [128, 2 * D], F32)
        wload = xt
        xln = sb("xln", [128, 2 * D], BF16)
        xfm = sb("xfm", [128, 8 * TS], BF16)
        cq_sb = sb("cq_sb", [128, 2 * LAT], F32)
        scr_s = sb("scr_s", [128, 2 * LAT], F32)
        cqn = sb("cqn", [128, 2 * LAT], BF16)
        cqfm = sb("cqfm", [128, 2 * LAT], BF16)
        qfm = sb("qfm", [128, 8 * TS], BF16)
        kstage = sb("kstage", [128, 2 * D], BF16)
        vstage = sb("vstage", [128, 2 * NH * 65], BF16)
        kact = sb("kact", [128, 2 * D], BF16)
        k2s = kact[:, 0:D]
        kbuf = sb("kbuf", [128, 6 * 256], BF16)
        vbuf = sb("vbuf", [128, 6 * 260], BF16)
        pbuf = sb("pbuf", [128, 4 * 1024], BF16)
        aev = sb("aev", [128, 4 * 256], BF16)
        atok = sb("atok", [128, 8 * 64], BF16)
        rc = sb("rc", [128, 8], F32)
        attn_fm = sb("attn_fm", [128, 8 * TS], BF16)
        x1 = sb("x1", [128, 2 * D], F32)
        ytm = xln
        yfm = sb("yfm", [128, 8 * TS], BF16)
        dscr = sb("dscr", [128, 8 * TS], BF16)
        q2 = dscr
        xr = sb("xr", [128, 8 * TS], BF16)
        xk = sb("xk", [128, 8 * TS], BF16)
        r_sb = sb("r_sb", [128, 2 * D], F32)
        e_scr = sb("e_scr", [128, 2 * D], F32)
        kactfm = sb("kactfm", [128, 8 * TS], BF16)
        o_sb = sb("o_sb", [128, 2 * D], F32)
        x2 = xt
        vrow = xt[0:1, 0:4 * LAT]
        vrow_b = pbuf[0:1, 0:8 * LAT]
        jnk = dscr[:, 0:D]
        st = sb("st", [128, 16], F32)
        sc = sb("sc", [1, 16], F32)
        scb = sb("scb", [1, 16], BF16)
        bias_b = sb("bias_b", [128, 1], F32)
        cvals = sb("cvals", [128, 4], F32)

        PS = cx.enter_context(nc.psum_tensor("PS", [128, 4096], F32))

        sm_in = Sem(cx.enter_context(nc.semaphore("sm_in")))
        sm_sp = Sem(cx.enter_context(nc.semaphore("sm_sp")))
        sm_out = Sem(cx.enter_context(nc.semaphore("sm_out")))
        sm_pe = Sem(cx.enter_context(nc.semaphore("sm_pe")))
        sm_act = Sem(cx.enter_context(nc.semaphore("sm_act")))
        sm_dve = Sem(cx.enter_context(nc.semaphore("sm_dve")))
        sm_pl = Sem(cx.enter_context(nc.semaphore("sm_pl")))
        sm_kvs = [Sem(cx.enter_context(nc.semaphore(f"sm_kv{i}"))) for i in range(6)]
        sm_spl = [Sem(cx.enter_context(nc.semaphore(f"sm_spl{i}"))) for i in range(4)]
        sm_cl = Sem(cx.enter_context(nc.semaphore("sm_cl")))

        P = Planner()
        P.set_own_sems({"vector": sm_dve, "scalar": sm_act, "gpsimd": sm_pl})
        BK = lambda b, off=0, n=512: PS[:, b * 512 + off: b * 512 + off + n]

        P.op("vector", lambda v: v.memset(cvals[:, 0:1], EPS), inc=sm_dve)
        P.op("vector", lambda v: v.memset(cvals[:, 1:2], 1.0), inc=sm_dve)
        P.op("vector", lambda v: v.memset(cvals[:, 2:3], 1e-12), inc=sm_dve)
        cv_mark = (sm_dve, sm_dve.n)
        C_EPS = cvals[:, 0:1]
        C_ONE = cvals[:, 1:2]
        C_TINY = cvals[:, 2:3]

        P.dma("sync", ident[:], ident_d.ap(), inc=sm_in)
        P.dma("sync", onesb[:], ones_d.ap(), inc=sm_in)
        for _s in range(4):
            P.dma("sync", mask_sb[:, _s * TS:(_s + 1) * TS], maskin[_s], inc=sm_in)
        P.dma("sync", cfg_sb[0:1, 0:2].bitcast(mybir.dt.int32), cfgi_in, inc=sm_in)
        P.dma("sync", cfg_sb[0:1, 2:4], cfgf_in, inc=sm_in)
        for c in range(8):
            P.dma("sync", mu_rc[:, c:c + 1],
                  mu_r_in[c * 128:(c + 1) * 128].rearrange("(p o) -> p o", o=1), inc=sm_in)
            P.dma("sync", mu_kc[:, c:c + 1],
                  mu_k_in[c * 128:(c + 1) * 128].rearrange("(p o) -> p o", o=1), inc=sm_in)
        for i, src_ap in enumerate((qg_in, qb_in, kg_in, kb_in)):
            P.dma("sync", vrow[0:1, i * LAT:(i + 1) * LAT],
                  src_ap.rearrange("(o l) -> o l", o=1), inc=sm_in)
        consts_ld = (sm_in, sm_in.n)

        P.op("vector", lambda v: v.tensor_copy(out=vrow_b[0:1, 0:4 * LAT], in_=vrow[:]),
             waits=[consts_ld], inc=sm_dve)
        P.op("vector", lambda v: v.tensor_tensor(out=vrow_b[0:1, 4 * LAT:8 * LAT],
                                                 in0=vrow[:], in1=vrow_b[0:1, 0:4 * LAT],
                                                 op=ALU.subtract), inc=sm_dve)
        gb_prep = (sm_dve, sm_dve.n)
        for i, dst in enumerate((gbq, bbq, gbk, bbk)):
            ii = i
            P.op("tensor", lambda t, ii=ii: t.matmul(
                BK(0, 0, LAT), onesb[0:1, :], vrow_b[0:1, ii * LAT:(ii + 1) * LAT],
                start=True, stop=False), waits=[gb_prep, (sm_dve, sm_dve.n)])
            mk = P.op("tensor", lambda t, ii=ii: t.matmul(
                BK(0, 0, LAT), onesb[0:1, :], vrow_b[0:1, (4 + ii) * LAT:(5 + ii) * LAT],
                start=False, stop=True), inc=sm_pe)
            P.op("vector", lambda v, dst=dst: v.tensor_copy(out=dst[:], in_=BK(0, 0, LAT)),
                 waits=[mk], inc=sm_dve)
        gb_done = (sm_dve, sm_dve.n)

        def load_cast(dst, src_ap, cols, waits=()):
            ld = P.dma("sync", wload[:, 0:cols], src_ap, waits=list(waits) + [gb_done],
                       inc=sm_in)
            return P.op("vector", lambda v: v.tensor_copy(out=dst, in_=wload[:, 0:cols]),
                        waits=[ld], inc=sm_dve)

        prev = None
        for c in range(8):
            prev = load_cast(w_dq[:, c * LAT:(c + 1) * LAT],
                             w_dq_in[c * 128:(c + 1) * 128, :], LAT,
                             waits=[prev] if prev else ())
        for c in range(8):
            prev = load_cast(w_dkv[:, c * LAT:(c + 1) * LAT],
                             w_dkv_in[c * 128:(c + 1) * 128, :], LAT, waits=[prev])
        for l in range(2):
            prev = load_cast(w_uq[:, l * D:(l + 1) * D],
                             w_uq_in[l * 128:(l + 1) * 128, :], D, waits=[prev])
        for l in range(2):
            prev = load_cast(w_ukv[:, l * 2 * D:(l + 1) * 2 * D],
                             w_ukv_in[l * 128:(l + 1) * 128, :], 2 * D, waits=[prev])

        sm_w = Sem(cx.enter_context(nc.semaphore("sm_w")))
        small_done = prev
        stg32 = (e_scr[:, 0:D], e_scr[:, D:2 * D])

        def wprep_gen():
            W_LIST = ((w_o_in, w_oT), (w_r_in, w_rT), (w_k_in, w_kT), (w_v_in, w_vT))
            cast_mk = {}
            used_mk = {}
            evac_mk = {}
            ldw_mk = {}
            rnd = 0
            for gi in range(32):
                w, i = gi // 8, gi % 8
                w_nat, w_t = W_LIST[w]
                h = gi % 2
                ld = P.dma("sync", stg32[h],
                           w_nat[i * 128:(i + 1) * 128, :],
                           waits=[small_done] if gi < 2 else [used_mk[gi - 2]],
                           inc=sm_w)
                ldw_mk[gi] = ld
                yield
                cast_mk[gi] = P.op("vector", lambda v, h=h: v.tensor_copy(
                    out=kactfm[:, h * D:(h + 1) * D], in_=stg32[h]),
                    waits=[ld] + ([used_mk[gi - 2]] if gi >= 2 else []), inc=sm_dve)
                yield
                mks = []
                for c in range(8):
                    mks.append(P.op("tensor", lambda t, c=c, h=h: t.matmul(
                        PS[:, c * 128:(c + 1) * 128],
                        kactfm[:, h * D + c * 128: h * D + (c + 1) * 128],
                        ident[:], start=True, stop=True),
                        waits=[cast_mk[gi]] + ([evac_mk[gi - 1]] if gi >= 1 else [])
                        if c == 0 else [], inc=sm_pe))
                evac_mk[gi] = P.op("scalar", lambda s, w_t=w_t, i=i: s.activation(
                    out=w_t.rearrange("p (c f) -> p c f", c=8)[:, :, i * 128:(i + 1) * 128],
                    in_=PS[:, 0:1024].rearrange("p (c f) -> p c f", c=8),
                    func=AF.Copy), waits=[mks[-1], cv_mark], inc=sm_act)
                used_mk[gi] = evac_mk[gi]
                rnd += 1
                yield
        _wgen = wprep_gen()
        for _ in _wgen:
            pass
        prev = small_done
        w_done = small_done
        P.op("gpsimd", lambda g: g.memset(vstage[:], 1.0), inc=sm_pl)
        P.op("gpsimd", lambda g: g.memset(yrows_all[:], 0.0), inc=sm_pl)
        P.op("gpsimd", lambda g: g.memset(y0_all[:], 0.0), inc=sm_pl)
        P.op("gpsimd", lambda g: g.memset(x10_all[:], 0.0), inc=sm_pl)
        P.op("gpsimd", lambda g: g.memset(sc[:], 0.0), inc=sm_pl)
        p0_pool = (sm_pl, sm_pl.n)

        def ln_stats(src_ap, n_elems, slot, waits, src_is_psum=False):
            s_sum = st[:, slot:slot + 1]
            s_sq = st[:, slot + 1:slot + 2]
            if src_is_psum:
                raise NotImplementedError
            d1 = P.op("vector", lambda v: v.tensor_reduce(
                out=s_sum, in_=src_ap, axis=AX.X, op=ALU.add), waits=waits, inc=sm_dve)
            d2 = P.op("vector", lambda v: v.scalar_tensor_tensor(
                out=jnk[:, 0:n_elems], in0=src_ap, scalar=1.0, in1=src_ap,
                op0=ALU.bypass, op1=ALU.mult, accum_out=s_sq), waits=waits, inc=sm_dve)
            m = st[:, slot + 2:slot + 3]
            d3 = P.op("vector", lambda v: v.tensor_scalar(
                out=m, in0=s_sum, scalar1=1.0 / n_elems, scalar2=0.0,
                op0=ALU.mult, op1=ALU.add), waits=[d1], inc=sm_dve)
            d4 = P.op("vector", lambda v: v.scalar_tensor_tensor(
                out=st[:, slot + 3:slot + 4], in0=m, scalar=1.0, in1=m,
                op0=ALU.mult, op1=ALU.mult), waits=[d3], inc=sm_dve)
            d5 = P.op("vector", lambda v: v.tensor_scalar(
                out=s_sq, in0=s_sq, scalar1=1.0 / n_elems,
                scalar2=st[:, slot + 3:slot + 4],
                op0=ALU.mult, op1=ALU.subtract), waits=[d2, d4], inc=sm_dve)
            d6 = P.op("scalar", lambda s: s.activation(
                out=s_sq, in_=s_sq, func=AF.Ln, bias=C_EPS), waits=[d5], inc=sm_act)
            d7 = P.op("scalar", lambda s: s.activation(
                out=s_sum, in_=s_sq, func=AF.Exp, scale=-0.5), waits=[d6], inc=sm_act)
            d8 = P.op("vector", lambda v: v.scalar_tensor_tensor(
                out=m, in0=m, scalar=-1.0, in1=s_sum,
                op0=ALU.mult, op1=ALU.mult), waits=[d7], inc=sm_dve)
            return d8

        def ln_apply(dst_ap, src_ap, slot, waits, eng="vector"):
            return P.op(eng, lambda v: v.tensor_scalar(
                out=dst_ap, in0=src_ap, scalar1=st[:, slot:slot + 1],
                scalar2=st[:, slot + 2:slot + 3],
                op0=ALU.mult, op1=ALU.add), waits=waits,
                inc=sm_dve if eng == "vector" else sm_pl)

        def idmm_batch(blocks, evac_out, evac_in, waits, evac_waits=()):
            mks = []
            for lhsT_ap, po in blocks:
                mk = P.op("tensor", lambda t, a=lhsT_ap, po=po: t.matmul(
                    PS[:, po:po + 128], a, ident[:], start=True, stop=True),
                    waits=waits if not mks else (), inc=sm_pe)
                mks.append(mk)
            ev = P.op("scalar", lambda s: s.activation(
                out=evac_out, in_=evac_in, func=AF.Copy),
                waits=[mks[-1]] + list(evac_waits), inc=sm_act)
            return mks[-1], ev

        spill_marks = []
        k2_chain_mark = None
        pv = {"ap1": [None, None], "mmx": [None, None], "evx": None,
              "mkv": None, "evk": None, "evv": None, "evc": None,
              "vmm": [None, None], "sp1": [None, None], "sp2": [None, None],
              "kmm2": None, "rm": None, "ev2": [None, None]}
        ldxs = {}
        for k0 in range(2):
            ldxs[k0] = P.dma("sync", xt[:, (k0 % 2) * D:(k0 % 2 + 1) * D],
                             x_in[k0 * 128:(k0 + 1) * 128, :],
                             waits=[w_done, p0_pool], inc=sm_in)
        k2_pend = None
        for kt in range(nkt):
            s = kt % 2
            stt = ln_stats(xt[:, s * D:(s + 1) * D], D, 0, [ldxs[kt]])
            ap1 = ln_apply(xln[:, s * D:(s + 1) * D], xt[:, s * D:(s + 1) * D],
                           0, [stt, pv["mmx"][s]])
            pv["ap1"][s] = ap1
            if kt + 2 < nkt:
                ldxs[kt + 2] = P.dma("sync", xt[:, s * D:(s + 1) * D],
                                     x_in[(kt + 2) * 128:(kt + 3) * 128, :],
                                     waits=[ap1], inc=sm_in)
            if k2_pend is not None:
                k2_chain_mark = k2_pend()
                k2_pend = None
            blocks = [(xln[:, s * D + c * 128: s * D + (c + 1) * 128], c * 128)
                      for c in range(8)]
            mmx, evx = idmm_batch(
                blocks,
                xfm.rearrange("p (c f) -> p c f", c=8)[:, :, 0:128],
                PS[:, 0:1024].rearrange("p (c f) -> p c f", c=8),
                [ap1, pv["evx"]], evac_waits=[pv["mkv"]])
            pv["mmx"][s] = mmx
            pv["evx"] = evx
            mkv = None
            for c in range(8):
                mkv = P.op("tensor", lambda t, c=c, s=s: t.matmul(
                    PS[:, 1536 + s * 256: 1792 + s * 256],
                    xfm[:, c * TS:c * TS + 128],
                    w_dkv[:, c * LAT:(c + 1) * LAT],
                    start=c == 0, stop=c == 7),
                    waits=[evx, pv["ev2"][s]] if c == 0 else (), inc=sm_pe)
            pv["mkv"] = mkv
            ev2 = P.op("vector", lambda v, s=s: v.tensor_scalar(
                out=cq_sb[:, 0:LAT], in0=PS[:, 1536 + s * 256: 1792 + s * 256],
                scalar1=1.0, scalar2=0.0,
                op0=ALU.mult, op1=ALU.add, accum_out=st[:, 4:5]), waits=[mkv], inc=sm_dve)
            pv["ev2"][s] = ev2
            sq2 = P.op("vector", lambda v: v.scalar_tensor_tensor(
                out=jnk[:, 0:LAT], in0=cq_sb[:, 0:LAT], scalar=1.0, in1=cq_sb[:, 0:LAT],
                op0=ALU.bypass, op1=ALU.mult, accum_out=st[:, 5:6]), waits=[ev2], inc=sm_dve)
            d3 = P.op("vector", lambda v: v.tensor_scalar(
                out=st[:, 6:7], in0=st[:, 4:5], scalar1=1.0 / LAT, scalar2=0.0,
                op0=ALU.mult, op1=ALU.add), waits=[ev2], inc=sm_dve)
            d4 = P.op("vector", lambda v: v.scalar_tensor_tensor(
                out=st[:, 7:8], in0=st[:, 6:7], scalar=1.0, in1=st[:, 6:7],
                op0=ALU.mult, op1=ALU.mult), waits=[d3], inc=sm_dve)
            d5 = P.op("vector", lambda v: v.tensor_scalar(
                out=st[:, 5:6], in0=st[:, 5:6], scalar1=1.0 / LAT, scalar2=st[:, 7:8],
                op0=ALU.mult, op1=ALU.subtract), waits=[sq2, d4], inc=sm_dve)
            d6 = P.op("scalar", lambda s2: s2.activation(
                out=st[:, 5:6], in_=st[:, 5:6], func=AF.Ln, bias=C_EPS), waits=[d5], inc=sm_act)
            d7 = P.op("scalar", lambda s2: s2.activation(
                out=st[:, 4:5], in_=st[:, 5:6], func=AF.Exp, scale=-0.5), waits=[d6], inc=sm_act)
            d8 = P.op("vector", lambda v: v.scalar_tensor_tensor(
                out=st[:, 6:7], in0=st[:, 6:7], scalar=-1.0, in1=st[:, 4:5],
                op0=ALU.mult, op1=ALU.mult), waits=[d7], inc=sm_dve)
            a1 = P.op("vector", lambda v: v.tensor_scalar(
                out=scr_s[:, 0:LAT], in0=cq_sb[:, 0:LAT], scalar1=st[:, 4:5],
                scalar2=st[:, 6:7], op0=ALU.mult, op1=ALU.add), waits=[d8], inc=sm_dve)
            a2 = P.op("vector", lambda v: v.scalar_tensor_tensor(
                out=scr_s[:, LAT:2 * LAT], in0=scr_s[:, 0:LAT], scalar=1.0,
                in1=gbk[:], op0=ALU.mult, op1=ALU.mult), waits=[a1, gb_done], inc=sm_dve)
            a3 = P.op("vector", lambda v: v.tensor_tensor(
                out=cqn[:, 0:LAT], in0=scr_s[:, LAT:2 * LAT], in1=bbk[:],
                op=ALU.add), waits=[a2], inc=sm_dve)
            co = s * 256
            blocks2 = [(cqn[:, l * 128:(l + 1) * 128], 1024 + l * 128) for l in range(2)]
            _, evc = idmm_batch(
                blocks2, cqfm[:, co:co + 256].rearrange("p (l f) -> p l f", l=2),
                PS[:, 1024:1280].rearrange("p (l f) -> p l f", l=2),
                [a3, pv["evc"]], evac_waits=[pv["vmm"][s]])
            pv["evc"] = evc
            kmm = None
            for c in range(8):
                for l in range(2):
                    first = (c == 0 and l == 0) or (c == 4 and l == 0)
                    stop = (c == 3 and l == 1) or (c == 7 and l == 1)
                    kmm = P.op("tensor", lambda t, c=c, l=l, first=first, stop=stop,
                               co=co: t.matmul(
                        PS[:, 2048 + c * 128: 2048 + c * 128 + 128],
                        w_ukv[:, l * 2048 + c * 128: l * 2048 + c * 128 + 128],
                        cqfm[:, co + l * 128: co + (l + 1) * 128],
                        start=first, stop=stop),
                        waits=[evc, pv["evk"]] if (c == 0 and l == 0) else (), inc=sm_pe)
            evk = P.op("scalar", lambda s2, s=s: s2.activation(
                out=kstage[:, s * D:(s + 1) * D], in_=PS[:, 2048:3072], func=AF.Copy),
                waits=[kmm, pv["sp1"][s]], inc=sm_act)
            pv["evk"] = evk
            vmm = None
            for n in range(2):
                for l in range(2):
                    vmm = P.op("tensor", lambda t, n=n, l=l, co=co: t.matmul(
                        PS[:, 3072 + n * 512: 3072 + (n + 1) * 512],
                        cqfm[:, co + l * 128: co + (l + 1) * 128],
                        w_ukv[:, l * 2048 + D + n * 512: l * 2048 + D + (n + 1) * 512],
                        start=l == 0, stop=l == 1),
                        waits=[pv["evv"]] if (n == 0 and l == 0) else (), inc=sm_pe)
            pv["vmm"][s] = vmm
            evv = P.op("scalar", lambda s2, s=s: s2.activation(
                out=vstage[:, s * NH * 65:(s + 1) * NH * 65]
                    .rearrange("p (h f) -> p h f", h=NH)[:, :, 0:64],
                in_=PS[:, 3072:4096].rearrange("p (h f) -> p h f", h=NH),
                func=AF.Copy), waits=[vmm, pv["sp2"][s], p0_pool], inc=sm_act)
            pv["evv"] = evv
            k2p = P.op("gpsimd", lambda g, s=s: g.tensor_tensor(
                out=k2s[:], in0=kstage[:, s * D:(s + 1) * D],
                in1=kstage[:, s * D:(s + 1) * D], op=ALU.mult),
                waits=[evk, pv["kmm2"]], inc=sm_pl)
            kmm2 = None
            for c in range(8):
                kmm2 = P.op("tensor", lambda t, c=c: t.matmul(
                    PS[0:1, 1280:1408], onesb[:, 0:1], k2s[:, c * 128:(c + 1) * 128],
                    start=c == 0, stop=c == 7),
                    waits=[k2p, pv["rm"]] if c == 0 else (), inc=sm_pe)
            pv["kmm2"] = kmm2
            def emit_k2red(kmm2=kmm2):
                rm = P.op("vector", lambda v: v.tensor_reduce(
                    out=sc[0:1, 1:2], in_=PS[0:1, 1280:1408], axis=AX.X, op=ALU.max),
                    waits=[kmm2], inc=sm_dve)
                pv["rm"] = rm
                return P.op("vector", lambda v: v.tensor_max(
                    out=sc[0:1, 0:1], in0=sc[0:1, 0:1], in1=sc[0:1, 1:2]),
                    waits=[rm, p0_pool], inc=sm_dve)
            k2_pend = emit_k2red
            sp1 = P.dma("sync", k_dram[kt], kstage[:, s * D:(s + 1) * D],
                        waits=[evk], inc=sm_sp)
            sp2 = P.dma("sync", v_dram[kt],
                        vstage[:, s * NH * 65:(s + 1) * NH * 65],
                        waits=[evv], inc=sm_sp, chain=False)
            pv["sp1"][s] = sp2
            pv["sp2"][s] = sp2
            spill_marks.append((sp2[0], sp2[1]))
            for _ in range(6):
                next(_wgen, None)
        for _ in _wgen:
            pass
        wprep_done = (sm_act, sm_act.n)
        k2_chain_mark = k2_pend()
        p1_done = [k2_chain_mark, spill_marks[-1], wprep_done]

        out_marks = []
        spillj_marks = []
        yrc_marks = []
        for t in range(nt):
            nkt_t = 4 * t + 4
            ldx0 = P.dma("sync", xt[:, 0:D], xq_in[t * TS:t * TS + 128, :],
                         waits=p1_done if t == 0 else [jprev["x1m"]], inc=sm_in)
            ldx1 = P.dma("sync", xt[:, D:2 * D], xq_in[t * TS + 128:t * TS + 256, :],
                         inc=sm_in, chain=False)
            xfm_evs = []
            for s_i, ldm in ((0, ldx1), (1, ldx1)):
                stt = ln_stats(xt[:, s_i * D:(s_i + 1) * D], D, 0 + 4 * s_i, [ldm])
                ap1 = ln_apply(xln[:, s_i * D:(s_i + 1) * D],
                               xt[:, s_i * D:(s_i + 1) * D], 0 + 4 * s_i, [stt])
                blocks = [(xln[:, s_i * D + c * 128: s_i * D + (c + 1) * 128],
                           c * 128) for c in range(8)]
                _, evx = idmm_batch(
                    blocks,
                    xfm.rearrange("p (c f) -> p c f", c=8)[:, :, s_i * 128:(s_i + 1) * 128],
                    PS[:, 0:1024].rearrange("p (c f) -> p c f", c=8),
                    [ap1] + (xfm_evs[-1:]))
                xfm_evs.append(evx)
            q_evs = []
            evc_prev = None
            for s_i in range(2):
                for c in range(8):
                    mcq = P.op("tensor", lambda tt, c=c, s_i=s_i: tt.matmul(
                        BK(0, 0, LAT), xfm[:, c * TS + s_i * 128: c * TS + s_i * 128 + 128],
                        w_dq[:, c * LAT:(c + 1) * LAT], start=c == 0, stop=c == 7),
                        waits=(list(xfm_evs) + ([jprev["ffn03"]] if t else []))
                        if c == 0 else (), inc=sm_pe)
                ev2 = P.op("vector", lambda v: v.tensor_scalar(
                    out=cq_sb[:, 0:LAT], in0=BK(0, 0, LAT), scalar1=1.0, scalar2=0.0,
                    op0=ALU.mult, op1=ALU.add, accum_out=st[:, 8:9]), waits=[mcq], inc=sm_dve)
                sq2 = P.op("vector", lambda v: v.scalar_tensor_tensor(
                    out=jnk[:, 0:LAT], in0=cq_sb[:, 0:LAT], scalar=1.0, in1=cq_sb[:, 0:LAT],
                    op0=ALU.bypass, op1=ALU.mult, accum_out=st[:, 9:10]), waits=[ev2], inc=sm_dve)
                d3 = P.op("vector", lambda v: v.tensor_scalar(
                    out=st[:, 10:11], in0=st[:, 8:9], scalar1=1.0 / LAT, scalar2=0.0,
                    op0=ALU.mult, op1=ALU.add), waits=[ev2], inc=sm_dve)
                d4 = P.op("vector", lambda v: v.scalar_tensor_tensor(
                    out=st[:, 11:12], in0=st[:, 10:11], scalar=1.0, in1=st[:, 10:11],
                    op0=ALU.mult, op1=ALU.mult), waits=[d3], inc=sm_dve)
                d5 = P.op("vector", lambda v: v.tensor_scalar(
                    out=st[:, 9:10], in0=st[:, 9:10], scalar1=1.0 / LAT, scalar2=st[:, 11:12],
                    op0=ALU.mult, op1=ALU.subtract), waits=[sq2, d4], inc=sm_dve)
                d6 = P.op("scalar", lambda s: s.activation(
                    out=st[:, 9:10], in_=st[:, 9:10], func=AF.Ln, bias=C_EPS),
                    waits=[d5], inc=sm_act)
                d7 = P.op("scalar", lambda s: s.activation(
                    out=st[:, 8:9], in_=st[:, 9:10], func=AF.Exp, scale=-0.5),
                    waits=[d6], inc=sm_act)
                d8 = P.op("vector", lambda v: v.scalar_tensor_tensor(
                    out=st[:, 10:11], in0=st[:, 10:11], scalar=-1.0, in1=st[:, 8:9],
                    op0=ALU.mult, op1=ALU.mult), waits=[d7], inc=sm_dve)
                a1 = P.op("vector", lambda v: v.tensor_scalar(
                    out=scr_s[:, 0:LAT], in0=cq_sb[:, 0:LAT], scalar1=st[:, 8:9],
                    scalar2=st[:, 10:11], op0=ALU.mult, op1=ALU.add), waits=[d8], inc=sm_dve)
                a2 = P.op("vector", lambda v: v.scalar_tensor_tensor(
                    out=scr_s[:, LAT:2 * LAT], in0=scr_s[:, 0:LAT], scalar=1.0,
                    in1=gbq[:], op0=ALU.mult, op1=ALU.mult), waits=[a1], inc=sm_dve)
                a3 = P.op("vector", lambda v: v.tensor_tensor(
                    out=cqn[:, 0:LAT], in0=scr_s[:, LAT:2 * LAT], in1=bbq[:],
                    op=ALU.add), waits=[a2], inc=sm_dve)
                blocks2 = [(cqn[:, l * 128:(l + 1) * 128], 512 + l * 128) for l in range(2)]
                _, evc = idmm_batch(
                    blocks2,
                    cqfm.rearrange("p (l f) -> p l f", l=2)[:, :, s_i * 128:(s_i + 1) * 128],
                    PS[:, 512:768].rearrange("p (l f) -> p l f", l=2),
                    [a3] + ([evc_prev] if (s_i == 1 and evc_prev is not None) else []))
                evc_prev = evc
                qmm = None
                for c in range(8):
                    for l in range(2):
                        first = (l == 0 and c % 2 == 0)
                        stop = (l == 1 and c % 2 == 1)
                        qmm = P.op("tensor", lambda tt, c=c, l=l, first=first, stop=stop, s_i=s_i: tt.matmul(
                            PS[:, 2048 + c * 256 + s_i * 128: 2048 + c * 256 + s_i * 128 + 128],
                            w_uq[:, l * D + c * 128: l * D + (c + 1) * 128],
                            cqfm[:, l * LAT + s_i * 128: l * LAT + s_i * 128 + 128],
                            start=first, stop=stop),
                            waits=([evc] + q_evs[-1:] + ([jprev["ffn47"]] if t else []))
                            if (c == 0 and l == 0) else (),
                            inc=sm_pe)
                ev_q = P.op("scalar", lambda s, s_i=s_i: s.activation(
                    out=qfm.rearrange("p (c f) -> p c f", c=8)[:, :, s_i * 128:(s_i + 1) * 128],
                    in_=PS[:, 2048:4096].rearrange("p (c f) -> p c f", c=8)[:, :, s_i * 128:(s_i + 1) * 128],
                    func=AF.Copy), waits=[qmm], inc=sm_act)
                q_evs.append(ev_q)
            q2p = P.op("gpsimd", lambda g: g.tensor_tensor(
                out=q2[:], in0=qfm[:], in1=qfm[:], op=ALU.mult),
                waits=q_evs + ([jprev["mixd"]] if t else []), inc=sm_pl)
            for c in range(8):
                qmm2 = P.op("tensor", lambda tt, c=c: tt.matmul(
                    PS[0:1, 0:TS], onesb[:, 0:1], q2[:, c * TS:(c + 1) * TS],
                    start=c == 0, stop=c == 7),
                    waits=[q2p] + xfm_evs if c == 0 else (), inc=sm_pe)
            b1 = P.op("vector", lambda v: v.tensor_reduce(
                out=sc[0:1, 2:3], in_=PS[0:1, 0:TS], axis=AX.X, op=ALU.max),
                waits=[qmm2], inc=sm_dve)
            b2 = P.op("vector", lambda v: v.tensor_mul(
                out=sc[0:1, 3:4], in0=sc[0:1, 2:3], in1=sc[0:1, 0:1]),
                waits=[b1], inc=sm_dve)
            b3 = P.op("scalar", lambda s: s.activation(
                out=sc[0:1, 3:4], in_=sc[0:1, 3:4], func=AF.Ln, bias=cvals[0:1, 2:3]),
                waits=[b2], inc=sm_act)
            b4 = P.op("scalar", lambda s: s.activation(
                out=sc[0:1, 3:4], in_=sc[0:1, 3:4], func=AF.Exp, scale=0.5),
                waits=[b3], inc=sm_act)
            b5 = P.op("vector", lambda v: v.tensor_scalar(
                out=scb[0:1, 0:1], in0=sc[0:1, 3:4], scalar1=-1.02 * INV_SQRT_DH,
                scalar2=0.0, op0=ALU.mult, op1=ALU.add), waits=[b4], inc=sm_dve)
            b6 = P.op("tensor", lambda tt: tt.matmul(
                PS[:, 256:257], onesb[0:1, :], scb[0:1, 0:1], start=True, stop=True),
                waits=[b5], inc=sm_pe)
            bias_mark = P.op("vector", lambda v: v.tensor_copy(
                out=bias_b[:], in_=PS[:, 256:257]), waits=[b6], inc=sm_dve)

            if stages == "qpath":
                om = P.dma("sync", out[t * TS: t * TS + 128, :], xt[:, 0:D],
                           waits=[bias_mark], inc=sm_out)
                out_marks.append(om)
                continue
            sub_pv = stages not in ("attn_scores", "attn_mm", "attn_ld")
            sub_epi = stages not in ("attn_scores", "attn_pv", "attn_mm", "attn_ld")
            attn_ev_marks = []
            pv_last_prev_quarter = None
            for g in range(4):
                kt_loads = []
                score_marks = []
                exp_marks = []
                pv_marks = []
                mask_marks = {}
                LAG = 3
                for kt in range(nkt_t + LAG):
                  if kt < nkt_t:
                    slot = kt % 6
                    w_sp = spill_marks[kt]
                    skv = sm_kvs[slot]
                    war = [score_marks[kt - 6], pv_marks[kt - 6]] if kt >= 6 else \
                        ([pv_last_prev_quarter] if pv_last_prev_quarter else [])
                    l1 = P.dma("sync", kbuf[:, slot * 256:(slot + 1) * 256],
                               k_dram[kt][:, g * 256:(g + 1) * 256],
                               waits=[w_sp] + war, inc=skv)
                    l2 = P.dma("sync", vbuf[:, slot * 260: slot * 260 + 260],
                               v_dram[kt][:, g * 260:(g + 1) * 260],
                               inc=skv, chain=False)
                    kt_loads.append((l2[0], l2[1]))
                    pb = (kt % 2) * 2
                    deps = [kt_loads[kt]]
                    if kt >= 2:
                        deps.append(exp_marks[kt - 2])
                    if kt == 0:
                        deps.append(bias_mark)
                    if kt <= 1 and attn_ev_marks:
                        deps.append(attn_ev_marks[-1])
                    smm = None
                    if stages == "attn_ld":
                        score_marks.append(kt_loads[kt]); exp_marks.append(kt_loads[kt])
                        pv_marks.append(kt_loads[kt])
                        continue
                    for hh in range(4):
                        pr = hh % 2
                        ch = 2 * g + hh // 2
                        soff = pr * 512 + (hh // 2) * 256
                        smm = P.op("tensor", lambda tt, hh=hh, pr=pr, ch=ch, pb=pb, slot=slot, soff=soff: tt.matmul(
                            PS[:, pb * 512 + soff: pb * 512 + soff + 256],
                            kbuf[pr * 64:(pr + 1) * 64,
                                 slot * 256 + (hh // 2) * 128: slot * 256 + (hh // 2) * 128 + 128],
                            qfm[pr * 64:(pr + 1) * 64, ch * TS:(ch + 1) * TS],
                            start=(hh // 2) == 0, stop=(hh // 2) == 1,
                            tile_position=(pr * 64, 0)),
                            waits=deps if hh == 0 else (), inc=sm_pe)
                    score_marks.append(smm)
                    pslot = kt % 4
                    edeps = [smm]
                    if kt >= 4 and kt - 4 - LAG + 1 > 0:
                        pass
                    if kt - 4 >= 0:
                        edeps.append(pv_marks[kt - 4] if kt - 4 < len(pv_marks) else None)
                    if stages == "attn_mm":
                        em = P.op("scalar", lambda s: s.nop(), waits=edeps, inc=sm_act)
                    else:
                        em = P.op("scalar", lambda s, pb=pb, pslot=pslot: s.activation(
                            out=pbuf[:, pslot * 1024:(pslot + 1) * 1024],
                            in_=PS[:, pb * 512: pb * 512 + 1024],
                            func=AF.Exp, bias=bias_b[:, 0:1], scale=INV_SQRT_DH),
                            waits=edeps, inc=sm_act)
                    exp_marks.append(em)
                    if sub_pv and kt >= nkt_t - 4:
                        sidx = kt - (nkt_t - 4)
                        mm_ = None
                        for hh in range(4):
                            hoff = (hh % 2) * 512 + (hh // 2) * 256
                            mm_ = P.op("gpsimd", lambda gg, sidx=sidx, pslot=pslot, hoff=hoff: gg.tensor_tensor(
                                out=pbuf[:, pslot * 1024 + hoff: pslot * 1024 + hoff + 256],
                                in0=pbuf[:, pslot * 1024 + hoff: pslot * 1024 + hoff + 256],
                                in1=mask_sb[:, sidx * TS:(sidx + 1) * TS], op=ALU.mult),
                                waits=[em] if hh == 0 else (), inc=sm_pl)
                        mask_marks[kt] = mm_
                  kp = kt - LAG
                  if kp >= 0 and sub_pv and stages != "attn_ld":
                    pv_dep = [mask_marks.get(kp, exp_marks[kp])]
                    pslotp = kp % 4
                    slotp = kp % 6
                    pvm = None
                    for hh in range(4):
                        pvm = P.op("tensor", lambda tt, hh=hh, pslot=pslotp, slot=slotp, kt=kp, nkt_t=nkt_t: tt.matmul(
                            PS[0:65, (4 + hh) * 512: (4 + hh) * 512 + 256],
                            vbuf[:, slot * 260 + hh * 65: slot * 260 + (hh + 1) * 65],
                            pbuf[:, pslot * 1024 + (hh % 2) * 512 + (hh // 2) * 256:
                                 pslot * 1024 + (hh % 2) * 512 + (hh // 2) * 256 + 256],
                            start=kt == 0, stop=kt == nkt_t - 1),
                            waits=pv_dep if hh == 0 else (), inc=sm_pe)
                    pv_marks.append(pvm)
                  elif kp >= 0 and (not sub_pv) and stages != "attn_ld":
                    pv_marks.append(exp_marks[kp])
                pv_last_prev_quarter = pv_marks[-1]
                if not sub_epi:
                    attn_ev_marks.append(pv_marks[-1])
                    continue
                evq = P.op("scalar", lambda s: s.activation(
                    out=aev[0:65, :].rearrange("p (h f) -> p h f", h=4),
                    in_=PS[0:65, 2048:4096].rearrange("p (h f) -> p h f", h=4)[:, :, 0:256],
                    func=AF.Copy), waits=[pv_marks[-1]], inc=sm_act)
                tmks = []
                for sl in range(8):
                    hh, qh = sl // 2, sl % 2
                    po = 65 * sl if sl < 7 else 512
                    tmks.append(P.op("tensor", lambda tt, hh=hh, qh=qh, po=po: tt.matmul(
                        PS[0:128, po:po + 65],
                        aev[0:65, hh * 256 + qh * 128: hh * 256 + qh * 128 + 128],
                        ident[0:65, 0:65], start=True, stop=True),
                        waits=[evq, exp_marks[-1], exp_marks[-2] if nkt_t >= 2 else evq]
                        if sl == 0 else (), inc=sm_pe))
                nmks = []
                for sl in range(8):
                    po = 65 * sl if sl < 7 else 512
                    r1 = P.op("vector", lambda v, po=po, sl=sl: v.reciprocal(
                        out=rc[:, sl:sl + 1], in_=PS[:, po + 64: po + 65]),
                        waits=[tmks[-1]] if sl == 0 else (), inc=sm_dve)
                    n1 = P.op("vector", lambda v, po=po, sl=sl: v.tensor_scalar(
                        out=atok[:, sl * 64:(sl + 1) * 64], in0=PS[:, po:po + 64],
                        scalar1=rc[:, sl:sl + 1], scalar2=0.0,
                        op0=ALU.mult, op1=ALU.add), waits=[r1], inc=sm_dve)
                    nmks.append(n1)
                bmks = []
                for sl in range(8):
                    hh, qh = sl // 2, sl % 2
                    po = 1024 + (hh // 2) * 512 + qh * 128
                    cp = (hh % 2) * 64
                    bmks.append(P.op("tensor", lambda tt, sl=sl, po=po, cp=cp: tt.matmul(
                        PS[cp:cp + 64, po:po + 128],
                        atok[:, sl * 64:(sl + 1) * 64], ident[:],
                        start=True, stop=True, tile_position=(0, cp)),
                        waits=[nmks[-1]] if sl == 0 else (), inc=sm_pe))
                evf = P.op("scalar", lambda s, g=g: s.activation(
                    out=attn_fm[:, 2 * g * TS: (2 * g + 2) * TS].rearrange(
                        "p (b x) -> p b x", b=2),
                    in_=PS[:, 1024:2048].rearrange("p (b x) -> p b x", b=2)[:, :, 0:256],
                    func=AF.Copy),
                    waits=[bmks[-1]], inc=sm_act)
                attn_ev_marks.append(evf)

            if stages.startswith("attn"):
                om = P.dma("sync", out[t * TS: t * TS + 128, :], xt[:, 0:D],
                           waits=list(attn_ev_marks), inc=sm_out)
                out_marks.append(om)
                continue
            omm = {}
            for s_i in range(2):
                for n in range(2):
                    for c in range(8):
                        omm[(s_i, n)] = P.op("tensor", lambda tt, s_i=s_i, n=n, c=c: tt.matmul(
                            PS[:, (2 * s_i + n) * 512: (2 * s_i + n + 1) * 512],
                            attn_fm[:, c * TS + s_i * 128: c * TS + s_i * 128 + 128],
                            w_oT[:, c * D + n * 512: c * D + (n + 1) * 512],
                            start=c == 0, stop=c == 7),
                            waits=attn_ev_marks + nmks if (c == 0 and s_i == 0 and n == 0) else (),
                            inc=sm_pe)
            y_evs = []
            x1ms = {}
            for s_i in range(2):
                e1 = P.op("vector", lambda v, s_i=s_i: v.tensor_scalar(
                    out=o_sb[:, s_i * D: s_i * D + 512], in0=BK(2 * s_i, 0, 512),
                    scalar1=1.0, scalar2=0.0, op0=ALU.mult, op1=ALU.add,
                    accum_out=st[:, 0:1]),
                    waits=[omm[(s_i, 0)]] + ([jprev["om"]] if (t and s_i == 0) else []),
                    inc=sm_dve)
                e2 = P.op("vector", lambda v, s_i=s_i: v.tensor_scalar(
                    out=o_sb[:, s_i * D + 512: (s_i + 1) * D], in0=BK(2 * s_i + 1, 0, 512),
                    scalar1=1.0, scalar2=0.0, op0=ALU.mult, op1=ALU.add,
                    accum_out=st[:, 4:5]), waits=[omm[(s_i, 1)]], inc=sm_dve)
                e3 = P.op("vector", lambda v: v.tensor_add(
                    out=st[:, 0:1], in0=st[:, 0:1], in1=st[:, 4:5]), waits=[e1, e2],
                    inc=sm_dve)
                sq = P.op("vector", lambda v, s_i=s_i: v.scalar_tensor_tensor(
                    out=jnk[:, 0:D], in0=o_sb[:, s_i * D:(s_i + 1) * D], scalar=1.0,
                    in1=o_sb[:, s_i * D:(s_i + 1) * D],
                    op0=ALU.bypass, op1=ALU.mult,
                    accum_out=st[:, 1:2]), waits=[e1, e2], inc=sm_dve)
                d3 = P.op("vector", lambda v: v.tensor_scalar(
                    out=st[:, 2:3], in0=st[:, 0:1], scalar1=1.0 / D, scalar2=0.0,
                    op0=ALU.mult, op1=ALU.add), waits=[e3], inc=sm_dve)
                d4 = P.op("vector", lambda v: v.scalar_tensor_tensor(
                    out=st[:, 3:4], in0=st[:, 2:3], scalar=1.0, in1=st[:, 2:3],
                    op0=ALU.mult, op1=ALU.mult), waits=[d3], inc=sm_dve)
                d5 = P.op("vector", lambda v: v.tensor_scalar(
                    out=st[:, 1:2], in0=st[:, 1:2], scalar1=1.0 / D, scalar2=st[:, 3:4],
                    op0=ALU.mult, op1=ALU.subtract), waits=[sq, d4], inc=sm_dve)
                d6 = P.op("scalar", lambda s: s.activation(
                    out=st[:, 1:2], in_=st[:, 1:2], func=AF.Ln, bias=C_EPS), waits=[d5],
                    inc=sm_act)
                d7 = P.op("scalar", lambda s: s.activation(
                    out=st[:, 0:1], in_=st[:, 1:2], func=AF.Exp, scale=-0.5), waits=[d6],
                    inc=sm_act)
                d8 = P.op("vector", lambda v: v.scalar_tensor_tensor(
                    out=st[:, 2:3], in0=st[:, 2:3], scalar=-1.0, in1=st[:, 0:1],
                    op0=ALU.mult, op1=ALU.mult), waits=[d7], inc=sm_dve)
                xm = P.op("vector", lambda v, s_i=s_i: v.tensor_scalar(
                    out=e_scr[:, s_i * D:(s_i + 1) * D], in0=xt[:, s_i * D:(s_i + 1) * D],
                    scalar1=1.0, scalar2=st[:, 2:3], op0=ALU.mult, op1=ALU.add),
                    waits=[d8] + ([jprev["x10"]] if (t and s_i == 0) else []), inc=sm_dve)
                x1m = P.op("vector", lambda v, s_i=s_i: v.scalar_tensor_tensor(
                    out=x1[:, s_i * D:(s_i + 1) * D], in0=o_sb[:, s_i * D:(s_i + 1) * D],
                    scalar=st[:, 0:1], in1=e_scr[:, s_i * D:(s_i + 1) * D],
                    op0=ALU.mult, op1=ALU.add), waits=[xm], inc=sm_dve)
                x1ms[s_i] = x1m
                sty = ln_stats(x1[:, s_i * D:(s_i + 1) * D], D, 8 + 4 * s_i, [x1m])
                y_evs.append(ln_apply(ytm[:, s_i * D:(s_i + 1) * D],
                                      x1[:, s_i * D:(s_i + 1) * D], 8 + 4 * s_i, [sty]))
            if stages == "oproj":
                om = P.dma("sync", out[t * TS: t * TS + 128, :], x1[:, 0:D],
                           waits=y_evs, inc=sm_out)
                out_marks.append(om)
                continue
            yfm_evs = []
            for s_i in range(2):
                blocks = [(ytm[:, s_i * D + c * 128: s_i * D + (c + 1) * 128], c * 128)
                          for c in range(8)]
                _, evy = idmm_batch(
                    blocks,
                    yfm.rearrange("p (c f) -> p c f", c=8)[:, :, s_i * 128:(s_i + 1) * 128],
                    PS[:, 0:1024].rearrange("p (c f) -> p c f", c=8),
                    [y_evs[s_i]] + ([omm[(1, 1)]] if s_i == 0 else yfm_evs[-1:]))
                yfm_evs.append(evy)
            yrc = P.op("gpsimd", lambda g, t=t: g.tensor_copy(
                out=yrows_all[:, t * 8:(t + 1) * 8],
                in_=yfm.rearrange("p (c f) -> p c f", c=8)[:, :, 255:256]),
                waits=yfm_evs, inc=sm_pl)
            y0c = P.op("gpsimd", lambda g, t=t: g.tensor_copy(
                out=y0_all[:, t * 8:(t + 1) * 8],
                in_=yfm.rearrange("p (c f) -> p c f", c=8)[:, :, 0:1]),
                inc=sm_pl)
            yrc_marks.append(y0c)
            x10 = P.dma("sync", x10_all[t:t + 1, :], x1[0:1, 0:D],
                        waits=[x1ms[0]], inc=sm_sp)

            dm = P.op("gpsimd", lambda g: g.tensor_tensor(
                out=dscr.rearrange("p (c f) -> p c f", c=8)[:, :, 1:256],
                in0=yfm.rearrange("p (c f) -> p c f", c=8)[:, :, 1:256],
                in1=yfm.rearrange("p (c f) -> p c f", c=8)[:, :, 0:255],
                op=ALU.subtract), waits=[yfm_evs[-1]], inc=sm_pl)
            mix_marks = []
            for c in range(8):
                m1 = P.op("vector", lambda g, c=c: g.tensor_scalar(
                    out=xr[:, c * TS:c * TS + 1], in0=yfm[:, c * TS:c * TS + 1],
                    scalar1=mu_rc[:, c:c + 1], scalar2=0.0,
                    op0=ALU.mult, op1=ALU.add), waits=[dm], inc=sm_dve)
                m2 = P.op("vector", lambda g, c=c: g.tensor_scalar(
                    out=xk[:, c * TS:c * TS + 1], in0=yfm[:, c * TS:c * TS + 1],
                    scalar1=mu_kc[:, c:c + 1], scalar2=0.0,
                    op0=ALU.mult, op1=ALU.add), inc=sm_dve)
                m3 = P.op("vector", lambda g, c=c: g.scalar_tensor_tensor(
                    out=xr[:, c * TS + 1:(c + 1) * TS],
                    in0=dscr[:, c * TS + 1:(c + 1) * TS],
                    scalar=mu_rc[:, c:c + 1],
                    in1=yfm[:, c * TS:(c + 1) * TS - 1],
                    op0=ALU.mult, op1=ALU.add), inc=sm_dve)
                m4 = P.op("vector", lambda g, c=c: g.scalar_tensor_tensor(
                    out=xk[:, c * TS + 1:(c + 1) * TS],
                    in0=dscr[:, c * TS + 1:(c + 1) * TS],
                    scalar=mu_kc[:, c:c + 1],
                    in1=yfm[:, c * TS:(c + 1) * TS - 1],
                    op0=ALU.mult, op1=ALU.add), inc=sm_dve)
                mix_marks += [m1, m2, m3, m4]
            mix_done = mix_marks[-1]
            rmm = {}
            kmm = {}
            for s_i in range(2):
                for n in range(2):
                    for c in range(8):
                        rmm[(s_i, n)] = P.op("tensor", lambda tt, s_i=s_i, n=n, c=c: tt.matmul(
                            PS[:, (2 * s_i + n) * 512: (2 * s_i + n + 1) * 512],
                            xr[:, c * TS + s_i * 128: c * TS + s_i * 128 + 128],
                            w_rT[:, c * D + n * 512: c * D + (n + 1) * 512],
                            start=c == 0, stop=c == 7),
                            waits=[mix_done]
                            if (s_i == 0 and n == 0 and c == 0) else (), inc=sm_pe)
            for s_i in range(2):
                for n in range(2):
                    for c in range(8):
                        kmm[(s_i, n)] = P.op("tensor", lambda tt, s_i=s_i, n=n, c=c: tt.matmul(
                            PS[:, 2048 + (2 * s_i + n) * 512: 2048 + (2 * s_i + n + 1) * 512],
                            xk[:, c * TS + s_i * 128: c * TS + s_i * 128 + 128],
                            w_kT[:, c * D + n * 512: c * D + (n + 1) * 512],
                            start=c == 0, stop=c == 7),
                            waits=[mix_done] if (s_i == 0 and n == 0 and c == 0) else (),
                            inc=sm_pe)
            r_marks = []
            for s_i in range(2):
                s1 = P.op("scalar", lambda s, s_i=s_i: s.activation(
                    out=e_scr[:, s_i * D:(s_i + 1) * D],
                    in_=PS[:, 2 * s_i * 512: (2 * s_i + 2) * 512],
                    func=AF.Exp, scale=-1.0), waits=[rmm[(s_i, 1)]], inc=sm_act)
                s2 = P.op("scalar", lambda s, s_i=s_i: s.activation(
                    out=e_scr[:, s_i * D:(s_i + 1) * D],
                    in_=e_scr[:, s_i * D:(s_i + 1) * D],
                    func=AF.Ln, bias=C_ONE), waits=[s1], inc=sm_act)
                s3 = P.op("scalar", lambda s, s_i=s_i: s.activation(
                    out=r_sb[:, s_i * D:(s_i + 1) * D],
                    in_=e_scr[:, s_i * D:(s_i + 1) * D],
                    func=AF.Exp, scale=-1.0), waits=[s2], inc=sm_act)
                r_marks.append(s3)
            ka_marks = []
            for s_i in range(2):
                for n in range(2):
                    rl = P.op("vector", lambda v, s_i=s_i, n=n: v.tensor_scalar_max(
                        out=kact[:, s_i * D + n * 512: s_i * D + (n + 1) * 512],
                        in0=PS[:, 2048 + (2 * s_i + n) * 512: 2048 + (2 * s_i + n + 1) * 512],
                        scalar1=0.0), waits=[kmm[(s_i, n)]], inc=sm_dve)
                    ka_marks.append(rl)
            sqk = P.op("gpsimd", lambda g: g.tensor_tensor(
                out=kact[:], in0=kact[:], in1=kact[:], op=ALU.mult),
                waits=ka_marks, inc=sm_pl)
            kfm_evs = []
            for s_i in range(2):
                blocks = [(kact[:, s_i * D + c * 128: s_i * D + (c + 1) * 128], c * 128)
                          for c in range(8)]
                _, evk2 = idmm_batch(
                    blocks,
                    kactfm.rearrange("p (c f) -> p c f", c=8)[:, :, s_i * 128:(s_i + 1) * 128],
                    PS[:, 0:1024].rearrange("p (c f) -> p c f", c=8),
                    [sqk] + ([r_marks[1]] if s_i == 0 else kfm_evs[-1:]))
                kfm_evs.append(evk2)
            vmm2 = {}
            for s_i in range(2):
                for n in range(2):
                    for c in range(8):
                        vmm2[(s_i, n)] = P.op("tensor", lambda tt, s_i=s_i, n=n, c=c: tt.matmul(
                            PS[:, 2048 + (2 * s_i + n) * 512: 2048 + (2 * s_i + n + 1) * 512],
                            kactfm[:, c * TS + s_i * 128: c * TS + s_i * 128 + 128],
                            w_vT[:, c * D + n * 512: c * D + (n + 1) * 512],
                            start=c == 0, stop=c == 7),
                            waits=kfm_evs + [ka_marks[-1]]
                            if (s_i == 0 and n == 0 and c == 0) else (), inc=sm_pe)
            f2_last = None
            for s_i in range(2):
                f1 = P.op("vector", lambda v, s_i=s_i: v.scalar_tensor_tensor(
                    out=o_sb[:, s_i * D: s_i * D + 512],
                    in0=PS[:, 2048 + 2 * s_i * 512: 2048 + (2 * s_i + 1) * 512],
                    scalar=1.0, in1=r_sb[:, s_i * D: s_i * D + 512],
                    op0=ALU.mult, op1=ALU.mult, accum_out=st[:, 0:1]),
                    waits=[vmm2[(s_i, 0)], r_marks[s_i]], inc=sm_dve)
                f2 = P.op("vector", lambda v, s_i=s_i: v.scalar_tensor_tensor(
                    out=o_sb[:, s_i * D + 512: (s_i + 1) * D],
                    in0=PS[:, 2048 + (2 * s_i + 1) * 512: 2048 + (2 * s_i + 2) * 512],
                    scalar=1.0, in1=r_sb[:, s_i * D + 512: (s_i + 1) * D],
                    op0=ALU.mult, op1=ALU.mult, accum_out=st[:, 4:5]),
                    waits=[vmm2[(s_i, 1)]], inc=sm_dve)
                f2_last = f2
                e3 = P.op("vector", lambda v: v.tensor_add(
                    out=st[:, 0:1], in0=st[:, 0:1], in1=st[:, 4:5]), waits=[f1, f2],
                    inc=sm_dve)
                sq = P.op("vector", lambda v, s_i=s_i: v.scalar_tensor_tensor(
                    out=jnk[:, 0:D], in0=o_sb[:, s_i * D:(s_i + 1) * D], scalar=1.0,
                    in1=o_sb[:, s_i * D:(s_i + 1) * D],
                    op0=ALU.bypass, op1=ALU.mult,
                    accum_out=st[:, 1:2]), waits=[f1, f2], inc=sm_dve)
                d3 = P.op("vector", lambda v: v.tensor_scalar(
                    out=st[:, 2:3], in0=st[:, 0:1], scalar1=1.0 / D, scalar2=0.0,
                    op0=ALU.mult, op1=ALU.add), waits=[e3], inc=sm_dve)
                d4 = P.op("vector", lambda v: v.scalar_tensor_tensor(
                    out=st[:, 3:4], in0=st[:, 2:3], scalar=1.0, in1=st[:, 2:3],
                    op0=ALU.mult, op1=ALU.mult), waits=[d3], inc=sm_dve)
                d5 = P.op("vector", lambda v: v.tensor_scalar(
                    out=st[:, 1:2], in0=st[:, 1:2], scalar1=1.0 / D, scalar2=st[:, 3:4],
                    op0=ALU.mult, op1=ALU.subtract), waits=[sq, d4], inc=sm_dve)
                d6 = P.op("scalar", lambda s: s.activation(
                    out=st[:, 1:2], in_=st[:, 1:2], func=AF.Ln, bias=C_EPS), waits=[d5],
                    inc=sm_act)
                d7 = P.op("scalar", lambda s: s.activation(
                    out=st[:, 0:1], in_=st[:, 1:2], func=AF.Exp, scale=-0.5), waits=[d6],
                    inc=sm_act)
                d8 = P.op("vector", lambda v: v.scalar_tensor_tensor(
                    out=st[:, 2:3], in0=st[:, 2:3], scalar=-1.0, in1=st[:, 0:1],
                    op0=ALU.mult, op1=ALU.mult), waits=[d7], inc=sm_dve)
                xmF = P.op("vector", lambda v, s_i=s_i: v.tensor_scalar(
                    out=e_scr[:, s_i * D:(s_i + 1) * D], in0=x1[:, s_i * D:(s_i + 1) * D],
                    scalar1=1.0, scalar2=st[:, 2:3], op0=ALU.mult, op1=ALU.add),
                    waits=[d8], inc=sm_dve)
                x2m = P.op("vector", lambda v, s_i=s_i: v.scalar_tensor_tensor(
                    out=o_sb[:, s_i * D:(s_i + 1) * D], in0=o_sb[:, s_i * D:(s_i + 1) * D],
                    scalar=st[:, 0:1], in1=e_scr[:, s_i * D:(s_i + 1) * D],
                    op0=ALU.mult, op1=ALU.add), waits=[xmF], inc=sm_dve)
                om = P.dma("sync", out[t * TS + s_i * 128: t * TS + (s_i + 1) * 128, :],
                           o_sb[:, s_i * D:(s_i + 1) * D], waits=[x2m], inc=sm_out)
                out_marks.append(om)
            jprev = {"x1m": x1ms[1], "ffn03": r_marks[-1], "ffn47": f2_last,
                     "om": out_marks[-1], "mixd": mix_done, "x10": x10}

        srow_sp = P.dma("sync", yrow_src[:], yrows_all[:],
                        waits=[yrc_marks[-1]], inc=sm_sp)
        cl = P.op("gpsimd", lambda g: g.collective_compute(
            "AllGather", ALU.bypass,
            replica_groups=[[0, 1], [2, 3], [4, 5], [6, 7]],
            ins=[yrow_src.ap().opt()], outs=[yrow_gath.ap().opt()]),
            waits=[(sm_sp, sm_sp.n)], inc=sm_cl)
        ldA = P.dma("sync", carry_A[:], yrow_gath[0], waits=[cl], inc=sm_in)
        ldB = P.dma("sync", carry_B[:], yrow_gath[1], inc=sm_in, chain=False)
        gcast = P.op("vector", lambda v: v.tensor_copy(
            out=scb[0:1, 1:3], in_=cfg_sb[0:1, 2:4]), waits=[consts_ld], inc=sm_dve)
        gmm = P.op("tensor", lambda tt: tt.matmul(
            PS[:, 3328:3330], onesb[0:1, :], scb[0:1, 1:3], start=True, stop=True),
            waits=[gcast, cl, jprev["ffn47"]], inc=sm_pe)
        ga = P.op("vector", lambda v: v.tensor_copy(
            out=gA_col[:], in_=PS[:, 3328:3329]), waits=[gmm], inc=sm_dve)
        gb = P.op("vector", lambda v: v.tensor_copy(
            out=gB_col[:], in_=PS[:, 3329:3330]), waits=[gmm], inc=sm_dve)
        c1 = P.op("vector", lambda v: v.tensor_scalar(
            out=carry_all[:], in0=carry_A[:],
            scalar1=gA_col[:, 0:1], scalar2=0.0, op0=ALU.mult, op1=ALU.add),
            waits=[ga, ldB], inc=sm_dve)
        cprep = P.op("vector", lambda v: v.scalar_tensor_tensor(
            out=carry_all[:, 8:64], in0=carry_B[:, 0:56], scalar=gB_col[:, 0:1],
            in1=carry_all[:, 8:64], op0=ALU.mult, op1=ALU.add),
            waits=[gb, c1], inc=sm_dve)

        fm = None
        for c in range(8):
            vY = y0_all.rearrange("p (t c) -> p c t", c=8)[:, c, :]
            vC = carry_all.rearrange("p (t c) -> p c t", c=8)[:, c, :]
            vR = carry_A.rearrange("p (t c) -> p c t", c=8)[:, c, :]
            vK = carry_B.rearrange("p (t c) -> p c t", c=8)[:, c, :]
            dd = P.op("gpsimd", lambda g, vY=vY, vC=vC, vR=vR: g.tensor_tensor(
                out=vR, in0=vY, in1=vC, op=ALU.subtract),
                waits=[cprep] if c == 0 else (), inc=sm_pl)
            k1 = P.op("vector", lambda v, vR=vR, vK=vK, vC=vC, c=c: v.scalar_tensor_tensor(
                out=vK, in0=vR, scalar=mu_kc[:, c:c + 1], in1=vC,
                op0=ALU.mult, op1=ALU.add), waits=[dd], inc=sm_dve)
            fm = P.op("vector", lambda v, vR=vR, vC=vC, c=c: v.scalar_tensor_tensor(
                out=vR, in0=vR, scalar=mu_rc[:, c:c + 1], in1=vC,
                op0=ALU.mult, op1=ALU.add), waits=[k1], inc=sm_dve)
        rm0 = {}
        km0 = {}
        for c in range(8):
            lhR = carry_A.rearrange("p (t c) -> p c t", c=8)[:, c, :]
            lhK = carry_B.rearrange("p (t c) -> p c t", c=8)[:, c, :]
            for n in range(2):
                rm0[n] = P.op("tensor", lambda tt, lhR=lhR, n=n, c=c: tt.matmul(
                    PS[0:8, n * 512:(n + 1) * 512], lhR,
                    w_rT[:, c * D + n * 512: c * D + (n + 1) * 512],
                    start=c == 0, stop=c == 7),
                    waits=[fm, jprev["ffn03"], jprev["om"]]
                    if (c == 0 and n == 0) else (), inc=sm_pe)
                km0[n] = P.op("tensor", lambda tt, lhK=lhK, n=n, c=c: tt.matmul(
                    PS[0:8, 1024 + n * 512: 1024 + (n + 1) * 512], lhK,
                    w_kT[:, c * D + n * 512: c * D + (n + 1) * 512],
                    start=c == 0, stop=c == 7), inc=sm_pe)
        s1f = P.op("scalar", lambda s: s.activation(
            out=e_scr[0:8, 0:D], in_=PS[0:8, 0:D], func=AF.Exp, scale=-1.0),
            waits=[rm0[1]], inc=sm_act)
        s2f = P.op("scalar", lambda s: s.activation(
            out=e_scr[0:8, 0:D], in_=e_scr[0:8, 0:D], func=AF.Ln,
            bias=cvals[0:8, 1:2]), waits=[s1f], inc=sm_act)
        s3f = P.op("scalar", lambda s: s.activation(
            out=e_scr[0:8, 0:D], in_=e_scr[0:8, 0:D], func=AF.Exp, scale=-1.0),
            waits=[s2f], inc=sm_act)
        rl0 = []
        for n in range(2):
            rl0.append(P.op("vector", lambda v, n=n: v.tensor_scalar_max(
                out=kact[0:8, n * 512:(n + 1) * 512],
                in0=PS[0:8, 1024 + n * 512: 1024 + (n + 1) * 512],
                scalar1=0.0), waits=[km0[1]] if n == 0 else (), inc=sm_dve))
        sq0 = P.op("gpsimd", lambda g: g.tensor_tensor(
            out=kact[0:8, 0:D], in0=kact[0:8, 0:D], in1=kact[0:8, 0:D],
            op=ALU.mult), waits=rl0, inc=sm_pl)
        tr0 = None
        for c in range(8):
            tr0 = P.op("tensor", lambda tt, c=c: tt.matmul(
                PS[:, 1536 + c * 8: 1544 + c * 8],
                kact[0:8, c * 128:(c + 1) * 128], ident[0:8, 0:8],
                start=True, stop=True),
                waits=[sq0, rl0[1]] if c == 0 else (), inc=sm_pe)
        ev0 = P.op("scalar", lambda s: s.activation(
            out=yrows_all[:], in_=PS[:, 1536:1600], func=AF.Copy),
            waits=[tr0], inc=sm_act)
        vm0 = {}
        for c in range(8):
            for n in range(2):
                vm0[n] = P.op("tensor", lambda tt, n=n, c=c: tt.matmul(
                    PS[0:8, 2048 + n * 512: 2048 + (n + 1) * 512],
                    yrows_all[:, c * 8:(c + 1) * 8],
                    w_vT[:, c * D + n * 512: c * D + (n + 1) * 512],
                    start=c == 0, stop=c == 7),
                    waits=[ev0] if (c == 0 and n == 0) else (), inc=sm_pe)
        f0a = P.op("vector", lambda v: v.scalar_tensor_tensor(
            out=e_scr[0:8, D:D + 512], in0=PS[0:8, 2048:2560], scalar=1.0,
            in1=e_scr[0:8, 0:512], op0=ALU.mult, op1=ALU.mult,
            accum_out=st[0:8, 0:1]), waits=[vm0[0], s3f], inc=sm_dve)
        f0b = P.op("vector", lambda v: v.scalar_tensor_tensor(
            out=e_scr[0:8, D + 512:2 * D], in0=PS[0:8, 2560:3072], scalar=1.0,
            in1=e_scr[0:8, 512:D], op0=ALU.mult, op1=ALU.mult,
            accum_out=st[0:8, 4:5]), waits=[vm0[1]], inc=sm_dve)
        e3f = P.op("vector", lambda v: v.tensor_add(
            out=st[0:8, 0:1], in0=st[0:8, 0:1], in1=st[0:8, 4:5]),
            waits=[f0a, f0b], inc=sm_dve)
        sqf = P.op("vector", lambda v: v.scalar_tensor_tensor(
            out=jnk[0:8, 0:D], in0=e_scr[0:8, D:2 * D], scalar=1.0,
            in1=e_scr[0:8, D:2 * D], op0=ALU.bypass, op1=ALU.mult,
            accum_out=st[0:8, 1:2]), waits=[f0a, f0b], inc=sm_dve)
        d3f = P.op("vector", lambda v: v.tensor_scalar(
            out=st[0:8, 2:3], in0=st[0:8, 0:1], scalar1=1.0 / D, scalar2=0.0,
            op0=ALU.mult, op1=ALU.add), waits=[e3f], inc=sm_dve)
        d4f = P.op("vector", lambda v: v.scalar_tensor_tensor(
            out=st[0:8, 3:4], in0=st[0:8, 2:3], scalar=1.0, in1=st[0:8, 2:3],
            op0=ALU.mult, op1=ALU.mult), waits=[d3f], inc=sm_dve)
        d5f = P.op("vector", lambda v: v.tensor_scalar(
            out=st[0:8, 1:2], in0=st[0:8, 1:2], scalar1=1.0 / D, scalar2=st[0:8, 3:4],
            op0=ALU.mult, op1=ALU.subtract), waits=[sqf, d4f], inc=sm_dve)
        d6f = P.op("scalar", lambda s: s.activation(
            out=st[0:8, 1:2], in_=st[0:8, 1:2], func=AF.Ln, bias=cvals[0:8, 0:1]),
            waits=[d5f], inc=sm_act)
        d7f = P.op("scalar", lambda s: s.activation(
            out=st[0:8, 0:1], in_=st[0:8, 1:2], func=AF.Exp, scale=-0.5),
            waits=[d6f], inc=sm_act)
        d8f = P.op("vector", lambda v: v.scalar_tensor_tensor(
            out=st[0:8, 2:3], in0=st[0:8, 2:3], scalar=-1.0, in1=st[0:8, 0:1],
            op0=ALU.mult, op1=ALU.mult), waits=[d7f], inc=sm_dve)
        xmf = P.op("vector", lambda v: v.tensor_scalar(
            out=x10_all[:], in0=x10_all[:], scalar1=1.0, scalar2=st[0:8, 2:3],
            op0=ALU.mult, op1=ALU.add), waits=[d8f], inc=sm_dve)
        x2f = P.op("vector", lambda v: v.scalar_tensor_tensor(
            out=x10_all[:], in0=e_scr[0:8, D:2 * D], scalar=st[0:8, 0:1],
            in1=x10_all[:], op0=ALU.mult, op1=ALU.add), waits=[xmf], inc=sm_dve)
        fo = None
        for j in range(nt):
            fo = P.dma("sync", out[j * TS:j * TS + 1, :], x10_all[j:j + 1, :],
                       waits=[x2f] if j == 0 else (), inc=sm_out)
        out_marks.append(fo)

        P.op("gpsimd", lambda g: g.nop(), waits=[out_marks[-1]], inc=sm_pl)

        with nc.Block() as block:
            @block.sync
            def _(sync):
                P.emit("sync", sync)

            @block.tensor
            def _(tensor):
                P.emit("tensor", tensor)

            @block.vector
            def _(vector):
                P.emit("vector", vector)

            @block.scalar
            def _(scalar):
                P.emit("scalar", scalar)

            @block.gpsimd
            def _(gpsimd):
                P.emit("gpsimd", gpsimd)

    return nc


_NC_CACHE = {}


def kernel(L0_attn_x, L0_attn_W_dq, L0_attn_W_uq, L0_attn_q_ln_g, L0_attn_q_ln_b,
           L0_attn_W_dkv, L0_attn_W_ukv, L0_attn_kv_ln_g, L0_attn_kv_ln_b,
           L0_attn_W_o, L0_ffn_W_r, L0_ffn_W_k, L0_ffn_W_v, L0_ffn_mu_r, L0_ffn_mu_k):
    if "nc" not in _NC_CACHE:
        _NC_CACHE["nc"] = build()
    nc = _NC_CACHE["nc"]
    x = np.ascontiguousarray(np.asarray(L0_attn_x), dtype=np.float32)
    shared = {
        "W_dq": np.ascontiguousarray(L0_attn_W_dq, np.float32),
        "W_uq": np.ascontiguousarray(L0_attn_W_uq, np.float32),
        "q_g": np.ascontiguousarray(L0_attn_q_ln_g, np.float32),
        "q_b": np.ascontiguousarray(L0_attn_q_ln_b, np.float32),
        "W_dkv": np.ascontiguousarray(L0_attn_W_dkv, np.float32),
        "W_ukv": np.ascontiguousarray(L0_attn_W_ukv, np.float32),
        "kv_g": np.ascontiguousarray(L0_attn_kv_ln_g, np.float32),
        "kv_b": np.ascontiguousarray(L0_attn_kv_ln_b, np.float32),
        "W_o": np.ascontiguousarray(L0_attn_W_o, np.float32),
        "W_r": np.ascontiguousarray(L0_ffn_W_r, np.float32),
        "W_k": np.ascontiguousarray(L0_ffn_W_k, np.float32),
        "W_v": np.ascontiguousarray(L0_ffn_W_v, np.float32),
        "mu_r": np.ascontiguousarray(L0_ffn_mu_r, np.float32),
        "mu_k": np.ascontiguousarray(L0_ffn_mu_k, np.float32),
    }
    in_maps = []
    for c in range(8):
        b, r = c // 2, c % 2
        m = dict(shared)
        m["x"] = np.ascontiguousarray(x[b])
        m["xq"] = np.ascontiguousarray(
            x[b].reshape(NT, TS, D)[r::2].reshape(8 * TS, D))
        m["maskin"] = host_masks(r)
        m["cfgi"] = np.zeros((1, 2), np.int32)
        m["cfgf"] = (np.array([[0.0, 1.0]], np.float32) if r == 0
                     else np.array([[1.0, 0.0]], np.float32))
        in_maps.append(m)
    res = run_bass_kernel_spmd(nc, in_maps, core_ids=list(range(8)))
    outp = np.empty((4, NT, TS, D), dtype=np.float32)
    for b in range(4):
        for r in range(2):
            outp[b, r::2] = res.results[2 * b + r]["out"].reshape(8, TS, D)
    return outp.reshape(4, S, D)
